# version 24
# baseline (speedup 1.0000x reference)
import numpy as np
import ml_dtypes
from contextlib import ExitStack

import concourse.bass as bass
import concourse.mybir as mybir
from concourse.bass_utils import run_bass_kernel_spmd

F32 = mybir.dt.float32
BF16 = mybir.dt.bfloat16
F8 = mybir.dt.float8e4
AT = mybir.ActivationFunctionType
OP = mybir.AluOpType
AX = mybir.AxisListType

T = 2048
D = 1024
H1 = 8
CH = 64
NCH = T // CH
TIER = 16
NTIER = CH // TIER
WIN = 512
NWIN = T // WIN
CPW = WIN // CH
LN_EPS = 1e-5
RMS_EPS = 1.1920929e-07
HLN_EPS = 1e-5

PHASES = 3
DEBUG = ()


class Sem:
    def __init__(self, es, nc, name):
        self.h = es.enter_context(nc.semaphore(name))
        self.v = 0

    def inc(self, instr, k=1):
        instr.then_inc(self.h, k)
        self.v += k
        return self.v

    def wait(self, eng, v=None):
        eng.wait_ge(self.h, self.v if v is None else v)


def build(phases=PHASES, debug=DEBUG, l1_nwin=None, l1_stop="full", l1_nchunk=None, l1_cstop="c4"):
    dbg = set(debug)
    if "l1a" in dbg:
        dbg |= {"qpost", "kpost", "vpost", "awin", "ggwin", "betaw"}
    if "l1b" in dbg:
        dbg |= {"owin"}
    nc = bass.Bass("TRN2", target_bir_lowering=False)
    es = ExitStack()

    def din(name, shape, dt=F32):
        return nc.dram_tensor(name, list(shape), dt, kind="ExternalInput").ap()

    x_in = din("x", (D, T))
    w = {}
    w["W_DQ"] = din("W_DQ", (512, 64))
    w["W_UQT"] = din("W_UQT", (512, 64))
    w["W_DKV"] = din("W_DKV", (1024, 64))
    w["W_UKVT"] = din("W_UKVT", (1024, 128))
    w["W_QRT"] = din("W_QRT", (64, 64))
    w["W_KRT"] = din("W_KRT", (64, 64))
    w["W_OT"] = din("W_OT", (64, 64))
    w["q_norm_w"] = din("q_norm_w", (64, 1))
    w["k_norm_w"] = din("k_norm_w", (64, 1))
    w["f0_kernel"] = nc.dram_tensor("f0_kernel", [D, D], BF16, kind="ExternalInput").ap()
    w["f0_in_scale"] = din("f0_in_scale", (1, 1))
    w["f0_k_scale"] = din("f0_k_scale", (1, 1))
    w["f0_bias"] = nc.dram_tensor("f0_bias", [128, 8], BF16, kind="ExternalInput").ap()
    for n in ("Wq", "Wk", "Wv", "Wa", "Wg", "Wo"):
        w[n] = din(n, (D, D))
    w["Wb"] = din("Wb", (128, 64))
    w["ba"] = din("ba", (128, 8))
    w["bb"] = din("bb", (H1, 1))
    for n in ("cqw", "ckw", "cvw"):
        w[n] = din(n, (128, 32))
    for n in ("cqb", "ckb", "cvb"):
        w[n] = din(n, (128, 8))
    w["ln_g"] = din("ln_g", (128, 1))
    w["ln_b"] = din("ln_b", (128, 1))
    for n in ("W_rT", "W_kT", "W_vT"):
        w[n] = din(n, (D, D))
    w["mu_r"] = din("mu_r", (128, 8))
    w["mu_k"] = din("mu_k", (128, 8))

    out_t = nc.dram_tensor("out", [D, T], F32, kind="ExternalOutput").ap()
    wbf = {n: nc.dram_tensor("bf_" + n, [D, D], BF16, kind="Internal").ap()
           for n in ("Wq", "Wk", "Wv", "Wa", "Wg", "Wo", "W_rT", "W_kT", "W_vT")}
    wsd = nc.dram_tensor("wsd", [128, 512], BF16, kind="Internal").ap()
    wsd2 = nc.dram_tensor("wsd2", [64, 1024], BF16, kind="Internal").ap()
    mskd = nc.dram_tensor("mskd", [128, 640], BF16, kind="Internal").ap()

    dbg_t = {}

    def sb(name, p, f, dt=F32, stack=None):
        side = "left" if stack is None else "right"
        return (stack or es).enter_context(nc.sbuf_tensor(name, [p, f], dt, side=side))

    ident_b = sb("ident_b", 128, 128, BF16)
    ident_f = sb("ident_f", 128, 128)
    tri0 = sb("tri0", 128, 128, BF16)
    maskAn = sb("maskAn", 64, 512, BF16)
    maskBn = sb("maskBn", 64, 512, BF16)
    ones2blk = sb("ones2blk", 128, 2, BF16)
    ind2_f = sb("ind2_f", 128, 128)
    onesrow_f = sb("onesrow_f", 1, 128)
    ones128_f = sb("ones128_f", 128, 128)
    onescol_b = sb("onescol_b", 128, 1, BF16)
    onescol_f = sb("onescol_f", 128, 1)
    zeros64 = sb("zeros64", 128, 64)
    epsc = sb("epsc", 128, 4)
    cscr = None
    statsa = sb("statsa", 128, 2048)
    lntmp = sb("lntmp", 128, 512)

    ph0s = ExitStack()
    cscr = sb("cscr", 128, 512, stack=ph0s)
    wdq = sb("wdq", 128, 256, stack=ph0s)
    wuqt = sb("wuqt", 128, 256, stack=ph0s)
    wdkv = sb("wdkv", 128, 512, stack=ph0s)
    wukvt = sb("wukvt", 128, 1024, stack=ph0s)
    mqt_blk = sb("mqt_blk", 128, 128)
    mkt_blk = sb("mkt_blk", 128, 128)
    mvt_blk = sb("mvt_blk", 128, 128)
    wqrt_t = sb("wqrt", 64, 64, BF16)
    wkrt_blk = sb("wkrt_blk", 128, 128)
    wot_t = sb("wot", 64, 64)
    qnw = sb("qnw", 128, 1)
    knw = sb("knw", 128, 1)
    f0bias = sb("f0bias", 128, 8)
    srow = sb("srow", 1, 8)
    sinv_c = sb("sinv_c", 128, 1)
    s2_c = sb("s2_c", 128, 1)
    skinv_c = sb("skinv_c", 128, 1)
    ba_c = sb("ba_c", 128, 8)
    bb_c = sb("bb_c", H1, 1)
    cw = {n: sb("cw_" + n, 128, 32) for n in ("cqw", "ckw", "cvw")}
    cb = {n: sb("cb_" + n, 128, 8) for n in ("cqb", "ckb", "cvb")}
    lng_c = sb("lng_c", 128, 1)
    lnb_c = sb("lnb_c", 128, 1)
    mur_c = sb("mur_c", 128, 8)
    muk_c = sb("muk_c", 128, 8)

    xa = [sb(f"xa{i}", 128, T) for i in range(8)]
    xb = [sb(f"xb{i}", 128, T) for i in range(8)]
    x2 = []

    pb = [es.enter_context(nc.psum_tensor(f"pb{i}", [128, 512], F32)) for i in range(8)]

    s_ld = Sem(es, nc, "s_ld")
    s_ldg = Sem(es, nc, "s_ldg")
    s_mv = Sem(es, nc, "s_mv")
    s_wc = Sem(es, nc, "s_wc")
    s_mw = Sem(es, nc, "s_mw")
    s_g = Sem(es, nc, "s_g")
    s_t = Sem(es, nc, "s_t")
    s_v = Sem(es, nc, "s_v")
    s_s = Sem(es, nc, "s_s")
    s_st = Sem(es, nc, "s_st")
    s_lda = Sem(es, nc, "s_lda")
    s_dmp = Sem(es, nc, "s_dmp")
    s_ldb = Sem(es, nc, "s_ldb")
    s_bar = Sem(es, nc, "s_bar")

    nc.const_aps.register(F32, RMS_EPS, epsc[:, 0:1])
    nc.const_aps.register(F32, LN_EPS, epsc[:, 1:2])
    nc.const_aps.register(F32, HLN_EPS, epsc[:, 2:3])

    block = es.enter_context(nc.Block())

    def on(name, fn):
        getattr(block, name)(fn)

    ALL_E = ("tensor", "vector", "scalar", "gpsimd")

    def barrier(engs=ALL_E):
        n = len(engs)
        base = s_bar.v
        for en in engs:
            on(en, lambda e: e.drain().then_inc(s_bar.h, 1))
        s_bar.v += n
        for en in engs:
            on(en, lambda e: e.wait_ge(s_bar.h, base + n))

    def dump(name, tiles):
        if name not in dbg:
            return
        n = len(tiles)
        p_ = tiles[0].shape[0]
        f_ = tiles[0].shape[1]
        dt = tiles[0].dtype
        hdl = nc.dram_tensor("dbg_" + name, [n * p_, f_], dt, kind="ExternalOutput").ap()
        dbg_t[name] = "dbg_" + name

        def f(e):
            s_t.wait(e)
            s_v.wait(e)
            s_s.wait(e)
            s_g.wait(e)
            for i, t_ in enumerate(tiles):
                s_st.inc(e.dma_start(out=hdl[i * p_:(i + 1) * p_, :], in_=t_[:] if hasattr(t_, 'shape') and not isinstance(t_, bass.AP) else t_), 16)
            s_st.wait(e)
            e.nop().then_inc(s_dmp.h, 1)
        on("sync", f)
        s_dmp.v += 1
        for en in ALL_E:
            on(en, lambda e: e.wait_ge(s_dmp.h, s_dmp.v))

    def phase0():
        def f(e):
            for i in range(8):
                s_ld.inc(e.dma_start(out=xa[i][:], in_=x_in[i * 128:(i + 1) * 128, :]), 16)
            s_ld.inc(e.dma_start(out=wdq[:].rearrange("p (a m) -> p a m", m=w["W_DQ"].shape[1]), in_=w["W_DQ"].rearrange("(a p) m -> p a m", p=128)), 16)
            s_ld.inc(e.dma_start(out=wuqt[:].rearrange("p (a m) -> p a m", m=w["W_UQT"].shape[1]), in_=w["W_UQT"].rearrange("(a p) m -> p a m", p=128)), 16)
            s_ld.inc(e.dma_start(out=wdkv[:].rearrange("p (a m) -> p a m", m=w["W_DKV"].shape[1]), in_=w["W_DKV"].rearrange("(a p) m -> p a m", p=128)), 16)
            s_ld.inc(e.dma_start(out=wukvt[:].rearrange("p (a m) -> p a m", m=w["W_UKVT"].shape[1]), in_=w["W_UKVT"].rearrange("(a p) m -> p a m", p=128)), 16)
            s_ld.inc(e.dma_start(out=wot_t[:], in_=w["W_OT"]), 16)
            s_ld.inc(e.dma_start(out=qnw[0:64, :], in_=w["q_norm_w"]), 16)
            s_ld.inc(e.dma_start(out=qnw[64:128, :], in_=w["q_norm_w"]), 16)
            s_ld.inc(e.dma_start(out=knw[0:64, :], in_=w["k_norm_w"]), 16)
            s_ld.inc(e.dma_start(out=knw[64:128, :], in_=w["k_norm_w"]), 16)
            s_ld.inc(e.dma_start(out=ba_c[:], in_=w["ba"]), 16)
            s_ld.inc(e.dma_start(out=bb_c[:], in_=w["bb"]), 16)
            s_ld.inc(e.dma_start(out=srow[0:1, 0:1], in_=w["f0_in_scale"]), 16)
            s_ld.inc(e.dma_start(out=srow[0:1, 1:2], in_=w["f0_k_scale"]), 16)
            for n in ("cqw", "ckw", "cvw"):
                s_ld.inc(e.dma_start(out=cw[n][:], in_=w[n]), 16)
            for n in ("cqb", "ckb", "cvb"):
                s_ld.inc(e.dma_start(out=cb[n][:], in_=w[n]), 16)
            s_ld.inc(e.dma_start(out=lng_c[:], in_=w["ln_g"]), 16)
            s_ld.inc(e.dma_start(out=lnb_c[:], in_=w["ln_b"]), 16)
            s_ld.inc(e.dma_start(out=mur_c[:], in_=w["mu_r"]), 16)
            s_ld.inc(e.dma_start(out=muk_c[:], in_=w["mu_k"]), 16)
        on("sync", f)

        def f(e):
            s_ldg.inc(e.dma_start(out=wqrt_t[:], in_=w["W_QRT"]), 16)
            s_ldg.inc(e.dma_start(out=wkrt_blk[0:64, 0:64], in_=w["W_KRT"]), 16)
            s_ldg.inc(e.dma_start(out=wkrt_blk[64:128, 64:128], in_=w["W_KRT"]), 16)
            s_ldg.inc(e.dma_start(out=f0bias[:], in_=w["f0_bias"]), 16)
            if phases >= 3:
                for n in ("Wq", "Wk", "Wv", "Wa", "Wg", "Wo", "W_rT", "W_kT", "W_vT"):
                    s_ldg.inc(e.dma_start(out=wbf[n], in_=w[n]), 16)
            for t_ in (mqt_blk, mkt_blk, mvt_blk, ones2blk, ind2_f, zeros64, ident_f):
                e.memset(t_[:], 0.0)
            e.memset(wkrt_blk[0:64, 64:128], 0.0)
            e.memset(wkrt_blk[64:128, 0:64], 0.0)
            e.drain()
            e.memset(ones2blk[0:64, 0:1], 1.0)
            e.memset(ones2blk[64:128, 1:2], 1.0)
            e.memset(ind2_f[0:2, 64:128], 1.0)
            e.drain()
            e.memset(ind2_f[0:1, 64:128], 0.0)
            e.memset(ind2_f[0:1, 0:64], 1.0)
            e.drain()
            for b0 in (32, 64, 96):
                e.tensor_copy(ind2_f[b0:b0 + 2, :], ind2_f[0:2, :])
            e.memset(onesrow_f[:], 1.0)
            e.memset(ones128_f[:], 1.0)
            e.memset(onescol_b[:], 1.0)
            e.memset(onescol_f[:], 1.0)
            e.memset(epsc[:, 0:1], RMS_EPS)
            e.memset(epsc[:, 1:2], LN_EPS)
            e.memset(epsc[:, 2:3], HLN_EPS)
            e.memset(cscr[:], 1.0)
            e.drain()
            e.affine_select(ident_f[:], cscr[:, 0:128], [[-1, 128]], OP.is_equal, 0.0,
                            base=0, channel_multiplier=1)
            e.affine_select(cscr[:, 128:256], cscr[:, 0:128], [[1, 128]], OP.is_ge, 0.0,
                            base=0, channel_multiplier=-1)
            e.drain()
            e.tensor_copy(ident_b[:], ident_f[:])
            e.tensor_copy(tri0[:], cscr[:, 128:256])
            e.drain()
            e.memset(cscr[:], -1.0)
            e.drain()
            e.affine_select(cscr[0:64, 0:512], cscr[64:128, 0:512], [[0, 8], [1, 64]],
                            OP.is_gt, 0.0, base=0, channel_multiplier=-1)
            e.drain()
            e.tensor_copy(maskAn[:], cscr[0:64, 0:512])
            e.drain()
            e.memset(cscr[64:128, :], -1.0)
            e.drain()
            e.affine_select(cscr[0:64, 0:512], cscr[64:128, 0:512], [[0, 8], [1, 64]],
                            OP.is_ge, 0.0, base=0, channel_multiplier=-1)
            e.drain()
            e.tensor_copy(maskBn[:], cscr[0:64, 0:512])
            s_g.inc(e.drain())
        on("gpsimd", f)

        def f(e):
            s_ld.wait(e)
            s_ldg.wait(e)
            s_g.wait(e)
            for a in range(4):
                e.matmul(pb[0][0:64, 0:64], wdq[:, a * 64:(a + 1) * 64], wuqt[:, a * 64:(a + 1) * 64],
                         start=(a == 0), stop=(a == 3))
            for a in range(8):
                e.matmul(pb[1][0:64, 0:128], wdkv[:, a * 64:(a + 1) * 64], wukvt[:, a * 128:(a + 1) * 128],
                         start=(a == 0), stop=(a == 7))
            s_t.inc(e.drain())
        on("tensor", f)

        def f(e):
            s_t.wait(e)
            e.copy(mqt_blk[0:64, 0:64], pb[0][0:64, 0:64])
            e.copy(mqt_blk[64:128, 64:128], pb[0][0:64, 0:64])
            e.copy(mkt_blk[0:64, 0:64], pb[1][0:64, 0:64])
            e.copy(mkt_blk[64:128, 64:128], pb[1][0:64, 0:64])
            e.copy(mvt_blk[0:64, 0:64], pb[1][0:64, 64:128])
            e.copy(mvt_blk[64:128, 64:128], pb[1][0:64, 64:128])
            s_s.inc(e.drain())
        on("scalar", f)
        barrier()

    def emit_module_ln(ysrc, dst, sqscr):
        mlv = {}
        for i in range(8):
            def f(e, i=i):
                if i >= 2:
                    s_t.wait(e, mlv[i - 2])
                s_s.inc(e.activation(sqscr[i % 2][:], ysrc[i][:], AT.Square))
            on("scalar", f)

            def f(e, i=i):
                s_s.wait(e)
                for tt in range(4):
                    mm = e.matmul(pb[tt][0:1, 0:512], onescol_b[:],
                                  sqscr[i % 2][:, tt * 512:(tt + 1) * 512],
                                  start=(i == 0), stop=(i == 7))
                s_t.inc(mm)
                mlv[i] = s_t.v
            on("tensor", f)

        def f(e):
            for i in range(8):
                for tt in range(4):
                    mm = e.matmul(pb[4 + tt][0:1, 0:512], onescol_f[:],
                                  ysrc[i][:, tt * 512:(tt + 1) * 512],
                                  start=(i == 0), stop=(i == 7))
            s_t.inc(mm)
        on("tensor", f)

        for tt in range(4):
            def f(e, tt=tt):
                s_t.wait(e)
                e.tensor_scalar(statsa[0:1, 0:512], pb[4 + tt][0:1, 0:512], 1.0 / D, None, OP.mult)
                e.tensor_scalar(statsa[0:1, 512:1024], pb[tt][0:1, 0:512], 1.0 / D, None, OP.mult)
                e.drain()
                e.tensor_mul(statsa[0:1, 1024:1536], statsa[0:1, 0:512], statsa[0:1, 0:512])
                e.drain()
                s_v.inc(e.tensor_sub(statsa[0:1, 1536:2048], statsa[0:1, 512:1024], statsa[0:1, 1024:1536]))
            on("vector", f)

            def f(e):
                s_v.wait(e)
                e.activation(statsa[0:1, 1024:1536], statsa[0:1, 1536:2048], AT.Ln, bias=LN_EPS)
                e.drain()
                s_s.inc(e.activation(statsa[0:1, 1536:2048], statsa[0:1, 1024:1536], AT.Exp, scale=-0.5))
            on("scalar", f)

            def f(e, tt=tt):
                s_s.wait(e)
                s_v.wait(e)
                e.matmul(pb[tt][0:128, 0:512], ones128_f[0:1, :],
                         statsa[0:1, 0:512], start=True, stop=True)
                s_t.inc(e.matmul(pb[4 + tt][0:128, 0:512], ones128_f[0:1, :],
                                 statsa[0:1, 1536:2048], start=True, stop=True))
            on("tensor", f)

            def f(e, tt=tt):
                s_t.wait(e)
                sl = slice(tt * 512, (tt + 1) * 512)
                for i in range(8):
                    e.tensor_sub(lntmp[:], ysrc[i][:, sl], pb[tt][:, 0:512])
                    e.drain()
                    mu = e.tensor_mul(dst[i][:, sl] if not isinstance(dst[i], bass.AP) else dst[i][:, sl], lntmp[:], pb[4 + tt][:, 0:512])
                    if i < 7:
                        e.drain()
                s_v.inc(mu)
            on("vector", f)
        barrier()

    def l0_attn_v2():
        ph = ExitStack()
        sqS = sb("sqS", 128, T, BF16, ph)
        qt = [sb(f"qt{i}", 128, T, BF16, ph) for i in range(4)]
        kt_ = [sb(f"kt{i}", 128, T, BF16, ph) for i in range(4)]
        vt = [sb(f"vt{i}", 128, 16 * 130, BF16, ph) for i in range(2)]
        rstdbQ = sqS
        rstdbK = sqS
        expsT = [sb(f"expsT{i}", 128, 512, BF16, ph) for i in range(2)]
        expsTb = [sb(f"expsTb{i}", 128, 512, BF16, ph) for i in range(2)]
        expTmp = sb("expTmp", 128, 128, BF16, ph)
        expTmp2 = sb("expTmp2", 128, 128, BF16, ph)
        ctxE = sb("ctxE", 64, 512, F32, ph)
        attnp = [sb(f"attnp{i}", 128, T, BF16, ph) for i in range(2)]
        sqf = [sqS, attnp[0]]

        def qcv(p, c0, c1):
            return xb[p][:, c0 // 2:c1 // 2].bitcast(BF16)

        def kcv(p, c0, c1):
            return xb[p][:, 1024 + c0 // 2:1024 + c1 // 2].bitcast(BF16)

        for g in range(4):
            pairs = (2 * g, 2 * g + 1)

            for pi, p in enumerate(pairs):
                for kind, blkw, dstv in (("qc", mqt_blk, qcv), ("kc", mkt_blk, kcv)):
                    def f(e, blkw=blkw, p=p):
                        s_s.wait(e)
                        for tt in range(4):
                            mm = e.matmul(pb[tt][0:128, 0:512], blkw[:],
                                          xa[p][:, tt * 512:(tt + 1) * 512],
                                          start=True, stop=True)
                        s_t.inc(mm)
                    on("tensor", f)

                    def f(e, dstv=dstv, p=p):
                        s_t.wait(e)
                        for tt in range(4):
                            cp = e.copy(dstv(p, tt * 512, (tt + 1) * 512), pb[tt][0:128, 0:512])
                        s_s.inc(cp)
                    on("scalar", f)

            for pi, p in enumerate(pairs):
                for grp in range(4):
                    def f(e, p=p, grp=grp):
                        s_s.wait(e)
                        for k in range(4):
                            sl = grp * 4 + k
                            mm = e.matmul(pb[4][0:128, k * 128:(k + 1) * 128],
                                          xa[p][:, sl * 128:(sl + 1) * 128], mvt_blk[:],
                                          start=True, stop=True)
                        s_t.inc(mm)
                    on("tensor", f)

                    def f(e, pi=pi, grp=grp):
                        s_t.wait(e)
                        src = pb[4][0:128, 0:512].rearrange("p (s h v) -> p s h v", s=4, h=2)
                        dstv = vt[pi][:, grp * 4 * 130:(grp * 4 + 4) * 130] \
                            .rearrange("p (s c) -> p s c", s=4)[:, :, 0:130] \
                            .rearrange("p s (h c) -> p s h c", h=2)[:, :, :, 0:64] if False else None
                        e.copy(vt[pi][:].rearrange("p (s c) -> p s c", s=16)
                               [:, grp * 4:grp * 4 + 4, 0:64],
                               pb[4][0:128, 0:512].rearrange("p (s h v) -> p s h v", s=4, h=2)[:, :, 0, :])
                        s_s.inc(e.copy(vt[pi][:].rearrange("p (s c) -> p s c", s=16)
                               [:, grp * 4:grp * 4 + 4, 65:129],
                               pb[4][0:128, 0:512].rearrange("p (s h v) -> p s h v", s=4, h=2)[:, :, 1, :]))
                    on("scalar", f)

            def f(e, pairs=pairs):
                for pi in range(2):
                    e.memset(vt[pi][:].rearrange("p (s c) -> p s c", s=16)[:, :, 64:65], 1.0)
                    e.memset(vt[pi][:].rearrange("p (s c) -> p s c", s=16)[:, :, 129:130], 1.0)
                s_g.inc(e.drain())
            on("gpsimd", f)

            for pi in range(2):
                for qk in (0, 1):
                    vsel = qcv if qk == 0 else kcv
                    dstb = rstdbQ if qk == 0 else rstdbK

                    def f(e, vsel=vsel, p=pairs[pi]):
                        s_s.wait(e)
                        s_t.wait(e)
                        s_v.wait(e)
                        e.activation(sqS[:], vsel(p, 0, 2048), AT.Square)
                        s_s.inc(e.drain())
                    on("scalar", f)

                    def f(e):
                        s_s.wait(e)
                        for tt in range(4):
                            mm = e.matmul(pb[4 + tt][0:2, 0:512], ones2blk[:],
                                          sqS[:, tt * 512:(tt + 1) * 512], start=True, stop=True)
                        s_t.inc(mm)
                    on("tensor", f)

                    def f(e):
                        s_t.wait(e)
                        for tt in range(4):
                            e.activation(statsa[32:34, tt * 512:(tt + 1) * 512],
                                         pb[4 + tt][0:2, 0:512],
                                         AT.Ln, bias=RMS_EPS, scale=1.0 / 64)
                        e.drain()
                        for tt in range(4):
                            ac = e.activation(statsa[0:2, tt * 512:(tt + 1) * 512],
                                              statsa[32:34, tt * 512:(tt + 1) * 512],
                                              AT.Exp, scale=-0.5)
                        s_s.inc(ac)
                    on("scalar", f)

                    bcv = {}
                    for tt in range(4):
                        def f(e, tt=tt):
                            if tt < 2:
                                s_s.wait(e)
                            else:
                                s_s.wait(e, bcv[tt - 2])
                            s_t.inc(e.matmul(pb[2 + tt % 2][0:128, 0:512], ind2_f[0:2, :],
                                     statsa[0:2, tt * 512:(tt + 1) * 512], start=True, stop=True))
                        on("tensor", f)

                        def f(e, dstb=dstb, tt=tt):
                            s_t.wait(e)
                            s_s.inc(e.copy(dstb[:, tt * 512:(tt + 1) * 512],
                                           pb[2 + tt % 2][0:128, 0:512]))
                            bcv[tt] = s_s.v
                        on("scalar", f)

                    def f(e, pi=pi, qk=qk, dstb=dstb):
                        s_s.wait(e)
                        vsel = qcv if qk == 0 else kcv
                        p = pairs[pi]
                        for hh in range(2):
                            hidx = 2 * pi + hh
                            dst = (qt if qk == 0 else kt_)[hidx]
                            nwc = qnw if qk == 0 else knw
                            full = vsel(p, 0, 2048)
                            st = e.scalar_tensor_tensor(dst[0:64, :], full[hh * 64:hh * 64 + 64, :],
                                                   nwc[hh * 64:hh * 64 + 64, :],
                                                   dstb[hh * 64:hh * 64 + 64, :],
                                                   OP.mult, OP.mult)
                        s_v.inc(st)
                    on("vector", f)

            for pi, p in enumerate(pairs):
                def f(e, p=p):
                    s_s.wait(e)
                    s_v.wait(e)
                    for tt in range(4):
                        mm = e.matmul(pb[tt][0:128, 0:512], wkrt_blk[:],
                                      xa[p][:, tt * 512:(tt + 1) * 512], start=True, stop=True)
                    s_t.inc(mm)
                on("tensor", f)

                def f(e, pi=pi):
                    s_t.wait(e)
                    for tt in range(4):
                        e.copy(kt_[2 * pi][64:128, tt * 512:(tt + 1) * 512], pb[tt][0:64, 0:512])
                        cp = e.copy(kt_[2 * pi + 1][64:128, tt * 512:(tt + 1) * 512], pb[tt][64:128, 0:512])
                    s_s.inc(cp)
                on("scalar", f)
            for hidx in range(4):
                def f(e, hidx=hidx):
                    s_s.wait(e)
                    s_v.wait(e)
                    for tt in range(4):
                        mm = e.matmul(pb[4 + tt][0:64, 0:512], wqrt_t[:],
                                      qt[hidx][0:64, tt * 512:(tt + 1) * 512], start=True, stop=True)
                    s_t.inc(mm)
                on("tensor", f)

                def f(e, hidx=hidx):
                    s_t.wait(e)
                    for tt in range(4):
                        cp = e.copy(qt[hidx][64:128, tt * 512:(tt + 1) * 512], pb[4 + tt][0:64, 0:512])
                    s_s.inc(cp)
                on("scalar", f)

            barrier()

            SC = 0.125
            svals = {}
            pk_abs = [0]
            for hh in range(4):
                hpair = hh % 2
                pi = hh // 2
                p = pairs[pi]
                qth, kth, vta = qt[hh], kt_[hh], vt[pi]
                vcol = hpair * 65
                for qsup in range(4):
                    nkt = 4 * qsup + 4
                    for pk in range(nkt // 2):
                        pa = pk_abs[0]
                        pk_abs[0] += 1
                        kts = (2 * pk, 2 * pk + 1)
                        bks = (pb[0], pb[1]) if pa % 2 == 0 else (pb[5], pb[6])
                        exs = (expsT[pa % 2], expsTb[pa % 2])
                        parts = tuple(kt >= 4 * qsup for kt in kts)
                        j0s = tuple((kt - 4 * qsup) * 128 if kt >= 4 * qsup else 0 for kt in kts)
                        FDs = tuple(512 - j0 for j0 in j0s)

                        def f(e, kts=kts, bks=bks, FDs=FDs, qsup=qsup, j0s=j0s,
                              qth=qth, kth=kth, pa=pa):
                            if pa >= 2:
                                s_s.wait(e, svals.get(pa - 2, 0))
                            for kt, bk, j0, FD in zip(kts, bks, j0s, FDs):
                                mm = e.matmul(bk[0:128, 0:FD], kth[:, kt * 128:(kt + 1) * 128],
                                              qth[:, qsup * 512 + j0:(qsup + 1) * 512],
                                              start=True, stop=True)
                            s_t.inc(mm)
                        on("tensor", f)

                        def f(e, bks=bks, exs=exs, parts=parts, FDs=FDs, pa=pa):
                            s_t.wait(e)
                            for i2 in range(2):
                                bk, ex, part, FD = bks[i2], exs[i2], parts[i2], FDs[i2]
                                tmp = expTmp if i2 == 0 else expTmp2
                                if part:
                                    ac = e.activation(tmp[:], bk[0:128, 0:128], AT.Exp, scale=SC)
                                    if FD > 128:
                                        ac = e.activation(ex[:, 128:FD], bk[0:128, 128:FD],
                                                          AT.Exp, scale=SC)
                                else:
                                    ac = e.activation(ex[:, 0:FD], bk[0:128, 0:FD], AT.Exp, scale=SC)
                            s_s.inc(ac)
                            svals[pa] = s_s.v
                        on("scalar", f)

                        if parts[0] or parts[1]:
                            def f(e, exs=exs, parts=parts):
                                s_s.wait(e)
                                for i2 in range(2):
                                    if parts[i2]:
                                        tmp = expTmp if i2 == 0 else expTmp2
                                        mu = e.tensor_mul(exs[i2][:, 0:128], tmp[:], tri0[:])
                                s_v.inc(mu)
                            on("vector", f)

                        def f(e, kts=kts, exs=exs, parts=parts, j0s=j0s, FDs=FDs,
                              nkt=nkt, vta=vta, vcol=vcol):
                            s_s.wait(e)
                            if parts[0] or parts[1] or kts[0] == 0:
                                s_v.wait(e)
                            for i2 in range(2):
                                kt, ex, j0, FD = kts[i2], exs[i2], j0s[i2], FDs[i2]
                                mm = e.matmul(pb[2][0:65, j0:512],
                                              vta[:, kt * 130 + vcol:kt * 130 + vcol + 65],
                                              ex[:, 0:FD], start=(kt == 0), stop=(kt == nkt - 1))
                            s_t.inc(mm)
                        on("tensor", f)

                    def f(e):
                        s_t.wait(e)
                        e.reciprocal(statsa[0:1, 0:512], pb[2][64:65, 0:512])
                        e.drain()
                        e.tensor_copy(ctxE[:], pb[2][0:64, 0:512])
                        s_v.inc(e.drain())
                    on("vector", f)

                    def f(e):
                        s_v.wait(e)
                        e.matmul(pb[3][0:64, 0:512], wot_t[:], ctxE[:], start=True, stop=True)
                        s_t.inc(e.matmul(pb[4][0:64, 0:512], onesrow_f[0:1, 0:64],
                                 statsa[0:1, 0:512], start=True, stop=True))
                    on("tensor", f)

                    def f(e):
                        s_t.wait(e)
                        s_v.wait(e)
                        e.tensor_copy(lntmp[0:64, :], pb[4][0:64, 0:512])
                        s_v.inc(e.drain())
                    on("vector", f)

                    def f(e, pi=pi, hpair=hpair, qsup=qsup):
                        s_t.wait(e)
                        s_v.inc(e.tensor_mul(attnp[pi][hpair * 64:hpair * 64 + 64, qsup * 512:(qsup + 1) * 512],
                                             pb[3][0:64, 0:512], lntmp[0:64, :]))
                    on("vector", f)

            def f(e, pairs=pairs):
                s_v.wait(e)
                for pi, p in enumerate(pairs):
                    ad = e.tensor_add(xb[p][:], xa[p][:], attnp[pi][:])
                s_g.inc(ad)
            on("gpsimd", f)
            barrier()

        dump("y1", xb)
        emit_module_ln(xb, xa, sqf)
        dump("x1", xa)
        ph.close()

    def l0_ffn():
        ph = ExitStack()
        x1q = [sb(f"x1q{i}", 128, T, F8, ph) for i in range(8)]
        kq8 = [sb(f"kq8{i}", 128, D, F8, ph) for i in range(8)]
        kbf2 = [sb(f"kbf2_{i}", 128, D, BF16, ph) for i in range(2)]
        xbf = sb("xbf", 128, T, BF16, ph)
        ybf2 = [sb(f"ybf{i}", 128, 512, BF16, ph) for i in range(2)]
        sq2 = sb("sq2", 128, T, BF16, ph)
        sqf = [xbf, sq2]

        def f(e):
            e.reciprocal(srow[0:1, 2:3], srow[0:1, 0:1])
            e.drain()
            e.reciprocal(srow[0:1, 3:4], srow[0:1, 1:2])
            e.drain()
            e.tensor_mul(srow[0:1, 4:5], srow[0:1, 0:1], srow[0:1, 1:2])
            s_v.inc(e.drain())
        on("vector", f)

        def f(e):
            s_v.wait(e)
            e.matmul(pb[0][0:128, 0:1], onesrow_f[:], srow[0:1, 2:3], start=True, stop=True)
            e.matmul(pb[0][0:128, 1:2], onesrow_f[:], srow[0:1, 3:4], start=True, stop=True)
            s_t.inc(e.matmul(pb[0][0:128, 2:3], onesrow_f[:], srow[0:1, 4:5], start=True, stop=True))
        on("tensor", f)

        def f(e):
            s_t.wait(e)
            e.copy(sinv_c[:], pb[0][0:128, 0:1])
            e.copy(skinv_c[:], pb[0][0:128, 1:2])
            e.copy(s2_c[:], pb[0][0:128, 2:3])
            s_s.inc(e.drain())
        on("scalar", f)

        for ci in range(8):
            sem = s_lda if ci % 2 == 0 else s_ldb

            def f(e, ci=ci, sem=sem):
                if ci < 2:
                    e.wait_ge(s_bar.h, s_bar.v)
                else:
                    s_s.wait(e)
                sem.inc(e.dma_start(out=kbf2[ci % 2][:], in_=w["f0_kernel"][ci * 128:(ci + 1) * 128, :]), 16)
            on("sync", f)

            def f(e, ci=ci, sem=sem):
                sem.wait(e)
                s_s.inc(e.activation(kq8[ci][:], kbf2[ci % 2][:], AT.Copy, scale=skinv_c[:]))
            on("scalar", f)

        def f(e):
            s_s.wait(e)
            xh = [xbf[:, 0:1024], xbf[:, 1024:2048]]
            for i in range(16):
                src_ap = xa[i // 2][:, (i % 2) * 1024:(i % 2 + 1) * 1024]
                dst_ap = x1q[i // 2][:, (i % 2) * 1024:(i % 2 + 1) * 1024]
                e.activation(xh[i % 2], src_ap, AT.Copy)
                e.drain()
                ac = e.activation(dst_ap, xh[i % 2], AT.Copy, scale=sinv_c[:])
            s_s.inc(ac)
        on("scalar", f)
        barrier()

        tiles = [(ot, tt) for ot in range(8) for tt in range(4)]
        gvals = {}
        for pi2 in range(16):
            pr = tiles[2 * pi2:2 * pi2 + 2]
            bks = (pb[0], pb[1]) if pi2 % 2 == 0 else (pb[2], pb[3])

            def f(e, pr=pr, bks=bks, pi2=pi2):
                if pi2 < 2:
                    s_v.wait(e)
                else:
                    s_v.wait(e, gvals[pi2 - 2])
                for (ot, tt), bank in zip(pr, bks):
                    for ci in range(8):
                        mm = e.matmul(bank[0:128, 0:512], kq8[ci][:, ot * 128:(ot + 1) * 128],
                                      x1q[ci][:, tt * 512:(tt + 1) * 512],
                                      start=(ci == 0), stop=(ci == 7))
                s_t.inc(mm)
            on("tensor", f)

            def f(e, pr=pr, bks=bks, pi2=pi2):
                s_t.wait(e)
                s_v.wait(e)
                for i2, ((ot, tt), bank) in enumerate(zip(pr, bks)):
                    ac = e.activation(ybf2[i2][:], bank[0:128, 0:512], AT.Copy, scale=s2_c[:])
                s_s.inc(ac)
            on("scalar", f)

            def f(e, pr=pr, pi2=pi2):
                s_s.wait(e)
                for i2, (ot, tt) in enumerate(pr):
                    e.tensor_scalar(sq2[:, i2 * 512:(i2 + 1) * 512], ybf2[i2][:],
                                    f0bias[:, ot:ot + 1], None, OP.add)
                e.drain()
                for i2, (ot, tt) in enumerate(pr):
                    ad = e.tensor_add(xb[ot][:, tt * 512:(tt + 1) * 512],
                                      sq2[:, i2 * 512:(i2 + 1) * 512],
                                      xa[ot][:, tt * 512:(tt + 1) * 512])
                s_v.inc(ad)
                gvals[pi2] = s_v.v
            on("vector", f)
        barrier()
        dump("y2", xb)
        emit_module_ln(xb, x2, sqf)
        dump("x2", [t_ for t_ in x2])
        ph.close()

    def l1_attn(l1s):
        wb_s = sb("wb_s", 128, 64, BF16, l1s)
        wcur = [sb(f"wcur{i}", 128, 2048, BF16, l1s) for i in range(2)]
        vpost = sb("vpost", 128, 4096, BF16, l1s)
        awin = sb("awin", 128, 4096, BF16, l1s)
        betaw = sb("betaw", H1, WIN, F32, l1s)
        betaf = lntmp[0:1, 0:512]
        gscr = sb("gscr", 128, 552, F32, l1s)
        giscr = sb("giscr", 128, 552, F32, l1s)
        convt = [sb(f"convt{i}", 128, 512, F32, l1s) for i in range(2)]
        SST = sb("SST", 128, 1024, BF16, l1s)
        wS = sb("wS", 128, 512, BF16, l1s)
        wping = [sb(f"wping{i}", 128, 128, BF16, l1s) for i in range(2)]
        rpS = sb("rpS", 128, 512, BF16, l1s)
        arecipS = sb("arecipS", 128, 512, F32, l1s)
        recipBA = sb("recipBA", 128, 512, F32, l1s)
        AgrS = sb("AgrS", 64, 512, BF16, l1s)
        BgrS = sb("BgrS", 64, 512, BF16, l1s)
        BDset = sb("BDset", 128, 1280, BF16, l1s)
        wCons = sb("wCons", 64, 1024, BF16, l1s)
        ktokN = sb("ktokN", 64, 1024, BF16, l1s)
        gCrows = sb("gCrows", 128, 256, F32, l1s)
        gCrow0 = sb("gCrow0", 1, 1024, F32, l1s)
        M8 = sb("M8", 128, 128, BF16, l1s)
        maskBnP = sb("maskBnP", 64, 512, BF16, l1s)
        R16S = sb("R16S", 64, 512, BF16, l1s)
        rv = statsa[:, 0:256].bitcast(BF16)
        Vg = statsa[:, 256:512].bitcast(BF16)
        bndq = sb("bndq", 128, 24, BF16, l1s)
        bndk = sb("bndk", 128, 24, BF16, l1s)
        bndv = sb("bndv", 128, 24, BF16, l1s)
        bndx = sb("bndx", 128, 8, BF16, l1s)
        y3w = None

        def xbv(i):
            return xb[i][:].bitcast(BF16)

        def pre_ap(P, od, c0, c1):
            t_ = xb[P * 2 + od // 4][:]
            base = (od % 4) * 515
            return t_[:, (base + c0) // 2 * 0:0] if False else xb[P * 2 + od // 4][:].bitcast(BF16)[:, base + c0:base + c1]

        def post_ap(P, c0, c1):
            if P == 0:
                return xb[6][:].bitcast(BF16)[:, c0:c1]
            if P == 1:
                return xb[7][:].bitcast(BF16)[:, c0:c1]
            return vpost[:, c0:c1]

        wcur_free = {"v": 0}

        def f(e):
            s_ldg.inc(e.dma_start(out=wb_s[:], in_=w["Wb"]), 16)
            e.memset(BDset[:], 0.0)
            e.memset(SST[:], 0.0)
            e.memset(gscr[:, 520:552], 0.0)
            e.memset(giscr[:, 520:552], 0.0)
            e.memset(bndq[:], 0.0)
            e.memset(bndk[:], 0.0)
            e.memset(bndv[:], 0.0)
            e.memset(bndx[:], 0.0)
            e.drain()
            for i in range(NTIER):
                e.tensor_copy(R16S[:, i * 128:(i + 1) * 128],
                              bass.AP(tensor=ident_b, offset=16 * i,
                                      ap=[[128, 64], [0, 8], [1, 16]]))
            s_g.inc(e.drain())
        on("gpsimd", f)

        def f(e):
            s_g.wait(e)
            e.tensor_reduce(statsa[:, 0:8],
                            ident_b[:].rearrange("p (h t) -> p h t", t=16),
                            AX.X, OP.add)
            e.drain()
            e.tensor_copy(M8[:].rearrange("p (h t) -> p h t", t=16),
                          bass.AP(tensor=statsa, offset=0,
                                  ap=[[2048, 128], [1, 8], [0, 16]]))
            s_v.inc(e.drain())
        on("vector", f)
        barrier()

        for wi in range(NWIN if l1_nwin is None else l1_nwin):
            l1_window(wi, locals())

    def l1_window(wi, V):
        wb_s = V["wb_s"]; wcur = V["wcur"]; vpost = V["vpost"]; awin = V["awin"]
        wcur_free = V["wcur_free"]
        betaw = V["betaw"]; betaf = V["betaf"]
        ggwin = xb[0][:].bitcast(BF16)
        owin = xb[1][:].bitcast(BF16)
        gscr = V["gscr"]; convt = V["convt"]; SST = V["SST"]; wS = V["wS"]
        giscr = V["giscr"]
        wping = V["wping"]; rpS = V["rpS"]; arecipS = V["arecipS"]; recipBA = V["recipBA"]
        AgrS = V["AgrS"]; BgrS = V["BgrS"]; BDset = V["BDset"]; wCons = V["wCons"]
        ktokN = V["ktokN"]; Vg = V["Vg"]; gCrows = V["gCrows"]; rv = V["rv"]
        gCrow0 = V["gCrow0"]
        M8 = V["M8"]; maskBnP = V["maskBnP"]; R16S = V["R16S"]
        bndq = V["bndq"]; bndk = V["bndk"]; bndv = V["bndv"]
        pre_ap = V["pre_ap"]; post_ap = V["post_ap"]

        tw = slice(wi * WIN, (wi + 1) * WIN)
        x2w = lambda ci: x2[ci][:, tw]


        def f(e):
            if wi == 0:
                s_ldg.wait(e)
            s_g.wait(e)
            s_v.wait(e)
            for ci in range(8):
                mm = e.matmul(pb[6][0:8, 0:512], wb_s[:, ci * 8:(ci + 1) * 8], x2w(ci),
                              start=(ci == 0), stop=(ci == 7))
            s_t.inc(mm)
        on("tensor", f)

        def f(e):
            s_t.wait(e)
            s_s.inc(e.activation(betaw[:, 0:WIN], pb[6][0:8, 0:512], AT.Sigmoid, bias=bb_c[:]))
        on("scalar", f)

        cwn = {0: "cqw", 1: "ckw", 2: "cvw"}
        cbn = {0: "cqb", 1: "ckb", 2: "cvb"}

        def restore_boundary(P):
            def f(e, P=P):
                s_s.wait(e)
                bnd = (bndq, bndk, bndv)[P]
                for od in range(8):
                    e.tensor_copy(pre_ap(P, od, 0, 3), bnd[:, od * 3:od * 3 + 3])
                e.drain()
                for od in range(8):
                    e.tensor_copy(bnd[:, od * 3:od * 3 + 3], pre_ap(P, od, 512, 515))
                s_g.inc(e.drain())
            on("gpsimd", f)

        def emit_conv(P):
            restore_boundary(P)
            for od in range(8):
                def f(e, P=P, od=od):
                    s_g.wait(e)
                    s_s.wait(e)
                    wv = cw[cwn[P]]
                    e.tensor_scalar(convt[od % 2][:], pre_ap(P, od, 0, 512),
                                    wv[:, od * 4 + 0:od * 4 + 1], None, OP.mult)
                    e.drain()
                    for j in range(1, 4):
                        st = e.scalar_tensor_tensor(convt[od % 2][:], pre_ap(P, od, j, j + 512),
                                                    wv[:, od * 4 + j:od * 4 + j + 1],
                                                    convt[od % 2][:], OP.mult, OP.add)
                        if j < 3:
                            e.drain()
                    s_v.inc(st)
                on("vector", f)

                def f(e, P=P, od=od):
                    s_v.wait(e)
                    if P <= 1 and od == 0:
                        s_st.wait(e)
                    if P == 1:
                        stg = lntmp[:, 0:256].bitcast(BF16)
                        s_s.inc(e.activation(stg, convt[od % 2][:], AT.Silu, bias=cb[cbn[P]][:, od:od + 1]))
                    else:
                        dst = post_ap(P, od * 512, (od + 1) * 512)
                        s_s.inc(e.activation(dst, convt[od % 2][:], AT.Silu, bias=cb[cbn[P]][:, od:od + 1]))
                on("scalar", f)
                if P == 1:
                    def f(e, od=od):
                        s_s.wait(e)
                        stg = lntmp[:, 0:256].bitcast(BF16)
                        s_v.inc(e.tensor_scalar(post_ap(1, od * 512, (od + 1) * 512), stg, float(128 ** -0.5), None, OP.mult))
                    on("vector", f)

        projs = (("Wq", 0), ("Wk", 1), ("Wv", 2), ("Wa", 3), ("Wg", 4))
        ptvals = {}
        pentry = s_t.v
        for mi, (mn, P) in enumerate(projs):
            for qtr in range(4):
                def f(e, mn=mn, qtr=qtr, mi=mi):
                    qi = mi * 4 + qtr
                    if qi >= 2:
                        s_t.wait(e, ptvals[qi - 2])
                    elif wi > 0:
                        s_t.wait(e, wcur_free["v"])
                    sem = s_mv if qi % 2 == 0 else s_mw
                    sem.inc(e.dma_start(
                        out=wcur[qtr % 2][:].rearrange("p (a m) -> p a m", m=1024),
                        in_=wbf[mn][qtr * 256:(qtr + 1) * 256, :].rearrange("(a p) m -> p a m", p=128)), 16)
                on("gpsimd", f)

                def f(e, qtr=qtr, mi=mi, P=P):
                    (s_mv if (mi * 4 + qtr) % 2 == 0 else s_mw).wait(e)
                    if qtr == 0:
                        s_s.wait(e)
                        s_g.wait(e)
                        if P == 4:
                            s_v.wait(e)
                    buf = wcur[qtr % 2]
                    for od in range(8):
                        for c2 in range(2):
                            ci = qtr * 2 + c2
                            mm = e.matmul(pb[od][0:128, 0:512],
                                          buf[:, c2 * 1024 + od * 128:c2 * 1024 + (od + 1) * 128],
                                          x2w(ci), start=(ci == 0), stop=(ci == 7))
                    s_t.inc(mm)
                    ptvals[mi * 4 + qtr] = s_t.v
                on("tensor", f)

            def f(e, P=P):
                s_t.wait(e)
                for od in range(8):
                    bank = pb[od]
                    if P <= 2:
                        op = e.copy(pre_ap(P, od, 3, 515), bank[0:128, 0:512])
                    elif P == 3:
                        op = e.activation(awin[:, od * 512:(od + 1) * 512], bank[0:128, 0:512],
                                          AT.Sigmoid, bias=ba_c[:, od:od + 1])
                    else:
                        op = e.activation(ggwin[:, od * 512:(od + 1) * 512], bank[0:128, 0:512],
                                          AT.Sigmoid)
                s_s.inc(op)
            on("scalar", f)
            if P <= 2:
                emit_conv(P)
        barrier()
        if wi == 0 and "l1a" in dbg:
            dump("qpost", [post_ap(0, 0, 4096)])
            dump("kpost", [post_ap(1, 0, 4096)])
            dump("vpost", [post_ap(2, 0, 4096)])
            dump("awin", [awin[:]])
            dump("ggwin", [ggwin[:]])
            dump("betaw", [betaw[:]])

        V = dict(V)
        V["_wi"] = wi
        V["_kpost_t"] = xb[7]
        V["_kpost_off"] = 0
        V["_kpost_pitch"] = 4096
        if l1_stop == "proj":
            return
        for c in range(CPW if l1_nchunk is None else l1_nchunk):
            l1_chunk(wi, c, V)
        if wi == 0 and "owin" in dbg:
            dump("owin", [xb[1][:].bitcast(BF16)])
        if l1_stop == "chunks":
            return

        l1_tail(wi, V)

    def l1_chunk(wi, c, V):
        awin = V["awin"]; vpost = V["vpost"]; betaw = V["betaw"]; betaf = V["betaf"]
        gscr = V["gscr"]; giscr = V["giscr"]; SST = V["SST"]; wS = V["wS"]; wping = V["wping"]
        convt = V["convt"]
        rpS = V["rpS"]; arecipS = V["arecipS"]; recipBA = V["recipBA"]
        AgrS = V["AgrS"]; BgrS = V["BgrS"]; BDset = V["BDset"]; wCons = V["wCons"]
        ktokN = V["ktokN"]; Vg = V["Vg"]; gCrows = V["gCrows"]; rv = V["rv"]
        gCrow0 = V["gCrow0"]
        M8 = V["M8"]; maskBnP = V["maskBnP"]; R16S = V["R16S"]
        pre_ap = V["pre_ap"]; post_ap = V["post_ap"]
        owin = xb[1][:].bitcast(BF16)
        c0 = c * CH

        def kx(h):
            return post_ap(1, h * 512 + c0, h * 512 + c0 + CH)

        def qx(h):
            return post_ap(0, h * 512 + c0, h * 512 + c0 + CH)

        aw_ap = awin[:].rearrange("p (h t) -> p h t", h=8)[:, :, c0:c0 + CH]
        vw_ap = vpost[:].rearrange("p (h t) -> p h t", h=8)[:, :, c0:c0 + CH]
        g_ap = gscr[:, 0:520].rearrange("p (h t) -> p h t", h=8)
        gprev_ap = g_ap[:, :, 0:CH]
        gcur_ap = g_ap[:, :, 1:CH + 1]

        def f(e):
            s_v.wait(e)
            s_t.wait(e)
            e.memset(gscr[:, 0:520].rearrange("p (h t) -> p h t", h=8)[:, :, 0:1], 1.0)
            e.memset(giscr[:, 0:520].rearrange("p (h t) -> p h t", h=8)[:, :, 0:1], 1.0)
            s_g.inc(e.drain())
            s_mv.inc(e.dma_start(out=betaf.rearrange("q (h t) -> q h t", h=8),
                                 in_=betaw[:, c0:c0 + CH].rearrange("h (q t) -> h q t", q=1)), 16)
        on("gpsimd", f)

        gi_ap = giscr[:, 0:520].rearrange("p (h t) -> p h t", h=8)
        giprev_ap = gi_ap[:, :, 0:CH]

        def f(e):
            s_g.wait(e)
            s_s.wait(e)
            e.reciprocal(convt[0][:].rearrange("p (h t) -> p h t", h=8), aw_ap)
            e.drain()
            for h in range(8):
                e.tensor_tensor_scan(gscr[:, h * 65 + 1:h * 65 + 65],
                                     awin[:, h * 512 + c0:h * 512 + c0 + CH],
                                     zeros64[:, 0:CH], 1.0, OP.mult, OP.add)
                e.tensor_tensor_scan(giscr[:, h * 65 + 1:h * 65 + 65],
                                     convt[0][:, h * CH:(h + 1) * CH],
                                     zeros64[:, 0:CH], 1.0, OP.mult, OP.add)
            e.drain()
            e.tensor_tensor(Vg[:].rearrange("p (h t) -> p h t", h=8), vw_ap, giprev_ap, OP.mult)
            s_v.inc(e.drain())
        on("vector", f)

        def f(e):
            s_mv.wait(e)
            s_v.wait(e)
            s_t.inc(e.matmul(pb[2][0:128, 0:512], onesrow_f[:], betaf, start=True, stop=True))
        on("tensor", f)

        def f(e):
            s_t.wait(e)
            e.tensor_tensor(recipBA[:], pb[2][0:128, 0:512], convt[0][:], OP.mult)
            e.drain()
            for i in range(NTIER):
                e.tensor_copy(lntmp[:, 128 * i:128 * (i + 1)],
                              recipBA[:].rearrange("p (h t) -> p h t", h=8)[:, :, 16 * i:16 * i + 16])
            for hg in range(2):
                cp = e.tensor_copy(statsa[:, 1792 + hg * 128:1792 + (hg + 1) * 128],
                                   bass.AP(tensor=gscr, offset=hg * 260 + 64,
                                           ap=[[552, 128], [65, 4], [1, 32]]))
            s_v.inc(cp)
        on("vector", f)

        def f(e):
            s_v.wait(e)
            for i in range(NTIER):
                tr = e.transpose(pb[2][0:128, 128 * i:128 * i + 128], lntmp[:, 128 * i:128 * (i + 1)], ident_f[:])
            s_t.inc(tr)
        on("tensor", f)

        def f(e):
            s_t.wait(e)
            s_v.inc(e.tensor_copy(arecipS[:], pb[2][0:128, 0:512]))
        on("vector", f)

        def f(e):
            s_v.wait(e)
            for hg in range(2):
                tr = e.transpose(pb[2][0:128, hg * 128:hg * 128 + 128],
                                 statsa[:, 1792 + hg * 128:1792 + (hg + 1) * 128],
                                 ident_f[:])
            s_t.inc(tr)
        on("tensor", f)

        def f(e):
            s_t.wait(e)
            for h in range(8):
                cp = e.tensor_copy(gCrow0[0:1, h * 128:(h + 1) * 128],
                                   pb[2][32 * (h % 4):32 * (h % 4) + 1,
                                         (h // 4) * 128:(h // 4) * 128 + 128])
            s_v.inc(cp)
        on("vector", f)

        def f(e):
            s_v.wait(e)
            for h in range(8):
                e.matmul(pb[0][0:64, h * 64:(h + 1) * 64], kx(h), kx(h), start=True, stop=True)
            for h in range(8):
                mm = e.matmul(pb[1][0:64, h * 64:(h + 1) * 64], kx(h), qx(h), start=True, stop=True)
            s_t.inc(mm)
        on("tensor", f)

        def f(e):
            s_t.wait(e)
            e.tensor_mul(AgrS[:], pb[0][0:64, 0:512], maskAn[:])
            s_v.inc(e.tensor_mul(BgrS[:], pb[1][0:64, 0:512], maskBn[:]))
        on("vector", f)

        OFFCOLS = {(0, 1): 512, (0, 2): 640, (1, 2): 768, (0, 3): 896, (1, 3): 1024, (2, 3): 1152}
        T1S = statsa[:, 768:1792].bitcast(BF16)
        t1v = {}
        for i in range(NTIER):
            def f(e, i=i):
                if i < 2:
                    s_v.wait(e)
                else:
                    s_v.wait(e, t1v[i - 2])
                s_t.inc(e.matmul(pb[4 + i % 2][0:128, 0:512], R16S[:, i * 128:(i + 1) * 128],
                                 AgrS[:], start=True, stop=True))
            on("tensor", f)

            def f(e, i=i):
                s_t.wait(e)
                s_v.inc(e.tensor_copy(T1S[:, i * 512:(i + 1) * 512], pb[4 + i % 2][0:128, 0:512]))
                t1v[i] = s_v.v
            on("vector", f)

        def t1sel(pp, i2):
            return T1S[:, pp * 512:(pp + 1) * 512] \
                .rearrange("p (h q) -> p h q", q=64)[:, :, 16 * i2:16 * i2 + 16]

        def f(e):
            s_v.wait(e)
            s_t.wait(e)
            for i in range(NTIER):
                e.tensor_mul(BDset[:, i * 128:(i + 1) * 128], t1sel(i, i), M8[:])
            for (pp, i2), col in OFFCOLS.items():
                mu = e.tensor_mul(BDset[:, col:col + 128], t1sel(pp, i2), M8[:])
            s_g.inc(mu)
        on("gpsimd", f)

        def f(e):
            s_s.wait(e)
            s_v.wait(e)
            for h in range(8):
                tr = e.transpose(pb[2][0:64, 0:512].bitcast(BF16)[:, h * 128:(h + 1) * 128],
                                 kx(h), ident_b[:])
            s_t.inc(tr)
        on("tensor", f)

        def f(e):
            s_t.wait(e)
            s_v.inc(e.tensor_scalar(ktokN[:], pb[2][0:64, 0:512].bitcast(BF16),
                                    -1.0, None, OP.mult))
        on("vector", f)
        if l1_cstop == "c1":
            return

        def f(e):
            s_v.wait(e)
            for h in range(8):
                mm = e.matmul(pb[6][0:128, h * 64:(h + 1) * 64], SST[:, h * 128:(h + 1) * 128],
                              kx(h), start=True, stop=True)
            s_t.inc(mm)
        on("tensor", f)

        def f(e):
            s_t.wait(e)
            s_v.inc(e.tensor_sub(rv[:], pb[6][0:128, 0:512], Vg[:]))
        on("vector", f)

        rTS = statsa[:, 512:768].bitcast(BF16)

        rvstg = statsa[:, 1792:2048].bitcast(BF16)

        def f(e):
            s_t.wait(e)
            s_v.wait(e)
            for i in range(NTIER):
                cp = e.tensor_copy(rvstg[:, 128 * i:128 * (i + 1)],
                                   rv.rearrange("p (h t) -> p h t", h=8)[:, :, 16 * i:16 * i + 16])
            s_v.inc(cp)
        on("vector", f)

        def f(e):
            s_v.wait(e)
            for i in range(NTIER):
                tr = e.transpose(pb[2][0:128, 64 * i:64 * i + 64].bitcast(BF16),
                                 rvstg[:, 128 * i:128 * (i + 1)], ident_b[:])
            s_t.inc(tr)
        on("tensor", f)

        def f(e):
            s_t.wait(e)
            s_v.inc(e.tensor_copy(rTS, pb[2][0:128, 0:256].bitcast(BF16)))
        on("vector", f)
        if l1_cstop == "c2":
            return

        for i in range(NTIER):
            def f(e, i=i):
                if i == 0:
                    s_g.wait(e)
                s_v.wait(e)
                mm1 = e.matmul(pb[3][0:128, 128 * i:128 * i + 128], ident_b[:],
                               rTS[:, 128 * i:128 * i + 128], start=True, stop=(i == 0))
                for pp in range(i):
                    mm1 = e.matmul(pb[3][0:128, 128 * i:128 * i + 128],
                                   BDset[:, OFFCOLS[(pp, i)]:OFFCOLS[(pp, i)] + 128],
                                   wS[:, pp * 128:(pp + 1) * 128], start=False, stop=(pp == i - 1))
                s_t.inc(mm1)
            on("tensor", f)

            def f(e, i=i):
                s_t.wait(e)
                s_v.inc(e.tensor_copy(rpS[:, 128 * i:128 * i + 128],
                                      pb[3][0:128, 128 * i:128 * i + 128]))
            on("vector", f)

            for m in range(TIER):
                def f(e, i=i, m=m):
                    s_v.wait(e)
                    bank = pb[4 + m % 2]
                    mm1 = e.matmul(bank[0:128, 0:128], ident_b[:], rpS[:, 128 * i:128 * i + 128],
                                   start=True, stop=(m == 0))
                    if m > 0:
                        mm1 = e.matmul(bank[0:128, 0:128], BDset[:, 128 * i:128 * i + 128],
                                       wping[(m + 1) % 2][:], start=False, stop=True)
                    s_t.inc(mm1)
                on("tensor", f)

                def f(e, i=i, m=m):
                    s_t.wait(e)
                    dst = wS[:, 128 * i:128 * i + 128] if m == TIER - 1 else wping[m % 2][:]
                    s_v.inc(e.tensor_mul(dst, pb[4 + m % 2][0:128, 0:128],
                                         arecipS[:, 128 * i:128 * i + 128]))
                on("vector", f)

        if l1_cstop == "c3":
            return
        def f(e):
            s_v.wait(e)
            s_t.wait(e)
            s_wc.inc(e.dma_start(out=wsd, in_=wS[:]), 16)
            e.wait_ge(s_wc.h, s_wc.v)
            for i in range(NTIER):
                srcp = bass.AP(tensor=wsd.tensor, offset=i * 128,
                               ap=[[512, 16], [16 * 512, 8], [1, 128]])
                s_wc.inc(e.dma_start(out=wCons[16 * i:16 * (i + 1), :], in_=srcp), 16)
            if l1_cstop == "c4a":
                e.wait_ge(s_wc.h, s_wc.v)
        on("gpsimd", f)
        if l1_cstop == "c4a":
            return

        def f(e):
            s_v.wait(e)
            for h in range(8):
                e.matmul(pb[0][0:128, h * 64:(h + 1) * 64], SST[:, h * 128:(h + 1) * 128],
                         qx(h), start=(h == 0), stop=False)
            for h in range(8):
                bank = pb[6] if h < 4 else pb[7]
                col = (h % 4) * 128
                e.matmul(bank[0:128, col:col + 128], ident_b[:], SST[:, h * 128:(h + 1) * 128],
                         start=(h % 4 == 0), stop=False)
            for h in range(8):
                bank = pb[1] if h < 4 else pb[2]
                col = (h % 4) * 128
                mm = e.matmul(bank[0:128, col:col + 128], ones128_f[0:1, :],
                              gCrow0[0:1, h * 128:(h + 1) * 128],
                              start=(h % 4 == 0), stop=(h % 4 == 3))
            s_t.inc(mm)
        on("tensor", f)

        def f(e):
            s_wc.wait(e)
            for h in range(8):
                mm = e.matmul(pb[0][0:128, h * 64:(h + 1) * 64], wCons[0:64, h * 128:(h + 1) * 128],
                              BgrS[0:64, h * 64:(h + 1) * 64], start=False, stop=(h == 7))
            for h in range(8):
                bank = pb[6] if h < 4 else pb[7]
                col = (h % 4) * 128
                mm = e.matmul(bank[0:128, col:col + 128], ktokN[0:64, h * 128:(h + 1) * 128],
                              wCons[0:64, h * 128:(h + 1) * 128], start=False, stop=(h % 4 == 3))
            s_t.inc(mm)
        on("tensor", f)

        def f(e, c=c):
            s_t.wait(e)
            s_v.inc(e.tensor_tensor(owin[:].rearrange("p (h t) -> p h t", h=8)[:, :, c0:c0 + CH],
                                    pb[0][0:128, 0:512].rearrange("p (h t) -> p h t", h=8),
                                    gcur_ap, OP.mult))
        on("vector", f)

        if l1_cstop == "c4b":
            return
        def f(e):
            s_t.wait(e)
            e.tensor_copy(statsa[:, 1024:1536], pb[1][0:128, 0:512])
            e.tensor_copy(statsa[:, 1536:2048], pb[2][0:128, 0:512])
            e.drain()
            e.tensor_mul(SST[:, 0:512], pb[6][0:128, 0:512], statsa[:, 1024:1536])
            s_v.inc(e.tensor_mul(SST[:, 512:1024], pb[7][0:128, 0:512], statsa[:, 1536:2048]))
        on("vector", f)


    def emit_win_ln(ysrc, dst, convt):
        sqb = [lntmp[:].bitcast(BF16)[:, 0:512], lntmp[:].bitcast(BF16)[:, 512:1024]]
        wlv = {}
        for i in range(8):
            def f(e, i=i):
                if i == 0:
                    s_v.wait(e)
                    s_t.wait(e)
                if i >= 2:
                    s_t.wait(e, wlv[i - 2])
                s_s.inc(e.activation(sqb[i % 2], ysrc[i], AT.Square))
            on("scalar", f)

            def f(e, i=i):
                s_s.wait(e)
                if i == 0:
                    s_v.wait(e)
                e.matmul(pb[5][0:1, 0:512], onescol_b[:], sqb[i % 2],
                         start=(i == 0), stop=(i == 7))
                s_t.inc(e.matmul(pb[4][0:1, 0:512], onescol_f[:], ysrc[i],
                                 start=(i == 0), stop=(i == 7)))
                wlv[i] = s_t.v
            on("tensor", f)

        def f(e):
            s_t.wait(e)
            e.tensor_scalar(statsa[0:1, 0:512], pb[4][0:1, 0:512], 1.0 / D, None, OP.mult)
            e.tensor_scalar(statsa[0:1, 512:1024], pb[5][0:1, 0:512], 1.0 / D, None, OP.mult)
            e.drain()
            e.tensor_mul(statsa[0:1, 1024:1536], statsa[0:1, 0:512], statsa[0:1, 0:512])
            e.drain()
            s_v.inc(e.tensor_sub(statsa[0:1, 1536:2048], statsa[0:1, 512:1024], statsa[0:1, 1024:1536]))
        on("vector", f)

        def f(e):
            s_v.wait(e)
            e.activation(statsa[0:1, 1024:1536], statsa[0:1, 1536:2048], AT.Ln, bias=LN_EPS)
            e.drain()
            s_s.inc(e.activation(statsa[0:1, 1536:2048], statsa[0:1, 1024:1536], AT.Exp, scale=-0.5))
        on("scalar", f)

        def f(e):
            s_s.wait(e)
            e.matmul(pb[6][0:128, 0:512], ones128_f[0:1, :], statsa[0:1, 0:512],
                     start=True, stop=True)
            s_t.inc(e.matmul(pb[7][0:128, 0:512], ones128_f[0:1, :], statsa[0:1, 1536:2048],
                             start=True, stop=True))
        on("tensor", f)

        def f(e):
            s_t.wait(e)
            s_g.wait(e)
            for i in range(8):
                e.tensor_sub(convt[i % 2][:], ysrc[i], pb[6][:, 0:512])
                e.drain()
                mu = e.tensor_mul(dst(i), convt[i % 2][:], pb[7][:, 0:512])
                if i < 7:
                    e.drain()
            s_v.inc(mu)
        on("vector", f)

    def l1_tail(wi, V):
        convt = V["convt"]; wcur = V["wcur"]; vpost = V["vpost"]; bndx = V["bndx"]
        wcur_free = V["wcur_free"]
        tw = slice(wi * WIN, (wi + 1) * WIN)
        x2w = lambda ci: x2[ci][:, tw]
        owin = xb[1][:].bitcast(BF16)
        ggwin = xb[0][:].bitcast(BF16)
        gow = xb[6][:].bitcast(BF16)
        x3v = xb[4][:].bitcast(BF16)
        xrv = xb[5][:].bitcast(BF16)
        xkv = xb[7][:].bitcast(BF16)
        kactv = vpost

        def y3w(i):
            return xb[2][:, i * 512:(i + 1) * 512] if i < 4 else xb[3][:, (i - 4) * 512:(i - 3) * 512]

        def rw(i):
            return xb[2][:, i * 512:(i + 1) * 512] if i < 4 else xb[3][:, (i - 4) * 512:(i - 3) * 512]

        def y4w(i):
            return xb[1][:, i * 512:(i + 1) * 512] if i < 4 else xb[0][:, (i - 4) * 512:(i - 3) * 512]

        def outw(i):
            return xb[6][:, i * 512:(i + 1) * 512] if i < 4 else xb[7][:, (i - 4) * 512:(i - 3) * 512]

        sqb = [lntmp[:].bitcast(BF16)[:, 0:512], lntmp[:].bitcast(BF16)[:, 512:1024]]
        for h in range(8):
            def f(e, h=h):
                if h == 0:
                    s_v.wait(e)
                if h >= 2:
                    s_t.wait(e)
                s_s.inc(e.activation(sqb[h % 2], owin[:, h * 512:(h + 1) * 512], AT.Square))
            on("scalar", f)

            def f(e, h=h):
                s_s.wait(e)
                s_v.wait(e)
                e.matmul(pb[4 + h % 2][0:1, 0:512], onescol_b[:],
                         owin[:, h * 512:(h + 1) * 512], start=True, stop=True)
                s_t.inc(e.matmul(pb[2 + h % 2][0:1, 0:512], onescol_b[:], sqb[h % 2],
                                 start=True, stop=True))
            on("tensor", f)

            def f(e, h=h):
                s_t.wait(e)
                e.tensor_scalar(statsa[0:1, 0:512], pb[4 + h % 2][0:1, 0:512],
                                1.0 / 128, None, OP.mult)
                e.tensor_scalar(statsa[0:1, 512:1024], pb[2 + h % 2][0:1, 0:512],
                                1.0 / 128, None, OP.mult)
                e.drain()
                e.tensor_mul(statsa[0:1, 1024:1536], statsa[0:1, 0:512],
                             statsa[0:1, 0:512])
                e.drain()
                s_v.inc(e.tensor_sub(statsa[0:1, 1536:2048], statsa[0:1, 512:1024],
                                     statsa[0:1, 1024:1536]))
            on("vector", f)

            def f(e):
                s_v.wait(e)
                e.activation(statsa[0:1, 1024:1536], statsa[0:1, 1536:2048],
                             AT.Ln, bias=HLN_EPS)
                e.drain()
                s_s.inc(e.activation(statsa[0:1, 1536:2048], statsa[0:1, 1024:1536],
                                     AT.Exp, scale=-0.5))
            on("scalar", f)

            bmean, brstd = (pb[6], pb[7]) if h % 2 == 0 else (pb[0], pb[1])

            def f(e, bmean=bmean, brstd=brstd):
                s_s.wait(e)
                s_v.wait(e)
                e.matmul(bmean[0:128, 0:512], ones128_f[0:1, :],
                         statsa[0:1, 0:512], start=True, stop=True)
                s_t.inc(e.matmul(brstd[0:128, 0:512], ones128_f[0:1, :],
                                 statsa[0:1, 1536:2048], start=True, stop=True))
            on("tensor", f)

            def f(e, h=h, bmean=bmean, brstd=brstd):
                s_t.wait(e)
                e.tensor_sub(convt[0][:], owin[:, h * 512:(h + 1) * 512],
                             bmean[:, 0:512])
                e.drain()
                e.tensor_mul(convt[1][:], convt[0][:], brstd[:, 0:512])
                e.drain()
                e.tensor_scalar(convt[0][:], convt[1][:], lng_c[:, 0:1],
                                lnb_c[:, 0:1], OP.mult, OP.add)
                e.drain()
                s_v.inc(e.tensor_mul(gow[:, h * 512:(h + 1) * 512], convt[0][:],
                                     ggwin[:, h * 512:(h + 1) * 512]))
            on("vector", f)

        def stream_mm(mat, rhs_fn, first_extra=None):
            tvals = {}
            entry = s_t.v
            for pair in range(4):
                def f(e, pair=pair, mat=mat):
                    s_t.wait(e, tvals[pair - 2] if pair >= 2 else entry)
                    (s_mv if pair % 2 == 0 else s_mw).inc(e.dma_start(
                        out=wcur[pair % 2][:].rearrange("p (a m) -> p a m", m=1024),
                        in_=wbf[mat][pair * 256:(pair + 1) * 256, :].rearrange(
                            "(a p) m -> p a m", p=128)), 16)
                on("gpsimd", f)

                def f(e, pair=pair):
                    (s_mv if pair % 2 == 0 else s_mw).wait(e)
                    if pair == 0:
                        s_v.wait(e)
                        if first_extra is not None:
                            first_extra(e)
                    for sl in range(2):
                        cj = pair * 2 + sl
                        for ci in range(8):
                            mm = e.matmul(pb[ci][0:128, 0:512],
                                          wcur[pair % 2][:, sl * 1024 + ci * 128:sl * 1024 + (ci + 1) * 128],
                                          rhs_fn(cj), start=(cj == 0), stop=(cj == 7))
                    s_t.inc(mm)
                    tvals[pair] = s_t.v
                    wcur_free["v"] = s_t.v
                on("tensor", f)

        stream_mm("Wo", lambda h: gow[:, h * 512:(h + 1) * 512])

        def f(e):
            s_t.wait(e)
            for i in range(8):
                ad = e.tensor_add(y3w(i), pb[i][0:128, 0:512], x2w(i))
            s_v.inc(ad)
        on("vector", f)

        emit_win_ln([y3w(i) for i in range(8)],
                    lambda i: x3v[:, i * 512:(i + 1) * 512], convt)
        if wi == 0 and "x3" in dbg:
            dump("x3", [x3v[:, 0:4096]])

        def f(e):
            s_v.wait(e)
            s_g.wait(e)
            for ci in range(8):
                e.tensor_copy(convt[0][:, 1:512], x3v[:, ci * 512:ci * 512 + 511])
                e.tensor_copy(convt[0][:, 0:1], bndx[:, ci:ci + 1])
                e.drain()
                e.tensor_sub(convt[1][:], x3v[:, ci * 512:(ci + 1) * 512], convt[0][:])
                e.drain()
                e.scalar_tensor_tensor(xrv[:, ci * 512:(ci + 1) * 512], convt[1][:],
                                       mur_c[:, ci:ci + 1], convt[0][:], OP.mult, OP.add)
                e.scalar_tensor_tensor(xkv[:, ci * 512:(ci + 1) * 512], convt[1][:],
                                       muk_c[:, ci:ci + 1], convt[0][:], OP.mult, OP.add)
                e.drain()
            s_v.inc(e.drain())
        on("vector", f)

        def f(e):
            s_v.wait(e)
            e.tensor_copy(bndx[:, 0:8].rearrange("p (c o) -> p c o", o=1),
                          x3v.rearrange("p (c t) -> p c t", t=512)[:, :, 511:512])
            s_g.inc(e.drain())
        on("gpsimd", f)

        stream_mm("W_rT", lambda cj: xrv[:, cj * 512:(cj + 1) * 512])

        def f(e):
            s_t.wait(e)
            for i in range(8):
                ac = e.activation(rw(i), pb[i][0:128, 0:512], AT.Sigmoid)
            s_s.inc(ac)
        on("scalar", f)

        def kwait(e):
            s_s.wait(e)
        stream_mm("W_kT", lambda cj: xkv[:, cj * 512:(cj + 1) * 512], first_extra=kwait)
        e2v = {}
        for i in range(8):
            def f(e, i=i):
                if i == 0:
                    s_t.wait(e)
                if i >= 2:
                    s_v.wait(e, e2v[i - 2])
                s_s.inc(e.activation(convt[i % 2][:], pb[i][0:128, 0:512], AT.Relu))
            on("scalar", f)

            def f(e, i=i):
                s_s.wait(e)
                s_v.inc(e.tensor_mul(kactv[:, i * 512:(i + 1) * 512], convt[i % 2][:],
                                     convt[i % 2][:]))
                e2v[i] = s_v.v
            on("vector", f)

        stream_mm("W_vT", lambda cj: kactv[:, cj * 512:(cj + 1) * 512])

        def f(e):
            s_t.wait(e)
            for i in range(8):
                e.tensor_mul(convt[i % 2][:], pb[i][0:128, 0:512], rw(i))
                e.drain()
                ad = e.tensor_add(y4w(i), convt[i % 2][:], x3v[:, i * 512:(i + 1) * 512])
                if i < 7:
                    e.drain()
            s_v.inc(ad)
        on("vector", f)

        emit_win_ln([y4w(i) for i in range(8)], outw, convt)

        def f(e):
            s_v.wait(e)
            for ci in range(8):
                s_st.inc(e.dma_start(out=out_t[ci * 128:(ci + 1) * 128, tw],
                                     in_=outw(ci)), 16)
        on("sync", f)

    phase0()
    ph0s.close()
    if phases >= 1:
        l0_attn_v2()
    if phases >= 2:
        for i in range(8):
            x2.append(xa[i][:, 0:1024].bitcast(BF16))
        l0_ffn()
    if phases >= 3:
        l1s = ExitStack()
        l1_attn(l1s)

    def store_out():
        def f(e):
            s_v.wait(e)
            s_s.wait(e)
            if phases < 3:
                for i in range(8):
                    s_st.inc(e.dma_start(out=out_t[i * 128:(i + 1) * 128, :], in_=xa[i][:]), 16)
            s_st.wait(e)
        on("sync", f)

    store_out()
    es.close()
    return nc, dbg_t


def _prep_weights(inputs):
    f32 = np.float32
    g = lambda n: np.ascontiguousarray(np.asarray(inputs[n], dtype=f32))
    wm = {}
    wm["W_DQ"] = g("L0_attn_W_DQ")
    wm["W_UQT"] = np.ascontiguousarray(g("L0_attn_W_UQ").T)
    wm["W_DKV"] = g("L0_attn_W_DKV")
    wm["W_UKVT"] = np.ascontiguousarray(g("L0_attn_W_UKV").T)
    wm["W_QRT"] = np.ascontiguousarray(g("L0_attn_W_QR").T)
    wm["W_KRT"] = np.ascontiguousarray(g("L0_attn_W_KR").T)
    wm["W_OT"] = np.ascontiguousarray(g("L0_attn_W_O").T)
    wm["q_norm_w"] = g("L0_attn_q_norm_w").reshape(64, 1)
    wm["k_norm_w"] = g("L0_attn_k_norm_w").reshape(64, 1)
    wm["f0_kernel"] = np.ascontiguousarray(np.asarray(inputs["L0_ffn_kernel"]))
    wm["f0_in_scale"] = g("L0_ffn_input_scale").reshape(1, 1)
    wm["f0_k_scale"] = g("L0_ffn_kernel_scale").reshape(1, 1)
    wm["f0_bias"] = np.ascontiguousarray(np.asarray(inputs["L0_ffn_bias"]).reshape(8, 128).T)
    for n in ("Wq", "Wk", "Wv", "Wa", "Wg", "Wo"):
        wm[n] = g("L1_attn_" + n)
    wm["Wb"] = np.ascontiguousarray(g("L1_attn_Wb").reshape(8, 128, 8).transpose(1, 0, 2).reshape(128, 64))
    wm["ba"] = np.ascontiguousarray(g("L1_attn_ba").reshape(8, 128).T)
    wm["bb"] = g("L1_attn_bb").reshape(H1, 1)
    for n in ("cqw", "ckw", "cvw"):
        wm[n] = np.ascontiguousarray(g("L1_attn_" + n).reshape(8, 128, 4).transpose(1, 0, 2).reshape(128, 32))
    for n in ("cqb", "ckb", "cvb"):
        wm[n] = np.ascontiguousarray(g("L1_attn_" + n).reshape(8, 128).T)
    wm["ln_g"] = g("L1_attn_ln_g").reshape(128, 1)
    wm["ln_b"] = g("L1_attn_ln_b").reshape(128, 1)
    wm["W_rT"] = np.ascontiguousarray(g("L1_ffn_W_r").T)
    wm["W_kT"] = np.ascontiguousarray(g("L1_ffn_W_k").T)
    wm["W_vT"] = np.ascontiguousarray(g("L1_ffn_W_v").T)
    wm["mu_r"] = np.ascontiguousarray(g("L1_ffn_mu_r").reshape(8, 128).T)
    wm["mu_k"] = np.ascontiguousarray(g("L1_ffn_mu_k").reshape(8, 128).T)
    return wm


def kernel(**inputs):
    x = np.asarray(inputs["L0_attn_x"], dtype=np.float32)
    B = x.shape[0]
    wm = _prep_weights(inputs)
    nc, _ = build()
    in_maps = []
    for b in range(B):
        m = {"x": np.ascontiguousarray(x[b].T)}
        m.update(wm)
        in_maps.append(m)
    res = run_bass_kernel_spmd(nc, in_maps, core_ids=list(range(B)))
    out = np.stack([np.asarray(r["out"]).T for r in res.results], axis=0)
    return out.astype(np.float32)


# version 25 (final)
# speedup vs baseline: 1.0016x; 1.0016x over previous
import numpy as np
import ml_dtypes
from contextlib import ExitStack

import concourse.bass as bass
import concourse.mybir as mybir
from concourse.bass_utils import run_bass_kernel_spmd

F32 = mybir.dt.float32
BF16 = mybir.dt.bfloat16
F8 = mybir.dt.float8e4
AT = mybir.ActivationFunctionType
OP = mybir.AluOpType
AX = mybir.AxisListType

T = 2048
D = 1024
H1 = 8
CH = 64
NCH = T // CH
TIER = 16
NTIER = CH // TIER
WIN = 512
NWIN = T // WIN
CPW = WIN // CH
LN_EPS = 1e-5
RMS_EPS = 1.1920929e-07
HLN_EPS = 1e-5

PHASES = 3
DEBUG = ()


class Sem:
    def __init__(self, es, nc, name):
        self.h = es.enter_context(nc.semaphore(name))
        self.v = 0

    def inc(self, instr, k=1):
        instr.then_inc(self.h, k)
        self.v += k
        return self.v

    def wait(self, eng, v=None):
        eng.wait_ge(self.h, self.v if v is None else v)


def build(phases=PHASES, debug=DEBUG, l1_nwin=None, l1_stop="full", l1_nchunk=None, l1_cstop="c4"):
    dbg = set(debug)
    if "l1a" in dbg:
        dbg |= {"qpost", "kpost", "vpost", "awin", "ggwin", "betaw"}
    if "l1b" in dbg:
        dbg |= {"owin"}
    nc = bass.Bass("TRN2", target_bir_lowering=False)
    es = ExitStack()

    def din(name, shape, dt=F32):
        return nc.dram_tensor(name, list(shape), dt, kind="ExternalInput").ap()

    x_in = din("x", (D, T))
    w = {}
    w["W_DQ"] = din("W_DQ", (512, 64))
    w["W_UQT"] = din("W_UQT", (512, 64))
    w["W_DKV"] = din("W_DKV", (1024, 64))
    w["W_UKVT"] = din("W_UKVT", (1024, 128))
    w["W_QRT"] = din("W_QRT", (64, 64))
    w["W_KRT"] = din("W_KRT", (64, 64))
    w["W_OT"] = din("W_OT", (64, 64))
    w["q_norm_w"] = din("q_norm_w", (64, 1))
    w["k_norm_w"] = din("k_norm_w", (64, 1))
    w["f0_kernel"] = nc.dram_tensor("f0_kernel", [D, D], BF16, kind="ExternalInput").ap()
    w["f0_in_scale"] = din("f0_in_scale", (1, 1))
    w["f0_k_scale"] = din("f0_k_scale", (1, 1))
    w["f0_bias"] = nc.dram_tensor("f0_bias", [128, 8], BF16, kind="ExternalInput").ap()
    for n in ("Wq", "Wk", "Wv", "Wa", "Wg", "Wo"):
        w[n] = din(n, (D, D))
    w["Wb"] = din("Wb", (128, 64))
    w["ba"] = din("ba", (128, 8))
    w["bb"] = din("bb", (H1, 1))
    for n in ("cqw", "ckw", "cvw"):
        w[n] = din(n, (128, 32))
    for n in ("cqb", "ckb", "cvb"):
        w[n] = din(n, (128, 8))
    w["ln_g"] = din("ln_g", (128, 1))
    w["ln_b"] = din("ln_b", (128, 1))
    for n in ("W_rT", "W_kT", "W_vT"):
        w[n] = din(n, (D, D))
    w["mu_r"] = din("mu_r", (128, 8))
    w["mu_k"] = din("mu_k", (128, 8))

    out_t = nc.dram_tensor("out", [D, T], F32, kind="ExternalOutput").ap()
    wbf = {n: nc.dram_tensor("bf_" + n, [D, D], BF16, kind="Internal").ap()
           for n in ("Wq", "Wk", "Wv", "Wa", "Wg", "Wo", "W_rT", "W_kT", "W_vT")}
    wsd = nc.dram_tensor("wsd", [128, 512], BF16, kind="Internal").ap()
    wsd2 = nc.dram_tensor("wsd2", [64, 1024], BF16, kind="Internal").ap()
    mskd = nc.dram_tensor("mskd", [128, 640], BF16, kind="Internal").ap()

    dbg_t = {}

    def sb(name, p, f, dt=F32, stack=None):
        side = "left" if stack is None else "right"
        return (stack or es).enter_context(nc.sbuf_tensor(name, [p, f], dt, side=side))

    ident_b = sb("ident_b", 128, 128, BF16)
    ident_f = sb("ident_f", 128, 128)
    tri0 = sb("tri0", 128, 128, BF16)
    maskAn = sb("maskAn", 64, 512, BF16)
    maskBn = sb("maskBn", 64, 512, BF16)
    ones2blk = sb("ones2blk", 128, 2, BF16)
    ind2_f = sb("ind2_f", 128, 128)
    onesrow_f = sb("onesrow_f", 1, 128)
    ones128_f = sb("ones128_f", 128, 128)
    onescol_b = sb("onescol_b", 128, 1, BF16)
    onescol_f = sb("onescol_f", 128, 1)
    zeros64 = sb("zeros64", 128, 64)
    epsc = sb("epsc", 128, 4)
    cscr = None
    statsa = sb("statsa", 128, 2048)
    lntmp = sb("lntmp", 128, 512)

    ph0s = ExitStack()
    cscr = sb("cscr", 128, 512, stack=ph0s)
    wdq = sb("wdq", 128, 256, stack=ph0s)
    wuqt = sb("wuqt", 128, 256, stack=ph0s)
    wdkv = sb("wdkv", 128, 512, stack=ph0s)
    wukvt = sb("wukvt", 128, 1024, stack=ph0s)
    mqt_blk = sb("mqt_blk", 128, 128)
    mkt_blk = sb("mkt_blk", 128, 128)
    mvt_blk = sb("mvt_blk", 128, 128)
    wqrt_t = sb("wqrt", 64, 64, BF16)
    wkrt_blk = sb("wkrt_blk", 128, 128)
    wot_t = sb("wot", 64, 64)
    qnw = sb("qnw", 128, 1)
    knw = sb("knw", 128, 1)
    f0bias = sb("f0bias", 128, 8)
    srow = sb("srow", 1, 8)
    sinv_c = sb("sinv_c", 128, 1)
    s2_c = sb("s2_c", 128, 1)
    skinv_c = sb("skinv_c", 128, 1)
    ba_c = sb("ba_c", 128, 8)
    bb_c = sb("bb_c", H1, 1)
    cw = {n: sb("cw_" + n, 128, 32) for n in ("cqw", "ckw", "cvw")}
    cb = {n: sb("cb_" + n, 128, 8) for n in ("cqb", "ckb", "cvb")}
    lng_c = sb("lng_c", 128, 1)
    lnb_c = sb("lnb_c", 128, 1)
    mur_c = sb("mur_c", 128, 8)
    muk_c = sb("muk_c", 128, 8)

    xa = [sb(f"xa{i}", 128, T) for i in range(8)]
    xb = [sb(f"xb{i}", 128, T) for i in range(8)]
    x2 = []

    pb = [es.enter_context(nc.psum_tensor(f"pb{i}", [128, 512], F32)) for i in range(8)]

    s_ld = Sem(es, nc, "s_ld")
    s_ldg = Sem(es, nc, "s_ldg")
    s_mv = Sem(es, nc, "s_mv")
    s_wc = Sem(es, nc, "s_wc")
    s_mw = Sem(es, nc, "s_mw")
    s_g = Sem(es, nc, "s_g")
    s_t = Sem(es, nc, "s_t")
    s_v = Sem(es, nc, "s_v")
    s_s = Sem(es, nc, "s_s")
    s_st = Sem(es, nc, "s_st")
    s_lda = Sem(es, nc, "s_lda")
    s_dmp = Sem(es, nc, "s_dmp")
    s_ldb = Sem(es, nc, "s_ldb")
    s_bar = Sem(es, nc, "s_bar")

    nc.const_aps.register(F32, RMS_EPS, epsc[:, 0:1])
    nc.const_aps.register(F32, LN_EPS, epsc[:, 1:2])
    nc.const_aps.register(F32, HLN_EPS, epsc[:, 2:3])

    block = es.enter_context(nc.Block())

    def on(name, fn):
        getattr(block, name)(fn)

    ALL_E = ("tensor", "vector", "scalar", "gpsimd")

    def barrier(engs=ALL_E):
        n = len(engs)
        base = s_bar.v
        for en in engs:
            on(en, lambda e: e.drain().then_inc(s_bar.h, 1))
        s_bar.v += n
        for en in engs:
            on(en, lambda e: e.wait_ge(s_bar.h, base + n))

    def dump(name, tiles):
        if name not in dbg:
            return
        n = len(tiles)
        p_ = tiles[0].shape[0]
        f_ = tiles[0].shape[1]
        dt = tiles[0].dtype
        hdl = nc.dram_tensor("dbg_" + name, [n * p_, f_], dt, kind="ExternalOutput").ap()
        dbg_t[name] = "dbg_" + name

        def f(e):
            s_t.wait(e)
            s_v.wait(e)
            s_s.wait(e)
            s_g.wait(e)
            for i, t_ in enumerate(tiles):
                s_st.inc(e.dma_start(out=hdl[i * p_:(i + 1) * p_, :], in_=t_[:] if hasattr(t_, 'shape') and not isinstance(t_, bass.AP) else t_), 16)
            s_st.wait(e)
            e.nop().then_inc(s_dmp.h, 1)
        on("sync", f)
        s_dmp.v += 1
        for en in ALL_E:
            on(en, lambda e: e.wait_ge(s_dmp.h, s_dmp.v))

    def phase0():
        def f(e):
            for i in range(8):
                s_ld.inc(e.dma_start(out=xa[i][:], in_=x_in[i * 128:(i + 1) * 128, :]), 16)
            s_ld.inc(e.dma_start(out=wdq[:].rearrange("p (a m) -> p a m", m=w["W_DQ"].shape[1]), in_=w["W_DQ"].rearrange("(a p) m -> p a m", p=128)), 16)
            s_ld.inc(e.dma_start(out=wuqt[:].rearrange("p (a m) -> p a m", m=w["W_UQT"].shape[1]), in_=w["W_UQT"].rearrange("(a p) m -> p a m", p=128)), 16)
            s_ld.inc(e.dma_start(out=wdkv[:].rearrange("p (a m) -> p a m", m=w["W_DKV"].shape[1]), in_=w["W_DKV"].rearrange("(a p) m -> p a m", p=128)), 16)
            s_ld.inc(e.dma_start(out=wukvt[:].rearrange("p (a m) -> p a m", m=w["W_UKVT"].shape[1]), in_=w["W_UKVT"].rearrange("(a p) m -> p a m", p=128)), 16)
            s_ld.inc(e.dma_start(out=wot_t[:], in_=w["W_OT"]), 16)
            s_ld.inc(e.dma_start(out=qnw[0:64, :], in_=w["q_norm_w"]), 16)
            s_ld.inc(e.dma_start(out=qnw[64:128, :], in_=w["q_norm_w"]), 16)
            s_ld.inc(e.dma_start(out=knw[0:64, :], in_=w["k_norm_w"]), 16)
            s_ld.inc(e.dma_start(out=knw[64:128, :], in_=w["k_norm_w"]), 16)
            s_ld.inc(e.dma_start(out=ba_c[:], in_=w["ba"]), 16)
            s_ld.inc(e.dma_start(out=bb_c[:], in_=w["bb"]), 16)
            s_ld.inc(e.dma_start(out=srow[0:1, 0:1], in_=w["f0_in_scale"]), 16)
            s_ld.inc(e.dma_start(out=srow[0:1, 1:2], in_=w["f0_k_scale"]), 16)
            for n in ("cqw", "ckw", "cvw"):
                s_ld.inc(e.dma_start(out=cw[n][:], in_=w[n]), 16)
            for n in ("cqb", "ckb", "cvb"):
                s_ld.inc(e.dma_start(out=cb[n][:], in_=w[n]), 16)
            s_ld.inc(e.dma_start(out=lng_c[:], in_=w["ln_g"]), 16)
            s_ld.inc(e.dma_start(out=lnb_c[:], in_=w["ln_b"]), 16)
            s_ld.inc(e.dma_start(out=mur_c[:], in_=w["mu_r"]), 16)
            s_ld.inc(e.dma_start(out=muk_c[:], in_=w["mu_k"]), 16)
        on("sync", f)

        def f(e):
            s_ldg.inc(e.dma_start(out=wqrt_t[:], in_=w["W_QRT"]), 16)
            s_ldg.inc(e.dma_start(out=wkrt_blk[0:64, 0:64], in_=w["W_KRT"]), 16)
            s_ldg.inc(e.dma_start(out=wkrt_blk[64:128, 64:128], in_=w["W_KRT"]), 16)
            s_ldg.inc(e.dma_start(out=f0bias[:], in_=w["f0_bias"]), 16)
            if phases >= 3:
                for n in ("Wq", "Wk", "Wv", "Wa", "Wg", "Wo", "W_rT", "W_kT", "W_vT"):
                    s_ldg.inc(e.dma_start(out=wbf[n], in_=w[n]), 16)
            for t_ in (mqt_blk, mkt_blk, mvt_blk, ones2blk, ind2_f, zeros64, ident_f):
                e.memset(t_[:], 0.0)
            e.memset(wkrt_blk[0:64, 64:128], 0.0)
            e.memset(wkrt_blk[64:128, 0:64], 0.0)
            e.drain()
            e.memset(ones2blk[0:64, 0:1], 1.0)
            e.memset(ones2blk[64:128, 1:2], 1.0)
            e.memset(ind2_f[0:2, 64:128], 1.0)
            e.drain()
            e.memset(ind2_f[0:1, 64:128], 0.0)
            e.memset(ind2_f[0:1, 0:64], 1.0)
            e.drain()
            for b0 in (32, 64, 96):
                e.tensor_copy(ind2_f[b0:b0 + 2, :], ind2_f[0:2, :])
            e.memset(onesrow_f[:], 1.0)
            e.memset(ones128_f[:], 1.0)
            e.memset(onescol_b[:], 1.0)
            e.memset(onescol_f[:], 1.0)
            e.memset(epsc[:, 0:1], RMS_EPS)
            e.memset(epsc[:, 1:2], LN_EPS)
            e.memset(epsc[:, 2:3], HLN_EPS)
            e.memset(cscr[:], 1.0)
            e.drain()
            e.affine_select(ident_f[:], cscr[:, 0:128], [[-1, 128]], OP.is_equal, 0.0,
                            base=0, channel_multiplier=1)
            e.affine_select(cscr[:, 128:256], cscr[:, 0:128], [[1, 128]], OP.is_ge, 0.0,
                            base=0, channel_multiplier=-1)
            e.drain()
            e.tensor_copy(ident_b[:], ident_f[:])
            e.tensor_copy(tri0[:], cscr[:, 128:256])
            e.drain()
            e.memset(cscr[:], -1.0)
            e.drain()
            e.affine_select(cscr[0:64, 0:512], cscr[64:128, 0:512], [[0, 8], [1, 64]],
                            OP.is_gt, 0.0, base=0, channel_multiplier=-1)
            e.drain()
            e.tensor_copy(maskAn[:], cscr[0:64, 0:512])
            e.drain()
            e.memset(cscr[64:128, :], -1.0)
            e.drain()
            e.affine_select(cscr[0:64, 0:512], cscr[64:128, 0:512], [[0, 8], [1, 64]],
                            OP.is_ge, 0.0, base=0, channel_multiplier=-1)
            e.drain()
            e.tensor_copy(maskBn[:], cscr[0:64, 0:512])
            s_g.inc(e.drain())
        on("gpsimd", f)

        def f(e):
            s_ld.wait(e)
            s_ldg.wait(e)
            s_g.wait(e)
            for a in range(4):
                e.matmul(pb[0][0:64, 0:64], wdq[:, a * 64:(a + 1) * 64], wuqt[:, a * 64:(a + 1) * 64],
                         start=(a == 0), stop=(a == 3))
            for a in range(8):
                e.matmul(pb[1][0:64, 0:128], wdkv[:, a * 64:(a + 1) * 64], wukvt[:, a * 128:(a + 1) * 128],
                         start=(a == 0), stop=(a == 7))
            s_t.inc(e.drain())
        on("tensor", f)

        def f(e):
            s_t.wait(e)
            e.copy(mqt_blk[0:64, 0:64], pb[0][0:64, 0:64])
            e.copy(mqt_blk[64:128, 64:128], pb[0][0:64, 0:64])
            e.copy(mkt_blk[0:64, 0:64], pb[1][0:64, 0:64])
            e.copy(mkt_blk[64:128, 64:128], pb[1][0:64, 0:64])
            e.copy(mvt_blk[0:64, 0:64], pb[1][0:64, 64:128])
            e.copy(mvt_blk[64:128, 64:128], pb[1][0:64, 64:128])
            s_s.inc(e.drain())
        on("scalar", f)
        barrier()

    def emit_module_ln(ysrc, dst, sqscr):
        mlv = {}
        for i in range(8):
            def f(e, i=i):
                if i >= 2:
                    s_t.wait(e, mlv[i - 2])
                s_s.inc(e.activation(sqscr[i % 2][:], ysrc[i][:], AT.Square))
            on("scalar", f)

            def f(e, i=i):
                s_s.wait(e)
                for tt in range(4):
                    mm = e.matmul(pb[tt][0:1, 0:512], onescol_b[:],
                                  sqscr[i % 2][:, tt * 512:(tt + 1) * 512],
                                  start=(i == 0), stop=(i == 7))
                s_t.inc(mm)
                mlv[i] = s_t.v
            on("tensor", f)

        def f(e):
            for i in range(8):
                for tt in range(4):
                    mm = e.matmul(pb[4 + tt][0:1, 0:512], onescol_f[:],
                                  ysrc[i][:, tt * 512:(tt + 1) * 512],
                                  start=(i == 0), stop=(i == 7))
            s_t.inc(mm)
        on("tensor", f)

        for tt in range(4):
            def f(e, tt=tt):
                s_t.wait(e)
                e.tensor_scalar(statsa[0:1, 0:512], pb[4 + tt][0:1, 0:512], 1.0 / D, None, OP.mult)
                e.tensor_scalar(statsa[0:1, 512:1024], pb[tt][0:1, 0:512], 1.0 / D, None, OP.mult)
                e.drain()
                e.tensor_mul(statsa[0:1, 1024:1536], statsa[0:1, 0:512], statsa[0:1, 0:512])
                e.drain()
                s_v.inc(e.tensor_sub(statsa[0:1, 1536:2048], statsa[0:1, 512:1024], statsa[0:1, 1024:1536]))
            on("vector", f)

            def f(e):
                s_v.wait(e)
                e.activation(statsa[0:1, 1024:1536], statsa[0:1, 1536:2048], AT.Ln, bias=LN_EPS)
                e.drain()
                s_s.inc(e.activation(statsa[0:1, 1536:2048], statsa[0:1, 1024:1536], AT.Exp, scale=-0.5))
            on("scalar", f)

            def f(e, tt=tt):
                s_s.wait(e)
                s_v.wait(e)
                e.matmul(pb[tt][0:128, 0:512], ones128_f[0:1, :],
                         statsa[0:1, 0:512], start=True, stop=True)
                s_t.inc(e.matmul(pb[4 + tt][0:128, 0:512], ones128_f[0:1, :],
                                 statsa[0:1, 1536:2048], start=True, stop=True))
            on("tensor", f)

            def f(e, tt=tt):
                s_t.wait(e)
                sl = slice(tt * 512, (tt + 1) * 512)
                for i in range(8):
                    e.tensor_sub(lntmp[:], ysrc[i][:, sl], pb[tt][:, 0:512])
                    e.drain()
                    mu = e.tensor_mul(dst[i][:, sl] if not isinstance(dst[i], bass.AP) else dst[i][:, sl], lntmp[:], pb[4 + tt][:, 0:512])
                    if i < 7:
                        e.drain()
                s_v.inc(mu)
            on("vector", f)
        barrier()

    def l0_attn_v2():
        ph = ExitStack()
        sqS = sb("sqS", 128, T, BF16, ph)
        qt = [sb(f"qt{i}", 128, T, BF16, ph) for i in range(4)]
        kt_ = [sb(f"kt{i}", 128, T, BF16, ph) for i in range(4)]
        vt = [sb(f"vt{i}", 128, 16 * 130, BF16, ph) for i in range(2)]
        rstdbQ = sqS
        rstdbK = sqS
        expsT = [sb(f"expsT{i}", 128, 512, BF16, ph) for i in range(2)]
        expsTb = [sb(f"expsTb{i}", 128, 512, BF16, ph) for i in range(2)]
        expTmp = sb("expTmp", 128, 128, BF16, ph)
        expTmp2 = sb("expTmp2", 128, 128, BF16, ph)
        ctxE = sb("ctxE", 64, 512, F32, ph)
        attnp = [sb(f"attnp{i}", 128, T, BF16, ph) for i in range(2)]
        sqf = [sqS, attnp[0]]

        def qcv(p, c0, c1):
            return xb[p][:, c0 // 2:c1 // 2].bitcast(BF16)

        def kcv(p, c0, c1):
            return xb[p][:, 1024 + c0 // 2:1024 + c1 // 2].bitcast(BF16)

        for g in range(4):
            pairs = (2 * g, 2 * g + 1)

            for pi, p in enumerate(pairs):
                for kind, blkw, dstv in (("qc", mqt_blk, qcv), ("kc", mkt_blk, kcv)):
                    def f(e, blkw=blkw, p=p):
                        s_s.wait(e)
                        for tt in range(4):
                            mm = e.matmul(pb[tt][0:128, 0:512], blkw[:],
                                          xa[p][:, tt * 512:(tt + 1) * 512],
                                          start=True, stop=True)
                        s_t.inc(mm)
                    on("tensor", f)

                    def f(e, dstv=dstv, p=p):
                        s_t.wait(e)
                        for tt in range(4):
                            cp = e.copy(dstv(p, tt * 512, (tt + 1) * 512), pb[tt][0:128, 0:512])
                        s_s.inc(cp)
                    on("scalar", f)

            for pi, p in enumerate(pairs):
                for grp in range(4):
                    def f(e, p=p, grp=grp):
                        s_s.wait(e)
                        for k in range(4):
                            sl = grp * 4 + k
                            mm = e.matmul(pb[4][0:128, k * 128:(k + 1) * 128],
                                          xa[p][:, sl * 128:(sl + 1) * 128], mvt_blk[:],
                                          start=True, stop=True)
                        s_t.inc(mm)
                    on("tensor", f)

                    def f(e, pi=pi, grp=grp):
                        s_t.wait(e)
                        src = pb[4][0:128, 0:512].rearrange("p (s h v) -> p s h v", s=4, h=2)
                        dstv = vt[pi][:, grp * 4 * 130:(grp * 4 + 4) * 130] \
                            .rearrange("p (s c) -> p s c", s=4)[:, :, 0:130] \
                            .rearrange("p s (h c) -> p s h c", h=2)[:, :, :, 0:64] if False else None
                        e.copy(vt[pi][:].rearrange("p (s c) -> p s c", s=16)
                               [:, grp * 4:grp * 4 + 4, 0:64],
                               pb[4][0:128, 0:512].rearrange("p (s h v) -> p s h v", s=4, h=2)[:, :, 0, :])
                        s_s.inc(e.copy(vt[pi][:].rearrange("p (s c) -> p s c", s=16)
                               [:, grp * 4:grp * 4 + 4, 65:129],
                               pb[4][0:128, 0:512].rearrange("p (s h v) -> p s h v", s=4, h=2)[:, :, 1, :]))
                    on("scalar", f)

            def f(e, pairs=pairs):
                for pi in range(2):
                    e.memset(vt[pi][:].rearrange("p (s c) -> p s c", s=16)[:, :, 64:65], 1.0)
                    e.memset(vt[pi][:].rearrange("p (s c) -> p s c", s=16)[:, :, 129:130], 1.0)
                s_g.inc(e.drain())
            on("gpsimd", f)

            for pi in range(2):
                for qk in (0, 1):
                    vsel = qcv if qk == 0 else kcv
                    dstb = rstdbQ if qk == 0 else rstdbK

                    def f(e, vsel=vsel, p=pairs[pi]):
                        s_s.wait(e)
                        s_t.wait(e)
                        s_v.wait(e)
                        e.activation(sqS[:], vsel(p, 0, 2048), AT.Square)
                        s_s.inc(e.drain())
                    on("scalar", f)

                    def f(e):
                        s_s.wait(e)
                        for tt in range(4):
                            mm = e.matmul(pb[4 + tt][0:2, 0:512], ones2blk[:],
                                          sqS[:, tt * 512:(tt + 1) * 512], start=True, stop=True)
                        s_t.inc(mm)
                    on("tensor", f)

                    def f(e):
                        s_t.wait(e)
                        for tt in range(4):
                            e.activation(statsa[32:34, tt * 512:(tt + 1) * 512],
                                         pb[4 + tt][0:2, 0:512],
                                         AT.Ln, bias=RMS_EPS, scale=1.0 / 64)
                        e.drain()
                        for tt in range(4):
                            ac = e.activation(statsa[0:2, tt * 512:(tt + 1) * 512],
                                              statsa[32:34, tt * 512:(tt + 1) * 512],
                                              AT.Exp, scale=-0.5)
                        s_s.inc(ac)
                    on("scalar", f)

                    bcv = {}
                    for tt in range(4):
                        def f(e, tt=tt):
                            if tt < 2:
                                s_s.wait(e)
                            else:
                                s_s.wait(e, bcv[tt - 2])
                            s_t.inc(e.matmul(pb[2 + tt % 2][0:128, 0:512], ind2_f[0:2, :],
                                     statsa[0:2, tt * 512:(tt + 1) * 512], start=True, stop=True))
                        on("tensor", f)

                        def f(e, dstb=dstb, tt=tt):
                            s_t.wait(e)
                            s_s.inc(e.copy(dstb[:, tt * 512:(tt + 1) * 512],
                                           pb[2 + tt % 2][0:128, 0:512]))
                            bcv[tt] = s_s.v
                        on("scalar", f)

                    def f(e, pi=pi, qk=qk, dstb=dstb):
                        s_s.wait(e)
                        vsel = qcv if qk == 0 else kcv
                        p = pairs[pi]
                        for hh in range(2):
                            hidx = 2 * pi + hh
                            dst = (qt if qk == 0 else kt_)[hidx]
                            nwc = qnw if qk == 0 else knw
                            full = vsel(p, 0, 2048)
                            st = e.scalar_tensor_tensor(dst[0:64, :], full[hh * 64:hh * 64 + 64, :],
                                                   nwc[hh * 64:hh * 64 + 64, :],
                                                   dstb[hh * 64:hh * 64 + 64, :],
                                                   OP.mult, OP.mult)
                        s_v.inc(st)
                    on("vector", f)

            for pi, p in enumerate(pairs):
                def f(e, p=p):
                    s_s.wait(e)
                    s_v.wait(e)
                    for tt in range(4):
                        mm = e.matmul(pb[tt][0:128, 0:512], wkrt_blk[:],
                                      xa[p][:, tt * 512:(tt + 1) * 512], start=True, stop=True)
                    s_t.inc(mm)
                on("tensor", f)

                def f(e, pi=pi):
                    s_t.wait(e)
                    for tt in range(4):
                        e.copy(kt_[2 * pi][64:128, tt * 512:(tt + 1) * 512], pb[tt][0:64, 0:512])
                        cp = e.copy(kt_[2 * pi + 1][64:128, tt * 512:(tt + 1) * 512], pb[tt][64:128, 0:512])
                    s_s.inc(cp)
                on("scalar", f)
            for hidx in range(4):
                def f(e, hidx=hidx):
                    s_s.wait(e)
                    s_v.wait(e)
                    for tt in range(4):
                        mm = e.matmul(pb[4 + tt][0:64, 0:512], wqrt_t[:],
                                      qt[hidx][0:64, tt * 512:(tt + 1) * 512], start=True, stop=True)
                    s_t.inc(mm)
                on("tensor", f)

                def f(e, hidx=hidx):
                    s_t.wait(e)
                    for tt in range(4):
                        cp = e.copy(qt[hidx][64:128, tt * 512:(tt + 1) * 512], pb[4 + tt][0:64, 0:512])
                    s_s.inc(cp)
                on("scalar", f)

            barrier()

            SC = 0.125
            svals = {}
            pk_abs = [0]
            for hh in range(4):
                hpair = hh % 2
                pi = hh // 2
                p = pairs[pi]
                qth, kth, vta = qt[hh], kt_[hh], vt[pi]
                vcol = hpair * 65
                for qsup in range(4):
                    nkt = 4 * qsup + 4
                    for pk in range(nkt // 2):
                        pa = pk_abs[0]
                        pk_abs[0] += 1
                        kts = (2 * pk, 2 * pk + 1)
                        bks = (pb[0], pb[1]) if pa % 2 == 0 else (pb[5], pb[6])
                        exs = (expsT[pa % 2], expsTb[pa % 2])
                        parts = tuple(kt >= 4 * qsup for kt in kts)
                        j0s = tuple((kt - 4 * qsup) * 128 if kt >= 4 * qsup else 0 for kt in kts)
                        FDs = tuple(512 - j0 for j0 in j0s)

                        def f(e, kts=kts, bks=bks, FDs=FDs, qsup=qsup, j0s=j0s,
                              qth=qth, kth=kth, pa=pa):
                            if pa >= 2:
                                s_s.wait(e, svals.get(pa - 2, 0))
                            for kt, bk, j0, FD in zip(kts, bks, j0s, FDs):
                                mm = e.matmul(bk[0:128, 0:FD], kth[:, kt * 128:(kt + 1) * 128],
                                              qth[:, qsup * 512 + j0:(qsup + 1) * 512],
                                              start=True, stop=True)
                            s_t.inc(mm)
                        on("tensor", f)

                        def f(e, bks=bks, exs=exs, parts=parts, FDs=FDs, pa=pa):
                            s_t.wait(e)
                            for i2 in range(2):
                                bk, ex, part, FD = bks[i2], exs[i2], parts[i2], FDs[i2]
                                tmp = expTmp if i2 == 0 else expTmp2
                                if part:
                                    ac = e.activation(tmp[:], bk[0:128, 0:128], AT.Exp, scale=SC)
                                    if FD > 128:
                                        ac = e.activation(ex[:, 128:FD], bk[0:128, 128:FD],
                                                          AT.Exp, scale=SC)
                                else:
                                    ac = e.activation(ex[:, 0:FD], bk[0:128, 0:FD], AT.Exp, scale=SC)
                            s_s.inc(ac)
                            svals[pa] = s_s.v
                        on("scalar", f)

                        if parts[0] or parts[1]:
                            def f(e, exs=exs, parts=parts):
                                s_s.wait(e)
                                for i2 in range(2):
                                    if parts[i2]:
                                        tmp = expTmp if i2 == 0 else expTmp2
                                        mu = e.tensor_mul(exs[i2][:, 0:128], tmp[:], tri0[:])
                                s_v.inc(mu)
                            on("vector", f)

                        def f(e, kts=kts, exs=exs, parts=parts, j0s=j0s, FDs=FDs,
                              nkt=nkt, vta=vta, vcol=vcol):
                            s_s.wait(e)
                            if parts[0] or parts[1] or kts[0] == 0:
                                s_v.wait(e)
                            for i2 in range(2):
                                kt, ex, j0, FD = kts[i2], exs[i2], j0s[i2], FDs[i2]
                                mm = e.matmul(pb[2][0:65, j0:512],
                                              vta[:, kt * 130 + vcol:kt * 130 + vcol + 65],
                                              ex[:, 0:FD], start=(kt == 0), stop=(kt == nkt - 1))
                            s_t.inc(mm)
                        on("tensor", f)

                    def f(e):
                        s_t.wait(e)
                        e.reciprocal(statsa[0:1, 0:512], pb[2][64:65, 0:512])
                        e.drain()
                        e.tensor_copy(ctxE[:], pb[2][0:64, 0:512])
                        s_v.inc(e.drain())
                    on("vector", f)

                    def f(e):
                        s_v.wait(e)
                        e.matmul(pb[3][0:64, 0:512], wot_t[:], ctxE[:], start=True, stop=True)
                        s_t.inc(e.matmul(pb[4][0:64, 0:512], onesrow_f[0:1, 0:64],
                                 statsa[0:1, 0:512], start=True, stop=True))
                    on("tensor", f)

                    def f(e):
                        s_t.wait(e)
                        s_v.wait(e)
                        e.tensor_copy(lntmp[0:64, :], pb[4][0:64, 0:512])
                        s_v.inc(e.drain())
                    on("vector", f)

                    def f(e, pi=pi, hpair=hpair, qsup=qsup):
                        s_t.wait(e)
                        s_v.inc(e.tensor_mul(attnp[pi][hpair * 64:hpair * 64 + 64, qsup * 512:(qsup + 1) * 512],
                                             pb[3][0:64, 0:512], lntmp[0:64, :]))
                    on("vector", f)

            def f(e, pairs=pairs):
                s_v.wait(e)
                for pi, p in enumerate(pairs):
                    ad = e.tensor_add(xb[p][:], xa[p][:], attnp[pi][:])
                s_g.inc(ad)
            on("gpsimd", f)
            barrier()

        dump("y1", xb)
        emit_module_ln(xb, xa, sqf)
        dump("x1", xa)
        ph.close()

    def l0_ffn():
        ph = ExitStack()
        x1q = [sb(f"x1q{i}", 128, T, F8, ph) for i in range(8)]
        kq8 = [sb(f"kq8{i}", 128, D, F8, ph) for i in range(8)]
        kbf2 = [sb(f"kbf2_{i}", 128, D, BF16, ph) for i in range(2)]
        xbf = sb("xbf", 128, T, BF16, ph)
        ybf2 = [sb(f"ybf{i}", 128, 512, BF16, ph) for i in range(2)]
        sq2 = sb("sq2", 128, T, BF16, ph)
        sqf = [xbf, sq2]

        def f(e):
            e.reciprocal(srow[0:1, 2:3], srow[0:1, 0:1])
            e.drain()
            e.reciprocal(srow[0:1, 3:4], srow[0:1, 1:2])
            e.drain()
            e.tensor_mul(srow[0:1, 4:5], srow[0:1, 0:1], srow[0:1, 1:2])
            s_v.inc(e.drain())
        on("vector", f)

        def f(e):
            s_v.wait(e)
            e.matmul(pb[0][0:128, 0:1], onesrow_f[:], srow[0:1, 2:3], start=True, stop=True)
            e.matmul(pb[0][0:128, 1:2], onesrow_f[:], srow[0:1, 3:4], start=True, stop=True)
            s_t.inc(e.matmul(pb[0][0:128, 2:3], onesrow_f[:], srow[0:1, 4:5], start=True, stop=True))
        on("tensor", f)

        def f(e):
            s_t.wait(e)
            e.copy(sinv_c[:], pb[0][0:128, 0:1])
            e.copy(skinv_c[:], pb[0][0:128, 1:2])
            e.copy(s2_c[:], pb[0][0:128, 2:3])
            s_s.inc(e.drain())
        on("scalar", f)

        for ci in range(8):
            sem = s_lda if ci % 2 == 0 else s_ldb

            def f(e, ci=ci, sem=sem):
                if ci < 2:
                    e.wait_ge(s_bar.h, s_bar.v)
                else:
                    s_s.wait(e)
                sem.inc(e.dma_start(out=kbf2[ci % 2][:], in_=w["f0_kernel"][ci * 128:(ci + 1) * 128, :]), 16)
            on("sync", f)

            def f(e, ci=ci, sem=sem):
                sem.wait(e)
                s_s.inc(e.activation(kq8[ci][:], kbf2[ci % 2][:], AT.Copy, scale=skinv_c[:]))
            on("scalar", f)

        def f(e):
            s_s.wait(e)
            xh = [xbf[:, 0:1024], xbf[:, 1024:2048]]
            for i in range(16):
                src_ap = xa[i // 2][:, (i % 2) * 1024:(i % 2 + 1) * 1024]
                dst_ap = x1q[i // 2][:, (i % 2) * 1024:(i % 2 + 1) * 1024]
                e.activation(xh[i % 2], src_ap, AT.Copy)
                e.drain()
                ac = e.activation(dst_ap, xh[i % 2], AT.Copy, scale=sinv_c[:])
            s_s.inc(ac)
        on("scalar", f)
        barrier()

        tiles = [(ot, tt) for ot in range(8) for tt in range(4)]
        gvals = {}
        for pi2 in range(16):
            pr = tiles[2 * pi2:2 * pi2 + 2]
            bks = (pb[0], pb[1]) if pi2 % 2 == 0 else (pb[2], pb[3])

            def f(e, pr=pr, bks=bks, pi2=pi2):
                if pi2 < 2:
                    s_v.wait(e)
                else:
                    s_v.wait(e, gvals[pi2 - 2])
                for (ot, tt), bank in zip(pr, bks):
                    for ci in range(8):
                        mm = e.matmul(bank[0:128, 0:512], kq8[ci][:, ot * 128:(ot + 1) * 128],
                                      x1q[ci][:, tt * 512:(tt + 1) * 512],
                                      start=(ci == 0), stop=(ci == 7))
                s_t.inc(mm)
            on("tensor", f)

            def f(e, pr=pr, bks=bks, pi2=pi2):
                s_t.wait(e)
                s_v.wait(e)
                for i2, ((ot, tt), bank) in enumerate(zip(pr, bks)):
                    ac = e.activation(ybf2[i2][:], bank[0:128, 0:512], AT.Copy, scale=s2_c[:])
                s_s.inc(ac)
            on("scalar", f)

            def f(e, pr=pr, pi2=pi2):
                s_s.wait(e)
                for i2, (ot, tt) in enumerate(pr):
                    e.tensor_scalar(sq2[:, i2 * 512:(i2 + 1) * 512], ybf2[i2][:],
                                    f0bias[:, ot:ot + 1], None, OP.add)
                e.drain()
                for i2, (ot, tt) in enumerate(pr):
                    ad = e.tensor_add(xb[ot][:, tt * 512:(tt + 1) * 512],
                                      sq2[:, i2 * 512:(i2 + 1) * 512],
                                      xa[ot][:, tt * 512:(tt + 1) * 512])
                s_v.inc(ad)
                gvals[pi2] = s_v.v
            on("vector", f)
        barrier()
        dump("y2", xb)
        emit_module_ln(xb, x2, sqf)
        dump("x2", [t_ for t_ in x2])
        ph.close()

    def l1_attn(l1s):
        wb_s = sb("wb_s", 128, 64, BF16, l1s)
        wcur = [sb(f"wcur{i}", 128, 2048, BF16, l1s) for i in range(2)]
        vpost = sb("vpost", 128, 4096, BF16, l1s)
        awin = sb("awin", 128, 4096, BF16, l1s)
        betaw = sb("betaw", H1, WIN, F32, l1s)
        betaf = lntmp[0:1, 0:512]
        gscr = sb("gscr", 128, 552, F32, l1s)
        giscr = sb("giscr", 128, 552, F32, l1s)
        convt = [sb(f"convt{i}", 128, 512, F32, l1s) for i in range(2)]
        SST = sb("SST", 128, 1024, BF16, l1s)
        wS = sb("wS", 128, 512, BF16, l1s)
        wping = [sb(f"wping{i}", 128, 128, BF16, l1s) for i in range(2)]
        rpS = sb("rpS", 128, 512, BF16, l1s)
        arecipS = sb("arecipS", 128, 512, F32, l1s)
        recipBA = sb("recipBA", 128, 512, F32, l1s)
        AgrS = sb("AgrS", 64, 512, BF16, l1s)
        BgrS = sb("BgrS", 64, 512, BF16, l1s)
        BDset = sb("BDset", 128, 1280, BF16, l1s)
        wCons = sb("wCons", 64, 1024, BF16, l1s)
        ktokN = sb("ktokN", 64, 1024, BF16, l1s)
        gCrows = sb("gCrows", 128, 256, F32, l1s)
        gCrow0 = sb("gCrow0", 1, 1024, F32, l1s)
        M8 = sb("M8", 128, 128, BF16, l1s)
        maskBnP = sb("maskBnP", 64, 512, BF16, l1s)
        R16S = sb("R16S", 64, 512, BF16, l1s)
        rv = statsa[:, 0:256].bitcast(BF16)
        Vg = statsa[:, 256:512].bitcast(BF16)
        bndq = sb("bndq", 128, 24, BF16, l1s)
        bndk = sb("bndk", 128, 24, BF16, l1s)
        bndv = sb("bndv", 128, 24, BF16, l1s)
        bndx = sb("bndx", 128, 8, BF16, l1s)
        y3w = None

        def xbv(i):
            return xb[i][:].bitcast(BF16)

        def pre_ap(P, od, c0, c1):
            t_ = xb[P * 2 + od // 4][:]
            base = (od % 4) * 515
            return t_[:, (base + c0) // 2 * 0:0] if False else xb[P * 2 + od // 4][:].bitcast(BF16)[:, base + c0:base + c1]

        def post_ap(P, c0, c1):
            if P == 0:
                return xb[6][:].bitcast(BF16)[:, c0:c1]
            if P == 1:
                return xb[7][:].bitcast(BF16)[:, c0:c1]
            return vpost[:, c0:c1]

        wcur_free = {"v": 0}

        def f(e):
            s_ldg.inc(e.dma_start(out=wb_s[:], in_=w["Wb"]), 16)
            e.memset(BDset[:], 0.0)
            e.memset(SST[:], 0.0)
            e.memset(gscr[:, 520:552], 0.0)
            e.memset(giscr[:, 520:552], 0.0)
            e.memset(bndq[:], 0.0)
            e.memset(bndk[:], 0.0)
            e.memset(bndv[:], 0.0)
            e.memset(bndx[:], 0.0)
            e.drain()
            for i in range(NTIER):
                e.tensor_copy(R16S[:, i * 128:(i + 1) * 128],
                              bass.AP(tensor=ident_b, offset=16 * i,
                                      ap=[[128, 64], [0, 8], [1, 16]]))
            s_g.inc(e.drain())
        on("gpsimd", f)

        def f(e):
            s_g.wait(e)
            e.tensor_reduce(statsa[:, 0:8],
                            ident_b[:].rearrange("p (h t) -> p h t", t=16),
                            AX.X, OP.add)
            e.drain()
            e.tensor_copy(M8[:].rearrange("p (h t) -> p h t", t=16),
                          bass.AP(tensor=statsa, offset=0,
                                  ap=[[2048, 128], [1, 8], [0, 16]]))
            s_v.inc(e.drain())
        on("vector", f)
        barrier()

        for wi in range(NWIN if l1_nwin is None else l1_nwin):
            l1_window(wi, locals())

    def l1_window(wi, V):
        wb_s = V["wb_s"]; wcur = V["wcur"]; vpost = V["vpost"]; awin = V["awin"]
        wcur_free = V["wcur_free"]
        betaw = V["betaw"]; betaf = V["betaf"]
        ggwin = xb[0][:].bitcast(BF16)
        owin = xb[1][:].bitcast(BF16)
        gscr = V["gscr"]; convt = V["convt"]; SST = V["SST"]; wS = V["wS"]
        giscr = V["giscr"]
        wping = V["wping"]; rpS = V["rpS"]; arecipS = V["arecipS"]; recipBA = V["recipBA"]
        AgrS = V["AgrS"]; BgrS = V["BgrS"]; BDset = V["BDset"]; wCons = V["wCons"]
        ktokN = V["ktokN"]; Vg = V["Vg"]; gCrows = V["gCrows"]; rv = V["rv"]
        gCrow0 = V["gCrow0"]
        M8 = V["M8"]; maskBnP = V["maskBnP"]; R16S = V["R16S"]
        bndq = V["bndq"]; bndk = V["bndk"]; bndv = V["bndv"]
        pre_ap = V["pre_ap"]; post_ap = V["post_ap"]

        tw = slice(wi * WIN, (wi + 1) * WIN)
        x2w = lambda ci: x2[ci][:, tw]


        def f(e):
            if wi == 0:
                s_ldg.wait(e)
            s_g.wait(e)
            s_v.wait(e)
            for ci in range(8):
                mm = e.matmul(pb[6][0:8, 0:512], wb_s[:, ci * 8:(ci + 1) * 8], x2w(ci),
                              start=(ci == 0), stop=(ci == 7))
            s_t.inc(mm)
        on("tensor", f)

        def f(e):
            s_t.wait(e)
            s_s.inc(e.activation(betaw[:, 0:WIN], pb[6][0:8, 0:512], AT.Sigmoid, bias=bb_c[:]))
        on("scalar", f)

        cwn = {0: "cqw", 1: "ckw", 2: "cvw"}
        cbn = {0: "cqb", 1: "ckb", 2: "cvb"}

        def restore_boundary(P):
            def f(e, P=P):
                s_s.wait(e)
                bnd = (bndq, bndk, bndv)[P]
                for od in range(8):
                    e.tensor_copy(pre_ap(P, od, 0, 3), bnd[:, od * 3:od * 3 + 3])
                e.drain()
                for od in range(8):
                    e.tensor_copy(bnd[:, od * 3:od * 3 + 3], pre_ap(P, od, 512, 515))
                s_g.inc(e.drain())
            on("gpsimd", f)

        def emit_conv(P):
            restore_boundary(P)
            for od in range(8):
                def f(e, P=P, od=od):
                    s_g.wait(e)
                    s_s.wait(e)
                    wv = cw[cwn[P]]
                    e.tensor_scalar(convt[od % 2][:], pre_ap(P, od, 0, 512),
                                    wv[:, od * 4 + 0:od * 4 + 1], None, OP.mult)
                    e.drain()
                    for j in range(1, 4):
                        st = e.scalar_tensor_tensor(convt[od % 2][:], pre_ap(P, od, j, j + 512),
                                                    wv[:, od * 4 + j:od * 4 + j + 1],
                                                    convt[od % 2][:], OP.mult, OP.add)
                        if j < 3:
                            e.drain()
                    s_v.inc(st)
                on("vector", f)

                def f(e, P=P, od=od):
                    s_v.wait(e)
                    if P <= 1 and od == 0:
                        s_st.wait(e)
                    if P == 1:
                        stg = lntmp[:, 0:256].bitcast(BF16)
                        s_s.inc(e.activation(stg, convt[od % 2][:], AT.Silu, bias=cb[cbn[P]][:, od:od + 1]))
                    else:
                        dst = post_ap(P, od * 512, (od + 1) * 512)
                        s_s.inc(e.activation(dst, convt[od % 2][:], AT.Silu, bias=cb[cbn[P]][:, od:od + 1]))
                on("scalar", f)
                if P == 1:
                    def f(e, od=od):
                        s_s.wait(e)
                        stg = lntmp[:, 0:256].bitcast(BF16)
                        s_v.inc(e.tensor_scalar(post_ap(1, od * 512, (od + 1) * 512), stg, float(128 ** -0.5), None, OP.mult))
                    on("vector", f)

        projs = (("Wq", 0), ("Wk", 1), ("Wv", 2), ("Wa", 3), ("Wg", 4))
        ptvals = {}
        pentry = s_t.v
        for mi, (mn, P) in enumerate(projs):
            for qtr in range(4):
                def f(e, mn=mn, qtr=qtr, mi=mi):
                    qi = mi * 4 + qtr
                    if qi >= 2:
                        s_t.wait(e, ptvals[qi - 2])
                    elif wi > 0:
                        s_t.wait(e, wcur_free["v"])
                    sem = s_mv if qi % 2 == 0 else s_mw
                    sem.inc(e.dma_start(
                        out=wcur[qtr % 2][:].rearrange("p (a m) -> p a m", m=1024),
                        in_=wbf[mn][qtr * 256:(qtr + 1) * 256, :].rearrange("(a p) m -> p a m", p=128)), 16)
                on("gpsimd", f)

                def f(e, qtr=qtr, mi=mi, P=P):
                    (s_mv if (mi * 4 + qtr) % 2 == 0 else s_mw).wait(e)
                    if qtr == 0:
                        s_s.wait(e)
                        s_g.wait(e)
                        if P == 4:
                            s_v.wait(e)
                    buf = wcur[qtr % 2]
                    for od in range(8):
                        for c2 in range(2):
                            ci = qtr * 2 + c2
                            mm = e.matmul(pb[od][0:128, 0:512],
                                          buf[:, c2 * 1024 + od * 128:c2 * 1024 + (od + 1) * 128],
                                          x2w(ci), start=(ci == 0), stop=(ci == 7))
                    s_t.inc(mm)
                    ptvals[mi * 4 + qtr] = s_t.v
                on("tensor", f)

            def f(e, P=P):
                s_t.wait(e)
                for od in range(8):
                    bank = pb[od]
                    if P <= 2:
                        op = e.copy(pre_ap(P, od, 3, 515), bank[0:128, 0:512])
                    elif P == 3:
                        op = e.activation(awin[:, od * 512:(od + 1) * 512], bank[0:128, 0:512],
                                          AT.Sigmoid, bias=ba_c[:, od:od + 1])
                    else:
                        op = e.activation(ggwin[:, od * 512:(od + 1) * 512], bank[0:128, 0:512],
                                          AT.Sigmoid)
                s_s.inc(op)
            on("scalar", f)
            if P <= 2:
                emit_conv(P)
        barrier()
        if wi == 0 and "l1a" in dbg:
            dump("qpost", [post_ap(0, 0, 4096)])
            dump("kpost", [post_ap(1, 0, 4096)])
            dump("vpost", [post_ap(2, 0, 4096)])
            dump("awin", [awin[:]])
            dump("ggwin", [ggwin[:]])
            dump("betaw", [betaw[:]])

        V = dict(V)
        V["_wi"] = wi
        V["_kpost_t"] = xb[7]
        V["_kpost_off"] = 0
        V["_kpost_pitch"] = 4096
        if l1_stop == "proj":
            return
        for c in range(CPW if l1_nchunk is None else l1_nchunk):
            l1_chunk(wi, c, V)
        if wi == 0 and "owin" in dbg:
            dump("owin", [xb[1][:].bitcast(BF16)])
        if l1_stop == "chunks":
            return

        l1_tail(wi, V)

    def l1_chunk(wi, c, V):
        awin = V["awin"]; vpost = V["vpost"]; betaw = V["betaw"]; betaf = V["betaf"]
        gscr = V["gscr"]; giscr = V["giscr"]; SST = V["SST"]; wS = V["wS"]; wping = V["wping"]
        convt = V["convt"]
        rpS = V["rpS"]; arecipS = V["arecipS"]; recipBA = V["recipBA"]
        AgrS = V["AgrS"]; BgrS = V["BgrS"]; BDset = V["BDset"]; wCons = V["wCons"]
        ktokN = V["ktokN"]; Vg = V["Vg"]; gCrows = V["gCrows"]; rv = V["rv"]
        gCrow0 = V["gCrow0"]
        M8 = V["M8"]; maskBnP = V["maskBnP"]; R16S = V["R16S"]
        pre_ap = V["pre_ap"]; post_ap = V["post_ap"]
        owin = xb[1][:].bitcast(BF16)
        c0 = c * CH

        def kx(h):
            return post_ap(1, h * 512 + c0, h * 512 + c0 + CH)

        def qx(h):
            return post_ap(0, h * 512 + c0, h * 512 + c0 + CH)

        aw_ap = awin[:].rearrange("p (h t) -> p h t", h=8)[:, :, c0:c0 + CH]
        vw_ap = vpost[:].rearrange("p (h t) -> p h t", h=8)[:, :, c0:c0 + CH]
        g_ap = gscr[:, 0:520].rearrange("p (h t) -> p h t", h=8)
        gprev_ap = g_ap[:, :, 0:CH]
        gcur_ap = g_ap[:, :, 1:CH + 1]

        def f(e):
            s_v.wait(e)
            s_t.wait(e)
            e.memset(gscr[:, 0:520].rearrange("p (h t) -> p h t", h=8)[:, :, 0:1], 1.0)
            e.memset(giscr[:, 0:520].rearrange("p (h t) -> p h t", h=8)[:, :, 0:1], 1.0)
            s_g.inc(e.drain())
            s_mv.inc(e.dma_start(out=betaf.rearrange("q (h t) -> q h t", h=8),
                                 in_=betaw[:, c0:c0 + CH].rearrange("h (q t) -> h q t", q=1)), 16)
        on("gpsimd", f)

        gi_ap = giscr[:, 0:520].rearrange("p (h t) -> p h t", h=8)
        giprev_ap = gi_ap[:, :, 0:CH]

        def f(e):
            s_g.wait(e)
            s_s.wait(e)
            e.reciprocal(convt[0][:].rearrange("p (h t) -> p h t", h=8), aw_ap)
            e.drain()
            for h in range(8):
                e.tensor_tensor_scan(gscr[:, h * 65 + 1:h * 65 + 65],
                                     awin[:, h * 512 + c0:h * 512 + c0 + CH],
                                     zeros64[:, 0:CH], 1.0, OP.mult, OP.add)
                e.tensor_tensor_scan(giscr[:, h * 65 + 1:h * 65 + 65],
                                     convt[0][:, h * CH:(h + 1) * CH],
                                     zeros64[:, 0:CH], 1.0, OP.mult, OP.add)
            e.drain()
            e.tensor_tensor(Vg[:].rearrange("p (h t) -> p h t", h=8), vw_ap, giprev_ap, OP.mult)
            s_v.inc(e.drain())
        on("vector", f)

        def f(e):
            s_mv.wait(e)
            s_v.wait(e)
            s_t.inc(e.matmul(pb[2][0:128, 0:512], onesrow_f[:], betaf, start=True, stop=True))
        on("tensor", f)

        def f(e):
            s_t.wait(e)
            e.tensor_tensor(recipBA[:], pb[2][0:128, 0:512], convt[0][:], OP.mult)
            e.drain()
            for i in range(NTIER):
                e.tensor_copy(lntmp[:, 128 * i:128 * (i + 1)],
                              recipBA[:].rearrange("p (h t) -> p h t", h=8)[:, :, 16 * i:16 * i + 16])
            for hg in range(2):
                cp = e.tensor_copy(statsa[:, 1792 + hg * 128:1792 + (hg + 1) * 128],
                                   bass.AP(tensor=gscr, offset=hg * 260 + 64,
                                           ap=[[552, 128], [65, 4], [1, 32]]))
            s_v.inc(cp)
        on("vector", f)

        def f(e):
            s_v.wait(e)
            for i in range(NTIER):
                tr = e.transpose(pb[2][0:128, 128 * i:128 * i + 128], lntmp[:, 128 * i:128 * (i + 1)], ident_f[:])
            s_t.inc(tr)
        on("tensor", f)

        def f(e):
            s_t.wait(e)
            s_v.inc(e.tensor_copy(arecipS[:], pb[2][0:128, 0:512]))
        on("vector", f)

        def f(e):
            s_v.wait(e)
            for hg in range(2):
                tr = e.transpose(pb[2][0:128, hg * 128:hg * 128 + 128],
                                 statsa[:, 1792 + hg * 128:1792 + (hg + 1) * 128],
                                 ident_f[:])
            s_t.inc(tr)
        on("tensor", f)

        def f(e):
            s_t.wait(e)
            for h in range(8):
                cp = e.tensor_copy(gCrow0[0:1, h * 128:(h + 1) * 128],
                                   pb[2][32 * (h % 4):32 * (h % 4) + 1,
                                         (h // 4) * 128:(h // 4) * 128 + 128])
            s_v.inc(cp)
        on("vector", f)

        def f(e):
            s_s.wait(e)
            s_v.wait(e)
            for h in range(8):
                e.matmul(pb[0][0:64, h * 64:(h + 1) * 64], kx(h), kx(h), start=True, stop=True)
            for h in range(8):
                mm = e.matmul(pb[1][0:64, h * 64:(h + 1) * 64], kx(h), qx(h), start=True, stop=True)
            s_t.inc(mm)
        on("tensor", f)

        def f(e):
            s_t.wait(e)
            e.tensor_mul(AgrS[:], pb[0][0:64, 0:512], maskAn[:])
            s_v.inc(e.tensor_mul(BgrS[:], pb[1][0:64, 0:512], maskBn[:]))
        on("vector", f)

        OFFCOLS = {(0, 1): 512, (0, 2): 640, (1, 2): 768, (0, 3): 896, (1, 3): 1024, (2, 3): 1152}
        T1S = statsa[:, 768:1792].bitcast(BF16)
        t1v = {}
        for i in range(NTIER):
            def f(e, i=i):
                if i < 2:
                    s_v.wait(e)
                else:
                    s_v.wait(e, t1v[i - 2])
                s_t.inc(e.matmul(pb[4 + i % 2][0:128, 0:512], R16S[:, i * 128:(i + 1) * 128],
                                 AgrS[:], start=True, stop=True))
            on("tensor", f)

            def f(e, i=i):
                s_t.wait(e)
                s_v.inc(e.tensor_copy(T1S[:, i * 512:(i + 1) * 512], pb[4 + i % 2][0:128, 0:512]))
                t1v[i] = s_v.v
            on("vector", f)

        def t1sel(pp, i2):
            return T1S[:, pp * 512:(pp + 1) * 512] \
                .rearrange("p (h q) -> p h q", q=64)[:, :, 16 * i2:16 * i2 + 16]

        def f(e):
            s_v.wait(e)
            s_t.wait(e)
            for i in range(NTIER):
                e.tensor_mul(BDset[:, i * 128:(i + 1) * 128], t1sel(i, i), M8[:])
            for (pp, i2), col in OFFCOLS.items():
                mu = e.tensor_mul(BDset[:, col:col + 128], t1sel(pp, i2), M8[:])
            s_g.inc(mu)
        on("gpsimd", f)

        def f(e):
            s_s.wait(e)
            s_v.wait(e)
            for h in range(8):
                tr = e.transpose(pb[2][0:64, 0:512].bitcast(BF16)[:, h * 128:(h + 1) * 128],
                                 kx(h), ident_b[:])
            s_t.inc(tr)
        on("tensor", f)

        def f(e):
            s_t.wait(e)
            s_v.inc(e.tensor_scalar(ktokN[:], pb[2][0:64, 0:512].bitcast(BF16),
                                    -1.0, None, OP.mult))
        on("vector", f)
        if l1_cstop == "c1":
            return

        def f(e):
            s_v.wait(e)
            for h in range(8):
                mm = e.matmul(pb[6][0:128, h * 64:(h + 1) * 64], SST[:, h * 128:(h + 1) * 128],
                              kx(h), start=True, stop=True)
            s_t.inc(mm)
        on("tensor", f)

        def f(e):
            s_t.wait(e)
            s_v.inc(e.tensor_sub(rv[:], pb[6][0:128, 0:512], Vg[:]))
        on("vector", f)

        rTS = statsa[:, 512:768].bitcast(BF16)

        rvstg = statsa[:, 1792:2048].bitcast(BF16)

        def f(e):
            s_t.wait(e)
            s_v.wait(e)
            for i in range(NTIER):
                cp = e.tensor_copy(rvstg[:, 128 * i:128 * (i + 1)],
                                   rv.rearrange("p (h t) -> p h t", h=8)[:, :, 16 * i:16 * i + 16])
            s_v.inc(cp)
        on("vector", f)

        def f(e):
            s_v.wait(e)
            for i in range(NTIER):
                tr = e.transpose(pb[2][0:128, 64 * i:64 * i + 64].bitcast(BF16),
                                 rvstg[:, 128 * i:128 * (i + 1)], ident_b[:])
            s_t.inc(tr)
        on("tensor", f)

        def f(e):
            s_t.wait(e)
            s_v.inc(e.tensor_copy(rTS, pb[2][0:128, 0:256].bitcast(BF16)))
        on("vector", f)
        if l1_cstop == "c2":
            return

        for i in range(NTIER):
            def f(e, i=i):
                if i == 0:
                    s_g.wait(e)
                s_v.wait(e)
                mm1 = e.matmul(pb[3][0:128, 128 * i:128 * i + 128], ident_b[:],
                               rTS[:, 128 * i:128 * i + 128], start=True, stop=(i == 0))
                for pp in range(i):
                    mm1 = e.matmul(pb[3][0:128, 128 * i:128 * i + 128],
                                   BDset[:, OFFCOLS[(pp, i)]:OFFCOLS[(pp, i)] + 128],
                                   wS[:, pp * 128:(pp + 1) * 128], start=False, stop=(pp == i - 1))
                s_t.inc(mm1)
            on("tensor", f)

            def f(e, i=i):
                s_t.wait(e)
                s_v.inc(e.tensor_copy(rpS[:, 128 * i:128 * i + 128],
                                      pb[3][0:128, 128 * i:128 * i + 128]))
            on("vector", f)

            for m in range(TIER):
                def f(e, i=i, m=m):
                    s_v.wait(e)
                    bank = pb[4 + m % 2]
                    mm1 = e.matmul(bank[0:128, 0:128], ident_b[:], rpS[:, 128 * i:128 * i + 128],
                                   start=True, stop=(m == 0))
                    if m > 0:
                        mm1 = e.matmul(bank[0:128, 0:128], BDset[:, 128 * i:128 * i + 128],
                                       wping[(m + 1) % 2][:], start=False, stop=True)
                    s_t.inc(mm1)
                on("tensor", f)

                def f(e, i=i, m=m):
                    s_t.wait(e)
                    dst = wS[:, 128 * i:128 * i + 128] if m == TIER - 1 else wping[m % 2][:]
                    s_v.inc(e.tensor_mul(dst, pb[4 + m % 2][0:128, 0:128],
                                         arecipS[:, 128 * i:128 * i + 128]))
                on("vector", f)

        if l1_cstop == "c3":
            return
        def f(e):
            s_v.wait(e)
            s_t.wait(e)
            s_wc.inc(e.dma_start(out=wsd, in_=wS[:]), 16)
            e.wait_ge(s_wc.h, s_wc.v)
            for i in range(NTIER):
                srcp = bass.AP(tensor=wsd.tensor, offset=i * 128,
                               ap=[[512, 16], [16 * 512, 8], [1, 128]])
                s_wc.inc(e.dma_start(out=wCons[16 * i:16 * (i + 1), :], in_=srcp), 16)
            if l1_cstop == "c4a":
                e.wait_ge(s_wc.h, s_wc.v)
        on("gpsimd", f)
        if l1_cstop == "c4a":
            return

        def f(e):
            s_v.wait(e)
            for h in range(8):
                e.matmul(pb[0][0:128, h * 64:(h + 1) * 64], SST[:, h * 128:(h + 1) * 128],
                         qx(h), start=(h == 0), stop=False)
            for h in range(8):
                bank = pb[6] if h < 4 else pb[7]
                col = (h % 4) * 128
                e.matmul(bank[0:128, col:col + 128], ident_b[:], SST[:, h * 128:(h + 1) * 128],
                         start=(h % 4 == 0), stop=False)
            for h in range(8):
                bank = pb[1] if h < 4 else pb[2]
                col = (h % 4) * 128
                mm = e.matmul(bank[0:128, col:col + 128], ones128_f[0:1, :],
                              gCrow0[0:1, h * 128:(h + 1) * 128],
                              start=(h % 4 == 0), stop=(h % 4 == 3))
            s_t.inc(mm)
        on("tensor", f)

        def f(e):
            s_wc.wait(e)
            for h in range(8):
                mm = e.matmul(pb[0][0:128, h * 64:(h + 1) * 64], wCons[0:64, h * 128:(h + 1) * 128],
                              BgrS[0:64, h * 64:(h + 1) * 64], start=False, stop=(h == 7))
            for h in range(8):
                bank = pb[6] if h < 4 else pb[7]
                col = (h % 4) * 128
                mm = e.matmul(bank[0:128, col:col + 128], ktokN[0:64, h * 128:(h + 1) * 128],
                              wCons[0:64, h * 128:(h + 1) * 128], start=False, stop=(h % 4 == 3))
            s_t.inc(mm)
        on("tensor", f)

        def f(e, c=c):
            s_t.wait(e)
            s_v.inc(e.tensor_tensor(owin[:].rearrange("p (h t) -> p h t", h=8)[:, :, c0:c0 + CH],
                                    pb[0][0:128, 0:512].rearrange("p (h t) -> p h t", h=8),
                                    gcur_ap, OP.mult))
        on("vector", f)

        if l1_cstop == "c4b":
            return
        def f(e):
            s_t.wait(e)
            e.tensor_copy(statsa[:, 1024:1536], pb[1][0:128, 0:512])
            e.tensor_copy(statsa[:, 1536:2048], pb[2][0:128, 0:512])
            e.drain()
            e.tensor_mul(SST[:, 0:512], pb[6][0:128, 0:512], statsa[:, 1024:1536])
            s_v.inc(e.tensor_mul(SST[:, 512:1024], pb[7][0:128, 0:512], statsa[:, 1536:2048]))
        on("vector", f)


    def emit_win_ln(ysrc, dst, convt):
        sqb = [lntmp[:].bitcast(BF16)[:, 0:512], lntmp[:].bitcast(BF16)[:, 512:1024]]
        wlv = {}
        for i in range(8):
            def f(e, i=i):
                if i == 0:
                    s_v.wait(e)
                    s_t.wait(e)
                if i >= 2:
                    s_t.wait(e, wlv[i - 2])
                s_s.inc(e.activation(sqb[i % 2], ysrc[i], AT.Square))
            on("scalar", f)

            def f(e, i=i):
                s_s.wait(e)
                if i == 0:
                    s_v.wait(e)
                e.matmul(pb[5][0:1, 0:512], onescol_b[:], sqb[i % 2],
                         start=(i == 0), stop=(i == 7))
                s_t.inc(e.matmul(pb[4][0:1, 0:512], onescol_f[:], ysrc[i],
                                 start=(i == 0), stop=(i == 7)))
                wlv[i] = s_t.v
            on("tensor", f)

        def f(e):
            s_t.wait(e)
            e.tensor_scalar(statsa[0:1, 0:512], pb[4][0:1, 0:512], 1.0 / D, None, OP.mult)
            e.tensor_scalar(statsa[0:1, 512:1024], pb[5][0:1, 0:512], 1.0 / D, None, OP.mult)
            e.drain()
            e.tensor_mul(statsa[0:1, 1024:1536], statsa[0:1, 0:512], statsa[0:1, 0:512])
            e.drain()
            s_v.inc(e.tensor_sub(statsa[0:1, 1536:2048], statsa[0:1, 512:1024], statsa[0:1, 1024:1536]))
        on("vector", f)

        def f(e):
            s_v.wait(e)
            e.activation(statsa[0:1, 1024:1536], statsa[0:1, 1536:2048], AT.Ln, bias=LN_EPS)
            e.drain()
            s_s.inc(e.activation(statsa[0:1, 1536:2048], statsa[0:1, 1024:1536], AT.Exp, scale=-0.5))
        on("scalar", f)

        def f(e):
            s_s.wait(e)
            e.matmul(pb[6][0:128, 0:512], ones128_f[0:1, :], statsa[0:1, 0:512],
                     start=True, stop=True)
            s_t.inc(e.matmul(pb[7][0:128, 0:512], ones128_f[0:1, :], statsa[0:1, 1536:2048],
                             start=True, stop=True))
        on("tensor", f)

        def f(e):
            s_t.wait(e)
            s_g.wait(e)
            for i in range(8):
                e.tensor_sub(convt[i % 2][:], ysrc[i], pb[6][:, 0:512])
                e.drain()
                mu = e.tensor_mul(dst(i), convt[i % 2][:], pb[7][:, 0:512])
                if i < 7:
                    e.drain()
            s_v.inc(mu)
        on("vector", f)

    def l1_tail(wi, V):
        convt = V["convt"]; wcur = V["wcur"]; vpost = V["vpost"]; bndx = V["bndx"]
        wcur_free = V["wcur_free"]
        tw = slice(wi * WIN, (wi + 1) * WIN)
        x2w = lambda ci: x2[ci][:, tw]
        owin = xb[1][:].bitcast(BF16)
        ggwin = xb[0][:].bitcast(BF16)
        gow = xb[6][:].bitcast(BF16)
        x3v = xb[4][:].bitcast(BF16)
        xrv = xb[5][:].bitcast(BF16)
        xkv = xb[7][:].bitcast(BF16)
        kactv = vpost

        def y3w(i):
            return xb[2][:, i * 512:(i + 1) * 512] if i < 4 else xb[3][:, (i - 4) * 512:(i - 3) * 512]

        def rw(i):
            return xb[2][:, i * 512:(i + 1) * 512] if i < 4 else xb[3][:, (i - 4) * 512:(i - 3) * 512]

        def y4w(i):
            return xb[1][:, i * 512:(i + 1) * 512] if i < 4 else xb[0][:, (i - 4) * 512:(i - 3) * 512]

        def outw(i):
            return xb[6][:, i * 512:(i + 1) * 512] if i < 4 else xb[7][:, (i - 4) * 512:(i - 3) * 512]

        sqb = [lntmp[:].bitcast(BF16)[:, 0:512], lntmp[:].bitcast(BF16)[:, 512:1024]]
        hls = {}
        hlr = {}
        for h in range(8):
            def f(e, h=h):
                if h == 0:
                    s_v.wait(e)
                if h >= 2:
                    s_t.wait(e, hls[h - 2])
                s_s.inc(e.activation(sqb[h % 2], owin[:, h * 512:(h + 1) * 512], AT.Square))
            on("scalar", f)

            def f(e, h=h):
                s_s.wait(e)
                if h < 2:
                    s_v.wait(e)
                else:
                    s_v.wait(e, hlr[h - 2])
                e.matmul(pb[4 + h % 2][0:1, 0:512], onescol_b[:],
                         owin[:, h * 512:(h + 1) * 512], start=True, stop=True)
                s_t.inc(e.matmul(pb[2 + h % 2][0:1, 0:512], onescol_b[:], sqb[h % 2],
                                 start=True, stop=True))
                hls[h] = s_t.v
            on("tensor", f)

            def f(e, h=h):
                s_t.wait(e)
                e.tensor_scalar(statsa[0:1, 0:512], pb[4 + h % 2][0:1, 0:512],
                                1.0 / 128, None, OP.mult)
                e.tensor_scalar(statsa[0:1, 512:1024], pb[2 + h % 2][0:1, 0:512],
                                1.0 / 128, None, OP.mult)
                e.drain()
                e.tensor_mul(statsa[0:1, 1024:1536], statsa[0:1, 0:512],
                             statsa[0:1, 0:512])
                e.drain()
                s_v.inc(e.tensor_sub(statsa[0:1, 1536:2048], statsa[0:1, 512:1024],
                                     statsa[0:1, 1024:1536]))
                hlr[h] = s_v.v
            on("vector", f)

            def f(e):
                s_v.wait(e)
                e.activation(statsa[0:1, 1024:1536], statsa[0:1, 1536:2048],
                             AT.Ln, bias=HLN_EPS)
                e.drain()
                s_s.inc(e.activation(statsa[0:1, 1536:2048], statsa[0:1, 1024:1536],
                                     AT.Exp, scale=-0.5))
            on("scalar", f)

            bmean, brstd = (pb[6], pb[7]) if h % 2 == 0 else (pb[0], pb[1])

            def f(e, bmean=bmean, brstd=brstd):
                s_s.wait(e)
                s_v.wait(e)
                e.matmul(bmean[0:128, 0:512], ones128_f[0:1, :],
                         statsa[0:1, 0:512], start=True, stop=True)
                s_t.inc(e.matmul(brstd[0:128, 0:512], ones128_f[0:1, :],
                                 statsa[0:1, 1536:2048], start=True, stop=True))
            on("tensor", f)

            def f(e, h=h, bmean=bmean, brstd=brstd):
                s_t.wait(e)
                e.tensor_sub(convt[0][:], owin[:, h * 512:(h + 1) * 512],
                             bmean[:, 0:512])
                e.drain()
                e.tensor_mul(convt[1][:], convt[0][:], brstd[:, 0:512])
                e.drain()
                e.tensor_scalar(convt[0][:], convt[1][:], lng_c[:, 0:1],
                                lnb_c[:, 0:1], OP.mult, OP.add)
                e.drain()
                s_v.inc(e.tensor_mul(gow[:, h * 512:(h + 1) * 512], convt[0][:],
                                     ggwin[:, h * 512:(h + 1) * 512]))
            on("vector", f)

        def stream_mm(mat, rhs_fn, first_extra=None, rhs_vals=None):
            tvals = {}
            entry = s_t.v
            for pair in range(4):
                def f(e, pair=pair, mat=mat):
                    s_t.wait(e, tvals[pair - 2] if pair >= 2 else entry)
                    (s_mv if pair % 2 == 0 else s_mw).inc(e.dma_start(
                        out=wcur[pair % 2][:].rearrange("p (a m) -> p a m", m=1024),
                        in_=wbf[mat][pair * 256:(pair + 1) * 256, :].rearrange(
                            "(a p) m -> p a m", p=128)), 16)
                on("gpsimd", f)

                def f(e, pair=pair):
                    (s_mv if pair % 2 == 0 else s_mw).wait(e)
                    if rhs_vals is not None:
                        s_v.wait(e, rhs_vals[min(2 * pair + 1, 7)])
                    elif pair == 0:
                        s_v.wait(e)
                    if pair == 0 and first_extra is not None:
                        first_extra(e)
                    for sl in range(2):
                        cj = pair * 2 + sl
                        for ci in range(8):
                            mm = e.matmul(pb[ci][0:128, 0:512],
                                          wcur[pair % 2][:, sl * 1024 + ci * 128:sl * 1024 + (ci + 1) * 128],
                                          rhs_fn(cj), start=(cj == 0), stop=(cj == 7))
                    s_t.inc(mm)
                    tvals[pair] = s_t.v
                    wcur_free["v"] = s_t.v
                on("tensor", f)

        stream_mm("Wo", lambda h: gow[:, h * 512:(h + 1) * 512])

        def f(e):
            s_t.wait(e)
            for i in range(8):
                ad = e.tensor_add(y3w(i), pb[i][0:128, 0:512], x2w(i))
            s_v.inc(ad)
        on("vector", f)

        emit_win_ln([y3w(i) for i in range(8)],
                    lambda i: x3v[:, i * 512:(i + 1) * 512], convt)
        if wi == 0 and "x3" in dbg:
            dump("x3", [x3v[:, 0:4096]])

        e0v = {}

        def f(e):
            s_v.wait(e)
            s_g.wait(e)
            for ci in range(8):
                e.tensor_copy(convt[0][:, 1:512], x3v[:, ci * 512:ci * 512 + 511])
                e.tensor_copy(convt[0][:, 0:1], bndx[:, ci:ci + 1])
                e.drain()
                e.tensor_sub(convt[1][:], x3v[:, ci * 512:(ci + 1) * 512], convt[0][:])
                e.drain()
                e.scalar_tensor_tensor(xrv[:, ci * 512:(ci + 1) * 512], convt[1][:],
                                       mur_c[:, ci:ci + 1], convt[0][:], OP.mult, OP.add)
                s_v.inc(e.scalar_tensor_tensor(xkv[:, ci * 512:(ci + 1) * 512], convt[1][:],
                                               muk_c[:, ci:ci + 1], convt[0][:], OP.mult, OP.add))
                e0v[ci] = s_v.v
                e.drain()
        on("vector", f)

        def f(e):
            s_v.wait(e)
            e.tensor_copy(bndx[:, 0:8].rearrange("p (c o) -> p c o", o=1),
                          x3v.rearrange("p (c t) -> p c t", t=512)[:, :, 511:512])
            s_g.inc(e.drain())
        on("gpsimd", f)

        stream_mm("W_rT", lambda cj: xrv[:, cj * 512:(cj + 1) * 512], rhs_vals=e0v)

        def f(e):
            s_t.wait(e)
            for i in range(8):
                ac = e.activation(rw(i), pb[i][0:128, 0:512], AT.Sigmoid)
            s_s.inc(ac)
        on("scalar", f)

        def kwait(e):
            s_s.wait(e)
        stream_mm("W_kT", lambda cj: xkv[:, cj * 512:(cj + 1) * 512], first_extra=kwait)
        e2v = {}
        for i in range(8):
            def f(e, i=i):
                if i == 0:
                    s_t.wait(e)
                if i >= 2:
                    s_v.wait(e, e2v[i - 2])
                s_s.inc(e.activation(convt[i % 2][:], pb[i][0:128, 0:512], AT.Relu))
            on("scalar", f)

            def f(e, i=i):
                s_s.wait(e)
                s_v.inc(e.tensor_mul(kactv[:, i * 512:(i + 1) * 512], convt[i % 2][:],
                                     convt[i % 2][:]))
                e2v[i] = s_v.v
            on("vector", f)

        stream_mm("W_vT", lambda cj: kactv[:, cj * 512:(cj + 1) * 512])

        def f(e):
            s_t.wait(e)
            for i in range(8):
                e.tensor_mul(convt[i % 2][:], pb[i][0:128, 0:512], rw(i))
                e.drain()
                ad = e.tensor_add(y4w(i), convt[i % 2][:], x3v[:, i * 512:(i + 1) * 512])
                if i < 7:
                    e.drain()
            s_v.inc(ad)
        on("vector", f)

        emit_win_ln([y4w(i) for i in range(8)], outw, convt)

        def f(e):
            s_v.wait(e)
            for ci in range(8):
                s_st.inc(e.dma_start(out=out_t[ci * 128:(ci + 1) * 128, tw],
                                     in_=outw(ci)), 16)
        on("sync", f)

    phase0()
    ph0s.close()
    if phases >= 1:
        l0_attn_v2()
    if phases >= 2:
        for i in range(8):
            x2.append(xa[i][:, 0:1024].bitcast(BF16))
        l0_ffn()
    if phases >= 3:
        l1s = ExitStack()
        l1_attn(l1s)

    def store_out():
        def f(e):
            s_v.wait(e)
            s_s.wait(e)
            if phases < 3:
                for i in range(8):
                    s_st.inc(e.dma_start(out=out_t[i * 128:(i + 1) * 128, :], in_=xa[i][:]), 16)
            s_st.wait(e)
        on("sync", f)

    store_out()
    es.close()
    return nc, dbg_t


def _prep_weights(inputs):
    f32 = np.float32
    g = lambda n: np.ascontiguousarray(np.asarray(inputs[n], dtype=f32))
    wm = {}
    wm["W_DQ"] = g("L0_attn_W_DQ")
    wm["W_UQT"] = np.ascontiguousarray(g("L0_attn_W_UQ").T)
    wm["W_DKV"] = g("L0_attn_W_DKV")
    wm["W_UKVT"] = np.ascontiguousarray(g("L0_attn_W_UKV").T)
    wm["W_QRT"] = np.ascontiguousarray(g("L0_attn_W_QR").T)
    wm["W_KRT"] = np.ascontiguousarray(g("L0_attn_W_KR").T)
    wm["W_OT"] = np.ascontiguousarray(g("L0_attn_W_O").T)
    wm["q_norm_w"] = g("L0_attn_q_norm_w").reshape(64, 1)
    wm["k_norm_w"] = g("L0_attn_k_norm_w").reshape(64, 1)
    wm["f0_kernel"] = np.ascontiguousarray(np.asarray(inputs["L0_ffn_kernel"]))
    wm["f0_in_scale"] = g("L0_ffn_input_scale").reshape(1, 1)
    wm["f0_k_scale"] = g("L0_ffn_kernel_scale").reshape(1, 1)
    wm["f0_bias"] = np.ascontiguousarray(np.asarray(inputs["L0_ffn_bias"]).reshape(8, 128).T)
    for n in ("Wq", "Wk", "Wv", "Wa", "Wg", "Wo"):
        wm[n] = g("L1_attn_" + n)
    wm["Wb"] = np.ascontiguousarray(g("L1_attn_Wb").reshape(8, 128, 8).transpose(1, 0, 2).reshape(128, 64))
    wm["ba"] = np.ascontiguousarray(g("L1_attn_ba").reshape(8, 128).T)
    wm["bb"] = g("L1_attn_bb").reshape(H1, 1)
    for n in ("cqw", "ckw", "cvw"):
        wm[n] = np.ascontiguousarray(g("L1_attn_" + n).reshape(8, 128, 4).transpose(1, 0, 2).reshape(128, 32))
    for n in ("cqb", "ckb", "cvb"):
        wm[n] = np.ascontiguousarray(g("L1_attn_" + n).reshape(8, 128).T)
    wm["ln_g"] = g("L1_attn_ln_g").reshape(128, 1)
    wm["ln_b"] = g("L1_attn_ln_b").reshape(128, 1)
    wm["W_rT"] = np.ascontiguousarray(g("L1_ffn_W_r").T)
    wm["W_kT"] = np.ascontiguousarray(g("L1_ffn_W_k").T)
    wm["W_vT"] = np.ascontiguousarray(g("L1_ffn_W_v").T)
    wm["mu_r"] = np.ascontiguousarray(g("L1_ffn_mu_r").reshape(8, 128).T)
    wm["mu_k"] = np.ascontiguousarray(g("L1_ffn_mu_k").reshape(8, 128).T)
    return wm


def kernel(**inputs):
    x = np.asarray(inputs["L0_attn_x"], dtype=np.float32)
    B = x.shape[0]
    wm = _prep_weights(inputs)
    nc, _ = build()
    in_maps = []
    for b in range(B):
        m = {"x": np.ascontiguousarray(x[b].T)}
        m.update(wm)
        in_maps.append(m)
    res = run_bass_kernel_spmd(nc, in_maps, core_ids=list(range(B)))
    out = np.stack([np.asarray(r["out"]).T for r in res.results], axis=0)
    return out.astype(np.float32)


# version 26
# speedup vs baseline: 1.0059x; 1.0043x over previous
import numpy as np
import ml_dtypes
from contextlib import ExitStack

import concourse.bass as bass
import concourse.mybir as mybir
from concourse.bass_utils import run_bass_kernel_spmd

F32 = mybir.dt.float32
BF16 = mybir.dt.bfloat16
F8 = mybir.dt.float8e4
AT = mybir.ActivationFunctionType
OP = mybir.AluOpType
AX = mybir.AxisListType

T = 2048
D = 1024
H1 = 8
CH = 64
NCH = T // CH
TIER = 16
NTIER = CH // TIER
WIN = 512
NWIN = T // WIN
CPW = WIN // CH
LN_EPS = 1e-5
RMS_EPS = 1.1920929e-07
HLN_EPS = 1e-5

PHASES = 3
DEBUG = ()


class Sem:
    def __init__(self, es, nc, name):
        self.h = es.enter_context(nc.semaphore(name))
        self.v = 0

    def inc(self, instr, k=1):
        instr.then_inc(self.h, k)
        self.v += k
        return self.v

    def wait(self, eng, v=None):
        eng.wait_ge(self.h, self.v if v is None else v)


def build(phases=PHASES, debug=DEBUG, l1_nwin=None, l1_stop="full", l1_nchunk=None, l1_cstop="c4"):
    dbg = set(debug)
    if "l1a" in dbg:
        dbg |= {"qpost", "kpost", "vpost", "awin", "ggwin", "betaw"}
    if "l1b" in dbg:
        dbg |= {"owin"}
    nc = bass.Bass("TRN2", target_bir_lowering=False)
    es = ExitStack()

    def din(name, shape, dt=F32):
        return nc.dram_tensor(name, list(shape), dt, kind="ExternalInput").ap()

    x_in = din("x", (D, T))
    w = {}
    w["W_DQ"] = din("W_DQ", (512, 64))
    w["W_UQT"] = din("W_UQT", (512, 64))
    w["W_DKV"] = din("W_DKV", (1024, 64))
    w["W_UKVT"] = din("W_UKVT", (1024, 128))
    w["W_QRT"] = din("W_QRT", (64, 64))
    w["W_KRT"] = din("W_KRT", (64, 64))
    w["W_OT"] = din("W_OT", (64, 64))
    w["q_norm_w"] = din("q_norm_w", (64, 1))
    w["k_norm_w"] = din("k_norm_w", (64, 1))
    w["f0_kernel"] = nc.dram_tensor("f0_kernel", [D, D], BF16, kind="ExternalInput").ap()
    w["f0_in_scale"] = din("f0_in_scale", (1, 1))
    w["f0_k_scale"] = din("f0_k_scale", (1, 1))
    w["f0_bias"] = nc.dram_tensor("f0_bias", [128, 8], BF16, kind="ExternalInput").ap()
    for n in ("Wq", "Wk", "Wv", "Wa", "Wg", "Wo"):
        w[n] = din(n, (D, D))
    w["Wb"] = din("Wb", (128, 64))
    w["ba"] = din("ba", (128, 8))
    w["bb"] = din("bb", (H1, 1))
    for n in ("cqw", "ckw", "cvw"):
        w[n] = din(n, (128, 32))
    for n in ("cqb", "ckb", "cvb"):
        w[n] = din(n, (128, 8))
    w["ln_g"] = din("ln_g", (128, 1))
    w["ln_b"] = din("ln_b", (128, 1))
    for n in ("W_rT", "W_kT", "W_vT"):
        w[n] = din(n, (D, D))
    w["mu_r"] = din("mu_r", (128, 8))
    w["mu_k"] = din("mu_k", (128, 8))

    out_t = nc.dram_tensor("out", [D, T], F32, kind="ExternalOutput").ap()
    wbf = {n: nc.dram_tensor("bf_" + n, [D, D], BF16, kind="Internal").ap()
           for n in ("Wq", "Wk", "Wv", "Wa", "Wg", "Wo", "W_rT", "W_kT", "W_vT")}
    wsd = nc.dram_tensor("wsd", [128, 512], BF16, kind="Internal").ap()
    wsd2 = nc.dram_tensor("wsd2", [64, 1024], BF16, kind="Internal").ap()
    mskd = nc.dram_tensor("mskd", [128, 640], BF16, kind="Internal").ap()

    dbg_t = {}

    def sb(name, p, f, dt=F32, stack=None):
        side = "left" if stack is None else "right"
        return (stack or es).enter_context(nc.sbuf_tensor(name, [p, f], dt, side=side))

    ident_b = sb("ident_b", 128, 128, BF16)
    ident_f = sb("ident_f", 128, 128)
    tri0 = sb("tri0", 128, 128, BF16)
    maskAn = sb("maskAn", 64, 512, BF16)
    maskBn = sb("maskBn", 64, 512, BF16)
    ones2blk = sb("ones2blk", 128, 2, BF16)
    ind2_f = sb("ind2_f", 128, 128)
    onesrow_f = sb("onesrow_f", 1, 128)
    ones128_f = sb("ones128_f", 128, 128)
    onescol_b = sb("onescol_b", 128, 1, BF16)
    onescol_f = sb("onescol_f", 128, 1)
    zeros64 = sb("zeros64", 128, 64)
    epsc = sb("epsc", 128, 4)
    cscr = None
    statsa = sb("statsa", 128, 2048)
    lntmp = sb("lntmp", 128, 512)

    ph0s = ExitStack()
    cscr = sb("cscr", 128, 512, stack=ph0s)
    wdq = sb("wdq", 128, 256, stack=ph0s)
    wuqt = sb("wuqt", 128, 256, stack=ph0s)
    wdkv = sb("wdkv", 128, 512, stack=ph0s)
    wukvt = sb("wukvt", 128, 1024, stack=ph0s)
    mqt_blk = sb("mqt_blk", 128, 128)
    mkt_blk = sb("mkt_blk", 128, 128)
    mvt_blk = sb("mvt_blk", 128, 128)
    wqrt_t = sb("wqrt", 64, 64, BF16)
    wkrt_blk = sb("wkrt_blk", 128, 128)
    wot_t = sb("wot", 64, 64)
    qnw = sb("qnw", 128, 1)
    knw = sb("knw", 128, 1)
    f0bias = sb("f0bias", 128, 8)
    srow = sb("srow", 1, 8)
    sinv_c = sb("sinv_c", 128, 1)
    s2_c = sb("s2_c", 128, 1)
    skinv_c = sb("skinv_c", 128, 1)
    ba_c = sb("ba_c", 128, 8)
    bb_c = sb("bb_c", H1, 1)
    cw = {n: sb("cw_" + n, 128, 32) for n in ("cqw", "ckw", "cvw")}
    cb = {n: sb("cb_" + n, 128, 8) for n in ("cqb", "ckb", "cvb")}
    lng_c = sb("lng_c", 128, 1)
    lnb_c = sb("lnb_c", 128, 1)
    mur_c = sb("mur_c", 128, 8)
    muk_c = sb("muk_c", 128, 8)

    xa = [sb(f"xa{i}", 128, T) for i in range(8)]
    xb = [sb(f"xb{i}", 128, T) for i in range(8)]
    x2 = []

    pb = [es.enter_context(nc.psum_tensor(f"pb{i}", [128, 512], F32)) for i in range(8)]

    s_ld = Sem(es, nc, "s_ld")
    s_ldg = Sem(es, nc, "s_ldg")
    s_mv = Sem(es, nc, "s_mv")
    s_wc = Sem(es, nc, "s_wc")
    s_mw = Sem(es, nc, "s_mw")
    s_g = Sem(es, nc, "s_g")
    s_t = Sem(es, nc, "s_t")
    s_v = Sem(es, nc, "s_v")
    s_s = Sem(es, nc, "s_s")
    s_st = Sem(es, nc, "s_st")
    s_lda = Sem(es, nc, "s_lda")
    s_dmp = Sem(es, nc, "s_dmp")
    s_ldb = Sem(es, nc, "s_ldb")
    s_bar = Sem(es, nc, "s_bar")

    nc.const_aps.register(F32, RMS_EPS, epsc[:, 0:1])
    nc.const_aps.register(F32, LN_EPS, epsc[:, 1:2])
    nc.const_aps.register(F32, HLN_EPS, epsc[:, 2:3])

    block = es.enter_context(nc.Block())

    def on(name, fn):
        getattr(block, name)(fn)

    ALL_E = ("tensor", "vector", "scalar", "gpsimd")

    def barrier(engs=ALL_E):
        n = len(engs)
        base = s_bar.v
        for en in engs:
            on(en, lambda e: e.drain().then_inc(s_bar.h, 1))
        s_bar.v += n
        for en in engs:
            on(en, lambda e: e.wait_ge(s_bar.h, base + n))

    def dump(name, tiles):
        if name not in dbg:
            return
        n = len(tiles)
        p_ = tiles[0].shape[0]
        f_ = tiles[0].shape[1]
        dt = tiles[0].dtype
        hdl = nc.dram_tensor("dbg_" + name, [n * p_, f_], dt, kind="ExternalOutput").ap()
        dbg_t[name] = "dbg_" + name

        def f(e):
            s_t.wait(e)
            s_v.wait(e)
            s_s.wait(e)
            s_g.wait(e)
            for i, t_ in enumerate(tiles):
                s_st.inc(e.dma_start(out=hdl[i * p_:(i + 1) * p_, :], in_=t_[:] if hasattr(t_, 'shape') and not isinstance(t_, bass.AP) else t_), 16)
            s_st.wait(e)
            e.nop().then_inc(s_dmp.h, 1)
        on("sync", f)
        s_dmp.v += 1
        for en in ALL_E:
            on(en, lambda e: e.wait_ge(s_dmp.h, s_dmp.v))

    def phase0():
        def f(e):
            for i in range(8):
                s_ld.inc(e.dma_start(out=xa[i][:], in_=x_in[i * 128:(i + 1) * 128, :]), 16)
            s_ld.inc(e.dma_start(out=wdq[:].rearrange("p (a m) -> p a m", m=w["W_DQ"].shape[1]), in_=w["W_DQ"].rearrange("(a p) m -> p a m", p=128)), 16)
            s_ld.inc(e.dma_start(out=wuqt[:].rearrange("p (a m) -> p a m", m=w["W_UQT"].shape[1]), in_=w["W_UQT"].rearrange("(a p) m -> p a m", p=128)), 16)
            s_ld.inc(e.dma_start(out=wdkv[:].rearrange("p (a m) -> p a m", m=w["W_DKV"].shape[1]), in_=w["W_DKV"].rearrange("(a p) m -> p a m", p=128)), 16)
            s_ld.inc(e.dma_start(out=wukvt[:].rearrange("p (a m) -> p a m", m=w["W_UKVT"].shape[1]), in_=w["W_UKVT"].rearrange("(a p) m -> p a m", p=128)), 16)
            s_ld.inc(e.dma_start(out=wot_t[:], in_=w["W_OT"]), 16)
            s_ld.inc(e.dma_start(out=qnw[0:64, :], in_=w["q_norm_w"]), 16)
            s_ld.inc(e.dma_start(out=qnw[64:128, :], in_=w["q_norm_w"]), 16)
            s_ld.inc(e.dma_start(out=knw[0:64, :], in_=w["k_norm_w"]), 16)
            s_ld.inc(e.dma_start(out=knw[64:128, :], in_=w["k_norm_w"]), 16)
            s_ld.inc(e.dma_start(out=ba_c[:], in_=w["ba"]), 16)
            s_ld.inc(e.dma_start(out=bb_c[:], in_=w["bb"]), 16)
            s_ld.inc(e.dma_start(out=srow[0:1, 0:1], in_=w["f0_in_scale"]), 16)
            s_ld.inc(e.dma_start(out=srow[0:1, 1:2], in_=w["f0_k_scale"]), 16)
            for n in ("cqw", "ckw", "cvw"):
                s_ld.inc(e.dma_start(out=cw[n][:], in_=w[n]), 16)
            for n in ("cqb", "ckb", "cvb"):
                s_ld.inc(e.dma_start(out=cb[n][:], in_=w[n]), 16)
            s_ld.inc(e.dma_start(out=lng_c[:], in_=w["ln_g"]), 16)
            s_ld.inc(e.dma_start(out=lnb_c[:], in_=w["ln_b"]), 16)
            s_ld.inc(e.dma_start(out=mur_c[:], in_=w["mu_r"]), 16)
            s_ld.inc(e.dma_start(out=muk_c[:], in_=w["mu_k"]), 16)
        on("sync", f)

        def f(e):
            s_ldg.inc(e.dma_start(out=wqrt_t[:], in_=w["W_QRT"]), 16)
            s_ldg.inc(e.dma_start(out=wkrt_blk[0:64, 0:64], in_=w["W_KRT"]), 16)
            s_ldg.inc(e.dma_start(out=wkrt_blk[64:128, 64:128], in_=w["W_KRT"]), 16)
            s_ldg.inc(e.dma_start(out=f0bias[:], in_=w["f0_bias"]), 16)
            if phases >= 3:
                for n in ("Wq", "Wk", "Wv", "Wa", "Wg", "Wo", "W_rT", "W_kT", "W_vT"):
                    s_ldg.inc(e.dma_start(out=wbf[n], in_=w[n]), 16)
            for t_ in (mqt_blk, mkt_blk, mvt_blk, ones2blk, ind2_f, zeros64, ident_f):
                e.memset(t_[:], 0.0)
            e.memset(wkrt_blk[0:64, 64:128], 0.0)
            e.memset(wkrt_blk[64:128, 0:64], 0.0)
            e.drain()
            e.memset(ones2blk[0:64, 0:1], 1.0)
            e.memset(ones2blk[64:128, 1:2], 1.0)
            e.memset(ind2_f[0:2, 64:128], 1.0)
            e.drain()
            e.memset(ind2_f[0:1, 64:128], 0.0)
            e.memset(ind2_f[0:1, 0:64], 1.0)
            e.drain()
            for b0 in (32, 64, 96):
                e.tensor_copy(ind2_f[b0:b0 + 2, :], ind2_f[0:2, :])
            e.memset(onesrow_f[:], 1.0)
            e.memset(ones128_f[:], 1.0)
            e.memset(onescol_b[:], 1.0)
            e.memset(onescol_f[:], 1.0)
            e.memset(epsc[:, 0:1], RMS_EPS)
            e.memset(epsc[:, 1:2], LN_EPS)
            e.memset(epsc[:, 2:3], HLN_EPS)
            e.memset(cscr[:], 1.0)
            e.drain()
            e.affine_select(ident_f[:], cscr[:, 0:128], [[-1, 128]], OP.is_equal, 0.0,
                            base=0, channel_multiplier=1)
            e.affine_select(cscr[:, 128:256], cscr[:, 0:128], [[1, 128]], OP.is_ge, 0.0,
                            base=0, channel_multiplier=-1)
            e.drain()
            e.tensor_copy(ident_b[:], ident_f[:])
            e.tensor_copy(tri0[:], cscr[:, 128:256])
            e.drain()
            e.memset(cscr[:], -1.0)
            e.drain()
            e.affine_select(cscr[0:64, 0:512], cscr[64:128, 0:512], [[0, 8], [1, 64]],
                            OP.is_gt, 0.0, base=0, channel_multiplier=-1)
            e.drain()
            e.tensor_copy(maskAn[:], cscr[0:64, 0:512])
            e.drain()
            e.memset(cscr[64:128, :], -1.0)
            e.drain()
            e.affine_select(cscr[0:64, 0:512], cscr[64:128, 0:512], [[0, 8], [1, 64]],
                            OP.is_ge, 0.0, base=0, channel_multiplier=-1)
            e.drain()
            e.tensor_copy(maskBn[:], cscr[0:64, 0:512])
            s_g.inc(e.drain())
        on("gpsimd", f)

        def f(e):
            s_ld.wait(e)
            s_ldg.wait(e)
            s_g.wait(e)
            for a in range(4):
                e.matmul(pb[0][0:64, 0:64], wdq[:, a * 64:(a + 1) * 64], wuqt[:, a * 64:(a + 1) * 64],
                         start=(a == 0), stop=(a == 3))
            for a in range(8):
                e.matmul(pb[1][0:64, 0:128], wdkv[:, a * 64:(a + 1) * 64], wukvt[:, a * 128:(a + 1) * 128],
                         start=(a == 0), stop=(a == 7))
            s_t.inc(e.drain())
        on("tensor", f)

        def f(e):
            s_t.wait(e)
            e.copy(mqt_blk[0:64, 0:64], pb[0][0:64, 0:64])
            e.copy(mqt_blk[64:128, 64:128], pb[0][0:64, 0:64])
            e.copy(mkt_blk[0:64, 0:64], pb[1][0:64, 0:64])
            e.copy(mkt_blk[64:128, 64:128], pb[1][0:64, 0:64])
            e.copy(mvt_blk[0:64, 0:64], pb[1][0:64, 64:128])
            e.copy(mvt_blk[64:128, 64:128], pb[1][0:64, 64:128])
            s_s.inc(e.drain())
        on("scalar", f)
        barrier()

    def emit_module_ln(ysrc, dst, sqscr):
        mlv = {}
        for i in range(8):
            def f(e, i=i):
                if i >= 2:
                    s_t.wait(e, mlv[i - 2])
                s_s.inc(e.activation(sqscr[i % 2][:], ysrc[i][:], AT.Square))
            on("scalar", f)

            def f(e, i=i):
                s_s.wait(e)
                for tt in range(4):
                    mm = e.matmul(pb[tt][0:1, 0:512], onescol_b[:],
                                  sqscr[i % 2][:, tt * 512:(tt + 1) * 512],
                                  start=(i == 0), stop=(i == 7))
                s_t.inc(mm)
                mlv[i] = s_t.v
            on("tensor", f)

        def f(e):
            for i in range(8):
                for tt in range(4):
                    mm = e.matmul(pb[4 + tt][0:1, 0:512], onescol_f[:],
                                  ysrc[i][:, tt * 512:(tt + 1) * 512],
                                  start=(i == 0), stop=(i == 7))
            s_t.inc(mm)
        on("tensor", f)

        for tt in range(4):
            def f(e, tt=tt):
                s_t.wait(e)
                e.tensor_scalar(statsa[0:1, 0:512], pb[4 + tt][0:1, 0:512], 1.0 / D, None, OP.mult)
                e.tensor_scalar(statsa[0:1, 512:1024], pb[tt][0:1, 0:512], 1.0 / D, None, OP.mult)
                e.drain()
                e.tensor_mul(statsa[0:1, 1024:1536], statsa[0:1, 0:512], statsa[0:1, 0:512])
                e.drain()
                s_v.inc(e.tensor_sub(statsa[0:1, 1536:2048], statsa[0:1, 512:1024], statsa[0:1, 1024:1536]))
            on("vector", f)

            def f(e):
                s_v.wait(e)
                e.activation(statsa[0:1, 1024:1536], statsa[0:1, 1536:2048], AT.Ln, bias=LN_EPS)
                e.drain()
                s_s.inc(e.activation(statsa[0:1, 1536:2048], statsa[0:1, 1024:1536], AT.Exp, scale=-0.5))
            on("scalar", f)

            def f(e, tt=tt):
                s_s.wait(e)
                s_v.wait(e)
                e.matmul(pb[tt][0:128, 0:512], ones128_f[0:1, :],
                         statsa[0:1, 0:512], start=True, stop=True)
                s_t.inc(e.matmul(pb[4 + tt][0:128, 0:512], ones128_f[0:1, :],
                                 statsa[0:1, 1536:2048], start=True, stop=True))
            on("tensor", f)

            def f(e, tt=tt):
                s_t.wait(e)
                sl = slice(tt * 512, (tt + 1) * 512)
                for i in range(8):
                    e.tensor_sub(lntmp[:], ysrc[i][:, sl], pb[tt][:, 0:512])
                    e.drain()
                    mu = e.tensor_mul(dst[i][:, sl] if not isinstance(dst[i], bass.AP) else dst[i][:, sl], lntmp[:], pb[4 + tt][:, 0:512])
                    if i < 7:
                        e.drain()
                s_v.inc(mu)
            on("vector", f)
        barrier()

    def l0_attn_v2():
        ph = ExitStack()
        sqS = sb("sqS", 128, T, BF16, ph)
        qt = [sb(f"qt{i}", 128, T, BF16, ph) for i in range(4)]
        kt_ = [sb(f"kt{i}", 128, T, BF16, ph) for i in range(4)]
        vt = [sb(f"vt{i}", 128, 16 * 130, BF16, ph) for i in range(2)]
        rstdbQ = sqS
        rstdbK = sqS
        expsT = [sb(f"expsT{i}", 128, 512, BF16, ph) for i in range(2)]
        expsTb = [sb(f"expsTb{i}", 128, 512, BF16, ph) for i in range(2)]
        expTmp = sb("expTmp", 128, 128, BF16, ph)
        expTmp2 = sb("expTmp2", 128, 128, BF16, ph)
        ctxE = sb("ctxE", 64, 512, F32, ph)
        attnp = [sb(f"attnp{i}", 128, T, BF16, ph) for i in range(2)]
        sqf = [sqS, attnp[0]]

        def qcv(p, c0, c1):
            return xb[p][:, c0 // 2:c1 // 2].bitcast(BF16)

        def kcv(p, c0, c1):
            return xb[p][:, 1024 + c0 // 2:1024 + c1 // 2].bitcast(BF16)

        for g in range(4):
            pairs = (2 * g, 2 * g + 1)

            for pi, p in enumerate(pairs):
                for kind, blkw, dstv in (("qc", mqt_blk, qcv), ("kc", mkt_blk, kcv)):
                    def f(e, blkw=blkw, p=p):
                        s_s.wait(e)
                        for tt in range(4):
                            mm = e.matmul(pb[tt][0:128, 0:512], blkw[:],
                                          xa[p][:, tt * 512:(tt + 1) * 512],
                                          start=True, stop=True)
                        s_t.inc(mm)
                    on("tensor", f)

                    def f(e, dstv=dstv, p=p):
                        s_t.wait(e)
                        for tt in range(4):
                            cp = e.copy(dstv(p, tt * 512, (tt + 1) * 512), pb[tt][0:128, 0:512])
                        s_s.inc(cp)
                    on("scalar", f)

            for pi, p in enumerate(pairs):
                for grp in range(4):
                    def f(e, p=p, grp=grp):
                        s_s.wait(e)
                        for k in range(4):
                            sl = grp * 4 + k
                            mm = e.matmul(pb[4][0:128, k * 128:(k + 1) * 128],
                                          xa[p][:, sl * 128:(sl + 1) * 128], mvt_blk[:],
                                          start=True, stop=True)
                        s_t.inc(mm)
                    on("tensor", f)

                    def f(e, pi=pi, grp=grp):
                        s_t.wait(e)
                        src = pb[4][0:128, 0:512].rearrange("p (s h v) -> p s h v", s=4, h=2)
                        dstv = vt[pi][:, grp * 4 * 130:(grp * 4 + 4) * 130] \
                            .rearrange("p (s c) -> p s c", s=4)[:, :, 0:130] \
                            .rearrange("p s (h c) -> p s h c", h=2)[:, :, :, 0:64] if False else None
                        e.copy(vt[pi][:].rearrange("p (s c) -> p s c", s=16)
                               [:, grp * 4:grp * 4 + 4, 0:64],
                               pb[4][0:128, 0:512].rearrange("p (s h v) -> p s h v", s=4, h=2)[:, :, 0, :])
                        s_s.inc(e.copy(vt[pi][:].rearrange("p (s c) -> p s c", s=16)
                               [:, grp * 4:grp * 4 + 4, 65:129],
                               pb[4][0:128, 0:512].rearrange("p (s h v) -> p s h v", s=4, h=2)[:, :, 1, :]))
                    on("scalar", f)

            def f(e, pairs=pairs):
                for pi in range(2):
                    e.memset(vt[pi][:].rearrange("p (s c) -> p s c", s=16)[:, :, 64:65], 1.0)
                    e.memset(vt[pi][:].rearrange("p (s c) -> p s c", s=16)[:, :, 129:130], 1.0)
                s_g.inc(e.drain())
            on("gpsimd", f)

            for pi in range(2):
                for qk in (0, 1):
                    vsel = qcv if qk == 0 else kcv
                    dstb = rstdbQ if qk == 0 else rstdbK

                    def f(e, vsel=vsel, p=pairs[pi]):
                        s_s.wait(e)
                        s_t.wait(e)
                        s_v.wait(e)
                        e.activation(sqS[:], vsel(p, 0, 2048), AT.Square)
                        s_s.inc(e.drain())
                    on("scalar", f)

                    def f(e):
                        s_s.wait(e)
                        for tt in range(4):
                            mm = e.matmul(pb[4 + tt][0:2, 0:512], ones2blk[:],
                                          sqS[:, tt * 512:(tt + 1) * 512], start=True, stop=True)
                        s_t.inc(mm)
                    on("tensor", f)

                    def f(e):
                        s_t.wait(e)
                        for tt in range(4):
                            e.activation(statsa[32:34, tt * 512:(tt + 1) * 512],
                                         pb[4 + tt][0:2, 0:512],
                                         AT.Ln, bias=RMS_EPS, scale=1.0 / 64)
                        e.drain()
                        for tt in range(4):
                            ac = e.activation(statsa[0:2, tt * 512:(tt + 1) * 512],
                                              statsa[32:34, tt * 512:(tt + 1) * 512],
                                              AT.Exp, scale=-0.5)
                        s_s.inc(ac)
                    on("scalar", f)

                    bcv = {}
                    for tt in range(4):
                        def f(e, tt=tt):
                            if tt < 2:
                                s_s.wait(e)
                            else:
                                s_s.wait(e, bcv[tt - 2])
                            s_t.inc(e.matmul(pb[2 + tt % 2][0:128, 0:512], ind2_f[0:2, :],
                                     statsa[0:2, tt * 512:(tt + 1) * 512], start=True, stop=True))
                        on("tensor", f)

                        def f(e, dstb=dstb, tt=tt):
                            s_t.wait(e)
                            s_s.inc(e.copy(dstb[:, tt * 512:(tt + 1) * 512],
                                           pb[2 + tt % 2][0:128, 0:512]))
                            bcv[tt] = s_s.v
                        on("scalar", f)

                    def f(e, pi=pi, qk=qk, dstb=dstb):
                        s_s.wait(e)
                        vsel = qcv if qk == 0 else kcv
                        p = pairs[pi]
                        for hh in range(2):
                            hidx = 2 * pi + hh
                            dst = (qt if qk == 0 else kt_)[hidx]
                            nwc = qnw if qk == 0 else knw
                            full = vsel(p, 0, 2048)
                            st = e.scalar_tensor_tensor(dst[0:64, :], full[hh * 64:hh * 64 + 64, :],
                                                   nwc[hh * 64:hh * 64 + 64, :],
                                                   dstb[hh * 64:hh * 64 + 64, :],
                                                   OP.mult, OP.mult)
                        s_v.inc(st)
                    on("vector", f)

            for pi, p in enumerate(pairs):
                def f(e, p=p):
                    s_s.wait(e)
                    s_v.wait(e)
                    for tt in range(4):
                        mm = e.matmul(pb[tt][0:128, 0:512], wkrt_blk[:],
                                      xa[p][:, tt * 512:(tt + 1) * 512], start=True, stop=True)
                    s_t.inc(mm)
                on("tensor", f)

                def f(e, pi=pi):
                    s_t.wait(e)
                    for tt in range(4):
                        e.copy(kt_[2 * pi][64:128, tt * 512:(tt + 1) * 512], pb[tt][0:64, 0:512])
                        cp = e.copy(kt_[2 * pi + 1][64:128, tt * 512:(tt + 1) * 512], pb[tt][64:128, 0:512])
                    s_s.inc(cp)
                on("scalar", f)
            for hidx in range(4):
                def f(e, hidx=hidx):
                    s_s.wait(e)
                    s_v.wait(e)
                    for tt in range(4):
                        mm = e.matmul(pb[4 + tt][0:64, 0:512], wqrt_t[:],
                                      qt[hidx][0:64, tt * 512:(tt + 1) * 512], start=True, stop=True)
                    s_t.inc(mm)
                on("tensor", f)

                def f(e, hidx=hidx):
                    s_t.wait(e)
                    for tt in range(4):
                        cp = e.copy(qt[hidx][64:128, tt * 512:(tt + 1) * 512], pb[4 + tt][0:64, 0:512])
                    s_s.inc(cp)
                on("scalar", f)

            barrier()

            SC = 0.125
            svals = {}
            pk_abs = [0]
            for hh in range(4):
                hpair = hh % 2
                pi = hh // 2
                p = pairs[pi]
                qth, kth, vta = qt[hh], kt_[hh], vt[pi]
                vcol = hpair * 65
                for qsup in range(4):
                    nkt = 4 * qsup + 4
                    for pk in range(nkt // 2):
                        pa = pk_abs[0]
                        pk_abs[0] += 1
                        kts = (2 * pk, 2 * pk + 1)
                        bks = (pb[0], pb[1]) if pa % 2 == 0 else (pb[5], pb[6])
                        exs = (expsT[pa % 2], expsTb[pa % 2])
                        parts = tuple(kt >= 4 * qsup for kt in kts)
                        j0s = tuple((kt - 4 * qsup) * 128 if kt >= 4 * qsup else 0 for kt in kts)
                        FDs = tuple(512 - j0 for j0 in j0s)

                        def f(e, kts=kts, bks=bks, FDs=FDs, qsup=qsup, j0s=j0s,
                              qth=qth, kth=kth, pa=pa):
                            if pa >= 2:
                                s_s.wait(e, svals.get(pa - 2, 0))
                            for kt, bk, j0, FD in zip(kts, bks, j0s, FDs):
                                mm = e.matmul(bk[0:128, 0:FD], kth[:, kt * 128:(kt + 1) * 128],
                                              qth[:, qsup * 512 + j0:(qsup + 1) * 512],
                                              start=True, stop=True)
                            s_t.inc(mm)
                        on("tensor", f)

                        def f(e, bks=bks, exs=exs, parts=parts, FDs=FDs, pa=pa):
                            s_t.wait(e)
                            for i2 in range(2):
                                bk, ex, part, FD = bks[i2], exs[i2], parts[i2], FDs[i2]
                                tmp = expTmp if i2 == 0 else expTmp2
                                if part:
                                    ac = e.activation(tmp[:], bk[0:128, 0:128], AT.Exp, scale=SC)
                                    if FD > 128:
                                        ac = e.activation(ex[:, 128:FD], bk[0:128, 128:FD],
                                                          AT.Exp, scale=SC)
                                else:
                                    ac = e.activation(ex[:, 0:FD], bk[0:128, 0:FD], AT.Exp, scale=SC)
                            s_s.inc(ac)
                            svals[pa] = s_s.v
                        on("scalar", f)

                        if parts[0] or parts[1]:
                            def f(e, exs=exs, parts=parts):
                                s_s.wait(e)
                                for i2 in range(2):
                                    if parts[i2]:
                                        tmp = expTmp if i2 == 0 else expTmp2
                                        mu = e.tensor_mul(exs[i2][:, 0:128], tmp[:], tri0[:])
                                s_v.inc(mu)
                            on("vector", f)

                        def f(e, kts=kts, exs=exs, parts=parts, j0s=j0s, FDs=FDs,
                              nkt=nkt, vta=vta, vcol=vcol):
                            s_s.wait(e)
                            if parts[0] or parts[1] or kts[0] == 0:
                                s_v.wait(e)
                            for i2 in range(2):
                                kt, ex, j0, FD = kts[i2], exs[i2], j0s[i2], FDs[i2]
                                mm = e.matmul(pb[2][0:65, j0:512],
                                              vta[:, kt * 130 + vcol:kt * 130 + vcol + 65],
                                              ex[:, 0:FD], start=(kt == 0), stop=(kt == nkt - 1))
                            s_t.inc(mm)
                        on("tensor", f)

                    def f(e):
                        s_t.wait(e)
                        e.reciprocal(statsa[0:1, 0:512], pb[2][64:65, 0:512])
                        e.drain()
                        e.tensor_copy(ctxE[:], pb[2][0:64, 0:512])
                        s_v.inc(e.drain())
                    on("vector", f)

                    def f(e):
                        s_v.wait(e)
                        e.matmul(pb[3][0:64, 0:512], wot_t[:], ctxE[:], start=True, stop=True)
                        s_t.inc(e.matmul(pb[4][0:64, 0:512], onesrow_f[0:1, 0:64],
                                 statsa[0:1, 0:512], start=True, stop=True))
                    on("tensor", f)

                    def f(e):
                        s_t.wait(e)
                        s_v.wait(e)
                        e.tensor_copy(lntmp[0:64, :], pb[4][0:64, 0:512])
                        s_v.inc(e.drain())
                    on("vector", f)

                    def f(e, pi=pi, hpair=hpair, qsup=qsup):
                        s_t.wait(e)
                        s_v.inc(e.tensor_mul(attnp[pi][hpair * 64:hpair * 64 + 64, qsup * 512:(qsup + 1) * 512],
                                             pb[3][0:64, 0:512], lntmp[0:64, :]))
                    on("vector", f)

            def f(e, pairs=pairs):
                s_v.wait(e)
                for pi, p in enumerate(pairs):
                    ad = e.tensor_add(xb[p][:], xa[p][:], attnp[pi][:])
                s_g.inc(ad)
            on("gpsimd", f)
            barrier()

        dump("y1", xb)
        emit_module_ln(xb, xa, sqf)
        dump("x1", xa)
        ph.close()

    def l0_ffn():
        ph = ExitStack()
        x1q = [sb(f"x1q{i}", 128, T, F8, ph) for i in range(8)]
        kq8 = [sb(f"kq8{i}", 128, D, F8, ph) for i in range(8)]
        kbf2 = [sb(f"kbf2_{i}", 128, D, BF16, ph) for i in range(2)]
        xbf = sb("xbf", 128, T, BF16, ph)
        ybf2 = [sb(f"ybf{i}", 128, 512, BF16, ph) for i in range(2)]
        sq2 = sb("sq2", 128, T, BF16, ph)
        sqf = [xbf, sq2]

        def f(e):
            e.reciprocal(srow[0:1, 2:3], srow[0:1, 0:1])
            e.drain()
            e.reciprocal(srow[0:1, 3:4], srow[0:1, 1:2])
            e.drain()
            e.tensor_mul(srow[0:1, 4:5], srow[0:1, 0:1], srow[0:1, 1:2])
            s_v.inc(e.drain())
        on("vector", f)

        def f(e):
            s_v.wait(e)
            e.matmul(pb[0][0:128, 0:1], onesrow_f[:], srow[0:1, 2:3], start=True, stop=True)
            e.matmul(pb[0][0:128, 1:2], onesrow_f[:], srow[0:1, 3:4], start=True, stop=True)
            s_t.inc(e.matmul(pb[0][0:128, 2:3], onesrow_f[:], srow[0:1, 4:5], start=True, stop=True))
        on("tensor", f)

        def f(e):
            s_t.wait(e)
            e.copy(sinv_c[:], pb[0][0:128, 0:1])
            e.copy(skinv_c[:], pb[0][0:128, 1:2])
            e.copy(s2_c[:], pb[0][0:128, 2:3])
            s_s.inc(e.drain())
        on("scalar", f)

        for ci in range(8):
            sem = s_lda if ci % 2 == 0 else s_ldb

            def f(e, ci=ci, sem=sem):
                if ci < 2:
                    e.wait_ge(s_bar.h, s_bar.v)
                else:
                    s_s.wait(e)
                sem.inc(e.dma_start(out=kbf2[ci % 2][:], in_=w["f0_kernel"][ci * 128:(ci + 1) * 128, :]), 16)
            on("sync", f)

            def f(e, ci=ci, sem=sem):
                sem.wait(e)
                s_s.inc(e.activation(kq8[ci][:], kbf2[ci % 2][:], AT.Copy, scale=skinv_c[:]))
            on("scalar", f)

        def f(e):
            s_s.wait(e)
            xh = [xbf[:, 0:1024], xbf[:, 1024:2048]]
            for i in range(16):
                src_ap = xa[i // 2][:, (i % 2) * 1024:(i % 2 + 1) * 1024]
                dst_ap = x1q[i // 2][:, (i % 2) * 1024:(i % 2 + 1) * 1024]
                e.activation(xh[i % 2], src_ap, AT.Copy)
                e.drain()
                ac = e.activation(dst_ap, xh[i % 2], AT.Copy, scale=sinv_c[:])
            s_s.inc(ac)
        on("scalar", f)
        barrier()

        tiles = [(ot, tt) for ot in range(8) for tt in range(4)]
        gvals = {}
        for pi2 in range(16):
            pr = tiles[2 * pi2:2 * pi2 + 2]
            bks = (pb[0], pb[1]) if pi2 % 2 == 0 else (pb[2], pb[3])

            def f(e, pr=pr, bks=bks, pi2=pi2):
                if pi2 < 2:
                    s_v.wait(e)
                else:
                    s_v.wait(e, gvals[pi2 - 2])
                for (ot, tt), bank in zip(pr, bks):
                    for ci in range(8):
                        mm = e.matmul(bank[0:128, 0:512], kq8[ci][:, ot * 128:(ot + 1) * 128],
                                      x1q[ci][:, tt * 512:(tt + 1) * 512],
                                      start=(ci == 0), stop=(ci == 7))
                s_t.inc(mm)
            on("tensor", f)

            def f(e, pr=pr, bks=bks, pi2=pi2):
                s_t.wait(e)
                s_v.wait(e)
                for i2, ((ot, tt), bank) in enumerate(zip(pr, bks)):
                    ac = e.activation(ybf2[i2][:], bank[0:128, 0:512], AT.Copy, scale=s2_c[:])
                s_s.inc(ac)
            on("scalar", f)

            def f(e, pr=pr, pi2=pi2):
                s_s.wait(e)
                for i2, (ot, tt) in enumerate(pr):
                    e.tensor_scalar(sq2[:, i2 * 512:(i2 + 1) * 512], ybf2[i2][:],
                                    f0bias[:, ot:ot + 1], None, OP.add)
                e.drain()
                for i2, (ot, tt) in enumerate(pr):
                    ad = e.tensor_add(xb[ot][:, tt * 512:(tt + 1) * 512],
                                      sq2[:, i2 * 512:(i2 + 1) * 512],
                                      xa[ot][:, tt * 512:(tt + 1) * 512])
                s_v.inc(ad)
                gvals[pi2] = s_v.v
            on("vector", f)
        barrier()
        dump("y2", xb)
        emit_module_ln(xb, x2, sqf)
        dump("x2", [t_ for t_ in x2])
        ph.close()

    def l1_attn(l1s):
        wb_s = sb("wb_s", 128, 64, BF16, l1s)
        wcur = [sb(f"wcur{i}", 128, 2048, BF16, l1s) for i in range(2)]
        vpost = sb("vpost", 128, 4096, BF16, l1s)
        awin = sb("awin", 128, 4096, BF16, l1s)
        betaw = sb("betaw", H1, WIN, F32, l1s)
        betaf = lntmp[0:1, 0:512]
        gscr = sb("gscr", 128, 552, F32, l1s)
        giscr = sb("giscr", 128, 552, F32, l1s)
        convt = [sb(f"convt{i}", 128, 512, F32, l1s) for i in range(2)]
        SST = sb("SST", 128, 1024, BF16, l1s)
        wS = sb("wS", 128, 512, BF16, l1s)
        wping = [sb(f"wping{i}", 128, 128, BF16, l1s) for i in range(2)]
        rpS = sb("rpS", 128, 512, BF16, l1s)
        arecipS = sb("arecipS", 128, 512, F32, l1s)
        recipBA = sb("recipBA", 128, 512, F32, l1s)
        AgrS = sb("AgrS", 64, 512, BF16, l1s)
        BgrS = sb("BgrS", 64, 512, BF16, l1s)
        BDset = sb("BDset", 128, 1280, BF16, l1s)
        wCons = sb("wCons", 64, 1024, BF16, l1s)
        ktokN = sb("ktokN", 64, 1024, BF16, l1s)
        gCrows = sb("gCrows", 128, 256, F32, l1s)
        gCrow0 = sb("gCrow0", 1, 1024, F32, l1s)
        M8 = sb("M8", 128, 128, BF16, l1s)
        maskBnP = sb("maskBnP", 64, 512, BF16, l1s)
        R16S = sb("R16S", 64, 512, BF16, l1s)
        rv = statsa[:, 0:256].bitcast(BF16)
        Vg = statsa[:, 256:512].bitcast(BF16)
        bndq = sb("bndq", 128, 24, BF16, l1s)
        bndk = sb("bndk", 128, 24, BF16, l1s)
        bndv = sb("bndv", 128, 24, BF16, l1s)
        bndx = sb("bndx", 128, 8, BF16, l1s)
        y3w = None

        def xbv(i):
            return xb[i][:].bitcast(BF16)

        def pre_ap(P, od, c0, c1):
            t_ = xb[P * 2 + od // 4][:]
            base = (od % 4) * 515
            return t_[:, (base + c0) // 2 * 0:0] if False else xb[P * 2 + od // 4][:].bitcast(BF16)[:, base + c0:base + c1]

        def post_ap(P, c0, c1):
            if P == 0:
                return xb[6][:].bitcast(BF16)[:, c0:c1]
            if P == 1:
                return xb[7][:].bitcast(BF16)[:, c0:c1]
            return vpost[:, c0:c1]

        wcur_free = {"v": 0}

        def f(e):
            s_ldg.inc(e.dma_start(out=wb_s[:], in_=w["Wb"]), 16)
            e.memset(BDset[:], 0.0)
            e.memset(SST[:], 0.0)
            e.memset(gscr[:, 520:552], 0.0)
            e.memset(giscr[:, 520:552], 0.0)
            e.memset(bndq[:], 0.0)
            e.memset(bndk[:], 0.0)
            e.memset(bndv[:], 0.0)
            e.memset(bndx[:], 0.0)
            e.drain()
            for i in range(NTIER):
                e.tensor_copy(R16S[:, i * 128:(i + 1) * 128],
                              bass.AP(tensor=ident_b, offset=16 * i,
                                      ap=[[128, 64], [0, 8], [1, 16]]))
            s_g.inc(e.drain())
        on("gpsimd", f)

        def f(e):
            s_g.wait(e)
            e.tensor_reduce(statsa[:, 0:8],
                            ident_b[:].rearrange("p (h t) -> p h t", t=16),
                            AX.X, OP.add)
            e.drain()
            e.tensor_copy(M8[:].rearrange("p (h t) -> p h t", t=16),
                          bass.AP(tensor=statsa, offset=0,
                                  ap=[[2048, 128], [1, 8], [0, 16]]))
            s_v.inc(e.drain())
        on("vector", f)
        barrier()

        for wi in range(NWIN if l1_nwin is None else l1_nwin):
            l1_window(wi, locals())

    def l1_window(wi, V):
        wb_s = V["wb_s"]; wcur = V["wcur"]; vpost = V["vpost"]; awin = V["awin"]
        wcur_free = V["wcur_free"]
        betaw = V["betaw"]; betaf = V["betaf"]
        ggwin = xb[0][:].bitcast(BF16)
        owin = xb[1][:].bitcast(BF16)
        gscr = V["gscr"]; convt = V["convt"]; SST = V["SST"]; wS = V["wS"]
        giscr = V["giscr"]
        wping = V["wping"]; rpS = V["rpS"]; arecipS = V["arecipS"]; recipBA = V["recipBA"]
        AgrS = V["AgrS"]; BgrS = V["BgrS"]; BDset = V["BDset"]; wCons = V["wCons"]
        ktokN = V["ktokN"]; Vg = V["Vg"]; gCrows = V["gCrows"]; rv = V["rv"]
        gCrow0 = V["gCrow0"]
        M8 = V["M8"]; maskBnP = V["maskBnP"]; R16S = V["R16S"]
        bndq = V["bndq"]; bndk = V["bndk"]; bndv = V["bndv"]
        pre_ap = V["pre_ap"]; post_ap = V["post_ap"]

        tw = slice(wi * WIN, (wi + 1) * WIN)
        x2w = lambda ci: x2[ci][:, tw]


        def f(e):
            if wi == 0:
                s_ldg.wait(e)
            s_g.wait(e)
            s_v.wait(e)
            for ci in range(8):
                mm = e.matmul(pb[6][0:8, 0:512], wb_s[:, ci * 8:(ci + 1) * 8], x2w(ci),
                              start=(ci == 0), stop=(ci == 7))
            s_t.inc(mm)
        on("tensor", f)

        def f(e):
            s_t.wait(e)
            s_s.inc(e.activation(betaw[:, 0:WIN], pb[6][0:8, 0:512], AT.Sigmoid, bias=bb_c[:]))
        on("scalar", f)

        cwn = {0: "cqw", 1: "ckw", 2: "cvw"}
        cbn = {0: "cqb", 1: "ckb", 2: "cvb"}

        def restore_boundary(P):
            def f(e, P=P):
                s_s.wait(e)
                bnd = (bndq, bndk, bndv)[P]
                for od in range(8):
                    e.tensor_copy(pre_ap(P, od, 0, 3), bnd[:, od * 3:od * 3 + 3])
                e.drain()
                for od in range(8):
                    e.tensor_copy(bnd[:, od * 3:od * 3 + 3], pre_ap(P, od, 512, 515))
                s_g.inc(e.drain())
            on("gpsimd", f)

        def emit_conv(P):
            restore_boundary(P)
            for od in range(8):
                def f(e, P=P, od=od):
                    s_g.wait(e)
                    s_s.wait(e)
                    wv = cw[cwn[P]]
                    e.tensor_scalar(convt[od % 2][:], pre_ap(P, od, 0, 512),
                                    wv[:, od * 4 + 0:od * 4 + 1], None, OP.mult)
                    e.drain()
                    for j in range(1, 4):
                        st = e.scalar_tensor_tensor(convt[od % 2][:], pre_ap(P, od, j, j + 512),
                                                    wv[:, od * 4 + j:od * 4 + j + 1],
                                                    convt[od % 2][:], OP.mult, OP.add)
                        if j < 3:
                            e.drain()
                    s_v.inc(st)
                on("vector", f)

                def f(e, P=P, od=od):
                    s_v.wait(e)
                    if P <= 1 and od == 0:
                        s_st.wait(e)
                    if P == 1:
                        stg = lntmp[:, 0:256].bitcast(BF16)
                        s_s.inc(e.activation(stg, convt[od % 2][:], AT.Silu, bias=cb[cbn[P]][:, od:od + 1]))
                    else:
                        dst = post_ap(P, od * 512, (od + 1) * 512)
                        s_s.inc(e.activation(dst, convt[od % 2][:], AT.Silu, bias=cb[cbn[P]][:, od:od + 1]))
                on("scalar", f)
                if P == 1:
                    def f(e, od=od):
                        s_s.wait(e)
                        stg = lntmp[:, 0:256].bitcast(BF16)
                        s_v.inc(e.tensor_scalar(post_ap(1, od * 512, (od + 1) * 512), stg, float(128 ** -0.5), None, OP.mult))
                    on("vector", f)

        projs = (("Wq", 0), ("Wk", 1), ("Wv", 2), ("Wa", 3), ("Wg", 4))
        ptvals = {}
        pentry = s_t.v
        for mi, (mn, P) in enumerate(projs):
            for qtr in range(4):
                def f(e, mn=mn, qtr=qtr, mi=mi):
                    qi = mi * 4 + qtr
                    if qi >= 2:
                        s_t.wait(e, ptvals[qi - 2])
                    elif wi > 0:
                        s_t.wait(e, wcur_free["v"])
                    sem = s_mv if qi % 2 == 0 else s_mw
                    sem.inc(e.dma_start(
                        out=wcur[qtr % 2][:].rearrange("p (a m) -> p a m", m=1024),
                        in_=wbf[mn][qtr * 256:(qtr + 1) * 256, :].rearrange("(a p) m -> p a m", p=128)), 16)
                on("gpsimd", f)

                def f(e, qtr=qtr, mi=mi, P=P):
                    (s_mv if (mi * 4 + qtr) % 2 == 0 else s_mw).wait(e)
                    if qtr == 0:
                        s_s.wait(e)
                        s_g.wait(e)
                        if P == 4:
                            s_v.wait(e)
                    buf = wcur[qtr % 2]
                    for od in range(8):
                        for c2 in range(2):
                            ci = qtr * 2 + c2
                            mm = e.matmul(pb[od][0:128, 0:512],
                                          buf[:, c2 * 1024 + od * 128:c2 * 1024 + (od + 1) * 128],
                                          x2w(ci), start=(ci == 0), stop=(ci == 7))
                    s_t.inc(mm)
                    ptvals[mi * 4 + qtr] = s_t.v
                on("tensor", f)

            def f(e, P=P):
                s_t.wait(e)
                for od in range(8):
                    bank = pb[od]
                    if P <= 2:
                        op = e.copy(pre_ap(P, od, 3, 515), bank[0:128, 0:512])
                    elif P == 3:
                        op = e.activation(awin[:, od * 512:(od + 1) * 512], bank[0:128, 0:512],
                                          AT.Sigmoid, bias=ba_c[:, od:od + 1])
                    else:
                        op = e.activation(ggwin[:, od * 512:(od + 1) * 512], bank[0:128, 0:512],
                                          AT.Sigmoid)
                s_s.inc(op)
            on("scalar", f)
            if P <= 2:
                emit_conv(P)
        barrier()
        if wi == 0 and "l1a" in dbg:
            dump("qpost", [post_ap(0, 0, 4096)])
            dump("kpost", [post_ap(1, 0, 4096)])
            dump("vpost", [post_ap(2, 0, 4096)])
            dump("awin", [awin[:]])
            dump("ggwin", [ggwin[:]])
            dump("betaw", [betaw[:]])

        V = dict(V)
        V["_wi"] = wi
        V["_kpost_t"] = xb[7]
        V["_kpost_off"] = 0
        V["_kpost_pitch"] = 4096
        if l1_stop == "proj":
            return
        for c in range(CPW if l1_nchunk is None else l1_nchunk):
            l1_chunk(wi, c, V)
        if wi == 0 and "owin" in dbg:
            dump("owin", [xb[1][:].bitcast(BF16)])
        if l1_stop == "chunks":
            return

        l1_tail(wi, V)

    def l1_chunk(wi, c, V):
        awin = V["awin"]; vpost = V["vpost"]; betaw = V["betaw"]; betaf = V["betaf"]
        gscr = V["gscr"]; giscr = V["giscr"]; SST = V["SST"]; wS = V["wS"]; wping = V["wping"]
        convt = V["convt"]
        rpS = V["rpS"]; arecipS = V["arecipS"]; recipBA = V["recipBA"]
        AgrS = V["AgrS"]; BgrS = V["BgrS"]; BDset = V["BDset"]; wCons = V["wCons"]
        ktokN = V["ktokN"]; Vg = V["Vg"]; gCrows = V["gCrows"]; rv = V["rv"]
        gCrow0 = V["gCrow0"]
        M8 = V["M8"]; maskBnP = V["maskBnP"]; R16S = V["R16S"]
        pre_ap = V["pre_ap"]; post_ap = V["post_ap"]
        owin = xb[1][:].bitcast(BF16)
        c0 = c * CH

        def kx(h):
            return post_ap(1, h * 512 + c0, h * 512 + c0 + CH)

        def qx(h):
            return post_ap(0, h * 512 + c0, h * 512 + c0 + CH)

        aw_ap = awin[:].rearrange("p (h t) -> p h t", h=8)[:, :, c0:c0 + CH]
        vw_ap = vpost[:].rearrange("p (h t) -> p h t", h=8)[:, :, c0:c0 + CH]
        g_ap = gscr[:, 0:520].rearrange("p (h t) -> p h t", h=8)
        gprev_ap = g_ap[:, :, 0:CH]
        gcur_ap = g_ap[:, :, 1:CH + 1]

        def f(e):
            s_v.wait(e)
            s_t.wait(e)
            e.memset(gscr[:, 0:520].rearrange("p (h t) -> p h t", h=8)[:, :, 0:1], 1.0)
            e.memset(giscr[:, 0:520].rearrange("p (h t) -> p h t", h=8)[:, :, 0:1], 1.0)
            s_g.inc(e.drain())
            s_mv.inc(e.dma_start(out=betaf.rearrange("q (h t) -> q h t", h=8),
                                 in_=betaw[:, c0:c0 + CH].rearrange("h (q t) -> h q t", q=1)), 16)
        on("gpsimd", f)

        gi_ap = giscr[:, 0:520].rearrange("p (h t) -> p h t", h=8)
        giprev_ap = gi_ap[:, :, 0:CH]

        def f(e):
            s_g.wait(e)
            s_s.wait(e)
            e.reciprocal(convt[0][:].rearrange("p (h t) -> p h t", h=8), aw_ap)
            e.drain()
            for h in range(8):
                e.tensor_tensor_scan(gscr[:, h * 65 + 1:h * 65 + 65],
                                     awin[:, h * 512 + c0:h * 512 + c0 + CH],
                                     zeros64[:, 0:CH], 1.0, OP.mult, OP.add)
                e.tensor_tensor_scan(giscr[:, h * 65 + 1:h * 65 + 65],
                                     convt[0][:, h * CH:(h + 1) * CH],
                                     zeros64[:, 0:CH], 1.0, OP.mult, OP.add)
            e.drain()
            e.tensor_tensor(Vg[:].rearrange("p (h t) -> p h t", h=8), vw_ap, giprev_ap, OP.mult)
            s_v.inc(e.drain())
        on("vector", f)

        def f(e):
            s_mv.wait(e)
            s_v.wait(e)
            s_t.inc(e.matmul(pb[2][0:128, 0:512], onesrow_f[:], betaf, start=True, stop=True))
        on("tensor", f)

        def f(e):
            s_t.wait(e)
            e.tensor_tensor(recipBA[:], pb[2][0:128, 0:512], convt[0][:], OP.mult)
            e.drain()
            for i in range(NTIER):
                e.tensor_copy(lntmp[:, 128 * i:128 * (i + 1)],
                              recipBA[:].rearrange("p (h t) -> p h t", h=8)[:, :, 16 * i:16 * i + 16])
            for hg in range(2):
                cp = e.tensor_copy(statsa[:, 1792 + hg * 128:1792 + (hg + 1) * 128],
                                   bass.AP(tensor=gscr, offset=hg * 260 + 64,
                                           ap=[[552, 128], [65, 4], [1, 32]]))
            s_v.inc(cp)
        on("vector", f)

        def f(e):
            s_v.wait(e)
            for i in range(NTIER):
                tr = e.transpose(pb[2][0:128, 128 * i:128 * i + 128], lntmp[:, 128 * i:128 * (i + 1)], ident_f[:])
            s_t.inc(tr)
        on("tensor", f)

        def f(e):
            s_t.wait(e)
            s_v.inc(e.tensor_copy(arecipS[:], pb[2][0:128, 0:512]))
        on("vector", f)

        def f(e):
            s_v.wait(e)
            for hg in range(2):
                tr = e.transpose(pb[2][0:128, hg * 128:hg * 128 + 128],
                                 statsa[:, 1792 + hg * 128:1792 + (hg + 1) * 128],
                                 ident_f[:])
            s_t.inc(tr)
        on("tensor", f)

        def f(e):
            s_t.wait(e)
            for h in range(8):
                cp = e.tensor_copy(gCrow0[0:1, h * 128:(h + 1) * 128],
                                   pb[2][32 * (h % 4):32 * (h % 4) + 1,
                                         (h // 4) * 128:(h // 4) * 128 + 128])
            s_v.inc(cp)
        on("vector", f)

        def f(e):
            s_s.wait(e)
            s_v.wait(e)
            for h in range(8):
                e.matmul(pb[0][0:64, h * 64:(h + 1) * 64], kx(h), kx(h), start=True, stop=True)
            for h in range(8):
                mm = e.matmul(pb[1][0:64, h * 64:(h + 1) * 64], kx(h), qx(h), start=True, stop=True)
            s_t.inc(mm)
        on("tensor", f)

        def f(e):
            s_t.wait(e)
            e.tensor_mul(AgrS[:], pb[0][0:64, 0:512], maskAn[:])
            s_v.inc(e.tensor_mul(BgrS[:], pb[1][0:64, 0:512], maskBn[:]))
        on("vector", f)

        OFFCOLS = {(0, 1): 512, (0, 2): 640, (1, 2): 768, (0, 3): 896, (1, 3): 1024, (2, 3): 1152}
        T1S = statsa[:, 768:1792].bitcast(BF16)
        t1v = {}
        for i in range(NTIER):
            def f(e, i=i):
                if i < 2:
                    s_v.wait(e)
                else:
                    s_v.wait(e, t1v[i - 2])
                s_t.inc(e.matmul(pb[4 + i % 2][0:128, 0:512], R16S[:, i * 128:(i + 1) * 128],
                                 AgrS[:], start=True, stop=True))
            on("tensor", f)

            def f(e, i=i):
                s_t.wait(e)
                s_v.inc(e.tensor_copy(T1S[:, i * 512:(i + 1) * 512], pb[4 + i % 2][0:128, 0:512]))
                t1v[i] = s_v.v
            on("vector", f)

        def t1sel(pp, i2):
            return T1S[:, pp * 512:(pp + 1) * 512] \
                .rearrange("p (h q) -> p h q", q=64)[:, :, 16 * i2:16 * i2 + 16]

        def f(e):
            s_v.wait(e)
            s_t.wait(e)
            for i in range(NTIER):
                e.tensor_mul(BDset[:, i * 128:(i + 1) * 128], t1sel(i, i), M8[:])
            for (pp, i2), col in OFFCOLS.items():
                mu = e.tensor_mul(BDset[:, col:col + 128], t1sel(pp, i2), M8[:])
            s_g.inc(mu)
        on("gpsimd", f)

        def f(e):
            s_s.wait(e)
            s_v.wait(e)
            for h in range(8):
                tr = e.transpose(pb[2][0:64, 0:512].bitcast(BF16)[:, h * 128:(h + 1) * 128],
                                 kx(h), ident_b[:])
            s_t.inc(tr)
        on("tensor", f)

        def f(e):
            s_t.wait(e)
            s_v.inc(e.tensor_scalar(ktokN[:], pb[2][0:64, 0:512].bitcast(BF16),
                                    -1.0, None, OP.mult))
        on("vector", f)
        if l1_cstop == "c1":
            return

        def f(e):
            s_v.wait(e)
            for h in range(8):
                mm = e.matmul(pb[6][0:128, h * 64:(h + 1) * 64], SST[:, h * 128:(h + 1) * 128],
                              kx(h), start=True, stop=True)
            s_t.inc(mm)
        on("tensor", f)

        rTS = statsa[:, 512:768].bitcast(BF16)
        rvstg = statsa[:, 1792:2048].bitcast(BF16)

        def f(e):
            s_t.wait(e)
            e.tensor_sub(rv[:], pb[6][0:128, 0:512], Vg[:])
            e.drain()
            for i in range(NTIER):
                cp = e.tensor_copy(rvstg[:, 128 * i:128 * (i + 1)],
                                   rv.rearrange("p (h t) -> p h t", h=8)[:, :, 16 * i:16 * i + 16])
            s_v.inc(cp)
        on("vector", f)

        def f(e):
            s_v.wait(e)
            for i in range(NTIER):
                tr = e.transpose(pb[2][0:128, 64 * i:64 * i + 64].bitcast(BF16),
                                 rvstg[:, 128 * i:128 * (i + 1)], ident_b[:])
            s_t.inc(tr)
        on("tensor", f)

        def f(e):
            s_t.wait(e)
            s_v.inc(e.tensor_copy(rTS, pb[2][0:128, 0:256].bitcast(BF16)))
        on("vector", f)
        if l1_cstop == "c2":
            return

        for i in range(NTIER):
            def f(e, i=i):
                if i == 0:
                    s_g.wait(e)
                s_v.wait(e)
                mm1 = e.matmul(pb[3][0:128, 128 * i:128 * i + 128], ident_b[:],
                               rTS[:, 128 * i:128 * i + 128], start=True, stop=(i == 0))
                for pp in range(i):
                    mm1 = e.matmul(pb[3][0:128, 128 * i:128 * i + 128],
                                   BDset[:, OFFCOLS[(pp, i)]:OFFCOLS[(pp, i)] + 128],
                                   wS[:, pp * 128:(pp + 1) * 128], start=False, stop=(pp == i - 1))
                s_t.inc(mm1)
            on("tensor", f)

            def f(e, i=i):
                s_t.wait(e)
                s_v.inc(e.tensor_copy(rpS[:, 128 * i:128 * i + 128],
                                      pb[3][0:128, 128 * i:128 * i + 128]))
            on("vector", f)

            for m in range(TIER):
                def f(e, i=i, m=m):
                    s_v.wait(e)
                    bank = pb[4 + m % 2]
                    mm1 = e.matmul(bank[0:128, 0:128], ident_b[:], rpS[:, 128 * i:128 * i + 128],
                                   start=True, stop=(m == 0))
                    if m > 0:
                        mm1 = e.matmul(bank[0:128, 0:128], BDset[:, 128 * i:128 * i + 128],
                                       wping[(m + 1) % 2][:], start=False, stop=True)
                    s_t.inc(mm1)
                on("tensor", f)

                def f(e, i=i, m=m):
                    s_t.wait(e)
                    dst = wS[:, 128 * i:128 * i + 128] if m == TIER - 1 else wping[m % 2][:]
                    s_v.inc(e.tensor_mul(dst, pb[4 + m % 2][0:128, 0:128],
                                         arecipS[:, 128 * i:128 * i + 128]))
                on("vector", f)

        if l1_cstop == "c3":
            return
        def f(e):
            s_v.wait(e)
            s_t.wait(e)
            s_wc.inc(e.dma_start(out=wsd, in_=wS[:]), 16)
            e.wait_ge(s_wc.h, s_wc.v)
            for i in range(NTIER):
                srcp = bass.AP(tensor=wsd.tensor, offset=i * 128,
                               ap=[[512, 16], [16 * 512, 8], [1, 128]])
                s_wc.inc(e.dma_start(out=wCons[16 * i:16 * (i + 1), :], in_=srcp), 16)
            if l1_cstop == "c4a":
                e.wait_ge(s_wc.h, s_wc.v)
        on("gpsimd", f)
        if l1_cstop == "c4a":
            return

        last_chunk = (wi == NWIN - 1 and c == CPW - 1)

        def f(e):
            s_v.wait(e)
            for h in range(8):
                mm = e.matmul(pb[0][0:128, h * 64:(h + 1) * 64], SST[:, h * 128:(h + 1) * 128],
                              qx(h), start=(h == 0), stop=False)
            if not last_chunk:
                for h in range(8):
                    bank = pb[6] if h < 4 else pb[7]
                    col = (h % 4) * 128
                    e.matmul(bank[0:128, col:col + 128], ident_b[:], SST[:, h * 128:(h + 1) * 128],
                             start=(h % 4 == 0), stop=False)
                for h in range(8):
                    bank = pb[1] if h < 4 else pb[2]
                    col = (h % 4) * 128
                    mm = e.matmul(bank[0:128, col:col + 128], ones128_f[0:1, :],
                                  gCrow0[0:1, h * 128:(h + 1) * 128],
                                  start=(h % 4 == 0), stop=(h % 4 == 3))
            s_t.inc(mm)
        on("tensor", f)

        def f(e):
            s_wc.wait(e)
            for h in range(8):
                mm = e.matmul(pb[0][0:128, h * 64:(h + 1) * 64], wCons[0:64, h * 128:(h + 1) * 128],
                              BgrS[0:64, h * 64:(h + 1) * 64], start=False, stop=(h == 7))
            if not last_chunk:
                for h in range(8):
                    mm = e.matmul(pb[6 if h < 4 else 7][0:128, (h % 4) * 128:(h % 4) * 128 + 128],
                                  ktokN[0:64, h * 128:(h + 1) * 128],
                                  wCons[0:64, h * 128:(h + 1) * 128], start=False, stop=(h % 4 == 3))
            s_t.inc(mm)
        on("tensor", f)

        def f(e, c=c):
            s_t.wait(e)
            s_v.inc(e.tensor_tensor(owin[:].rearrange("p (h t) -> p h t", h=8)[:, :, c0:c0 + CH],
                                    pb[0][0:128, 0:512].rearrange("p (h t) -> p h t", h=8),
                                    gcur_ap, OP.mult))
        on("vector", f)

        if l1_cstop == "c4b":
            return
        if not last_chunk:
            def f(e):
                s_t.wait(e)
                e.tensor_copy(statsa[:, 1024:1536], pb[1][0:128, 0:512])
                e.tensor_copy(statsa[:, 1536:2048], pb[2][0:128, 0:512])
                e.drain()
                e.tensor_mul(SST[:, 0:512], pb[6][0:128, 0:512], statsa[:, 1024:1536])
                s_v.inc(e.tensor_mul(SST[:, 512:1024], pb[7][0:128, 0:512], statsa[:, 1536:2048]))
            on("vector", f)


    def emit_win_ln(ysrc, dst, convt):
        sqb = [lntmp[:].bitcast(BF16)[:, 0:512], lntmp[:].bitcast(BF16)[:, 512:1024]]
        wlv = {}
        for i in range(8):
            def f(e, i=i):
                if i == 0:
                    s_v.wait(e)
                    s_t.wait(e)
                if i >= 2:
                    s_t.wait(e, wlv[i - 2])
                s_s.inc(e.activation(sqb[i % 2], ysrc[i], AT.Square))
            on("scalar", f)

            def f(e, i=i):
                s_s.wait(e)
                if i == 0:
                    s_v.wait(e)
                e.matmul(pb[5][0:1, 0:512], onescol_b[:], sqb[i % 2],
                         start=(i == 0), stop=(i == 7))
                s_t.inc(e.matmul(pb[4][0:1, 0:512], onescol_f[:], ysrc[i],
                                 start=(i == 0), stop=(i == 7)))
                wlv[i] = s_t.v
            on("tensor", f)

        def f(e):
            s_t.wait(e)
            e.tensor_scalar(statsa[0:1, 0:512], pb[4][0:1, 0:512], 1.0 / D, None, OP.mult)
            e.tensor_scalar(statsa[0:1, 512:1024], pb[5][0:1, 0:512], 1.0 / D, None, OP.mult)
            e.drain()
            e.tensor_mul(statsa[0:1, 1024:1536], statsa[0:1, 0:512], statsa[0:1, 0:512])
            e.drain()
            s_v.inc(e.tensor_sub(statsa[0:1, 1536:2048], statsa[0:1, 512:1024], statsa[0:1, 1024:1536]))
        on("vector", f)

        def f(e):
            s_v.wait(e)
            e.activation(statsa[0:1, 1024:1536], statsa[0:1, 1536:2048], AT.Ln, bias=LN_EPS)
            e.drain()
            s_s.inc(e.activation(statsa[0:1, 1536:2048], statsa[0:1, 1024:1536], AT.Exp, scale=-0.5))
        on("scalar", f)

        def f(e):
            s_s.wait(e)
            e.matmul(pb[6][0:128, 0:512], ones128_f[0:1, :], statsa[0:1, 0:512],
                     start=True, stop=True)
            s_t.inc(e.matmul(pb[7][0:128, 0:512], ones128_f[0:1, :], statsa[0:1, 1536:2048],
                             start=True, stop=True))
        on("tensor", f)

        def f(e):
            s_t.wait(e)
            s_g.wait(e)
            for i in range(8):
                e.tensor_sub(convt[i % 2][:], ysrc[i], pb[6][:, 0:512])
                e.drain()
                mu = e.tensor_mul(dst(i), convt[i % 2][:], pb[7][:, 0:512])
                if i < 7:
                    e.drain()
            s_v.inc(mu)
        on("vector", f)

    def l1_tail(wi, V):
        convt = V["convt"]; wcur = V["wcur"]; vpost = V["vpost"]; bndx = V["bndx"]
        wcur_free = V["wcur_free"]
        tw = slice(wi * WIN, (wi + 1) * WIN)
        x2w = lambda ci: x2[ci][:, tw]
        owin = xb[1][:].bitcast(BF16)
        ggwin = xb[0][:].bitcast(BF16)
        gow = xb[6][:].bitcast(BF16)
        x3v = xb[4][:].bitcast(BF16)
        xrv = xb[5][:].bitcast(BF16)
        xkv = xb[7][:].bitcast(BF16)
        kactv = vpost

        def y3w(i):
            return xb[2][:, i * 512:(i + 1) * 512] if i < 4 else xb[3][:, (i - 4) * 512:(i - 3) * 512]

        def rw(i):
            return xb[2][:, i * 512:(i + 1) * 512] if i < 4 else xb[3][:, (i - 4) * 512:(i - 3) * 512]

        def y4w(i):
            return xb[1][:, i * 512:(i + 1) * 512] if i < 4 else xb[0][:, (i - 4) * 512:(i - 3) * 512]

        def outw(i):
            return xb[6][:, i * 512:(i + 1) * 512] if i < 4 else xb[7][:, (i - 4) * 512:(i - 3) * 512]

        sqb = [lntmp[:].bitcast(BF16)[:, 0:512], lntmp[:].bitcast(BF16)[:, 512:1024]]
        hls = {}
        hlr = {}
        for h in range(8):
            def f(e, h=h):
                if h == 0:
                    s_v.wait(e)
                if h >= 2:
                    s_t.wait(e, hls[h - 2])
                s_s.inc(e.activation(sqb[h % 2], owin[:, h * 512:(h + 1) * 512], AT.Square))
            on("scalar", f)

            def f(e, h=h):
                s_s.wait(e)
                if h < 2:
                    s_v.wait(e)
                else:
                    s_v.wait(e, hlr[h - 2])
                e.matmul(pb[4 + h % 2][0:1, 0:512], onescol_b[:],
                         owin[:, h * 512:(h + 1) * 512], start=True, stop=True)
                s_t.inc(e.matmul(pb[2 + h % 2][0:1, 0:512], onescol_b[:], sqb[h % 2],
                                 start=True, stop=True))
                hls[h] = s_t.v
            on("tensor", f)

            def f(e, h=h):
                s_t.wait(e)
                e.tensor_scalar(statsa[0:1, 0:512], pb[4 + h % 2][0:1, 0:512],
                                1.0 / 128, None, OP.mult)
                e.tensor_scalar(statsa[0:1, 512:1024], pb[2 + h % 2][0:1, 0:512],
                                1.0 / 128, None, OP.mult)
                e.drain()
                e.tensor_mul(statsa[0:1, 1024:1536], statsa[0:1, 0:512],
                             statsa[0:1, 0:512])
                e.drain()
                s_v.inc(e.tensor_sub(statsa[0:1, 1536:2048], statsa[0:1, 512:1024],
                                     statsa[0:1, 1024:1536]))
                hlr[h] = s_v.v
            on("vector", f)

            def f(e):
                s_v.wait(e)
                e.activation(statsa[0:1, 1024:1536], statsa[0:1, 1536:2048],
                             AT.Ln, bias=HLN_EPS)
                e.drain()
                s_s.inc(e.activation(statsa[0:1, 1536:2048], statsa[0:1, 1024:1536],
                                     AT.Exp, scale=-0.5))
            on("scalar", f)

            bmean, brstd = (pb[6], pb[7]) if h % 2 == 0 else (pb[0], pb[1])

            def f(e, bmean=bmean, brstd=brstd):
                s_s.wait(e)
                s_v.wait(e)
                e.matmul(bmean[0:128, 0:512], ones128_f[0:1, :],
                         statsa[0:1, 0:512], start=True, stop=True)
                s_t.inc(e.matmul(brstd[0:128, 0:512], ones128_f[0:1, :],
                                 statsa[0:1, 1536:2048], start=True, stop=True))
            on("tensor", f)

            def f(e, h=h, bmean=bmean, brstd=brstd):
                s_t.wait(e)
                e.tensor_sub(convt[0][:], owin[:, h * 512:(h + 1) * 512],
                             bmean[:, 0:512])
                e.drain()
                e.tensor_mul(convt[1][:], convt[0][:], brstd[:, 0:512])
                e.drain()
                e.tensor_scalar(convt[0][:], convt[1][:], lng_c[:, 0:1],
                                lnb_c[:, 0:1], OP.mult, OP.add)
                e.drain()
                s_v.inc(e.tensor_mul(gow[:, h * 512:(h + 1) * 512], convt[0][:],
                                     ggwin[:, h * 512:(h + 1) * 512]))
            on("vector", f)

        def stream_mm(mat, rhs_fn, first_extra=None, rhs_vals=None):
            tvals = {}
            entry = s_t.v
            for pair in range(4):
                def f(e, pair=pair, mat=mat):
                    s_t.wait(e, tvals[pair - 2] if pair >= 2 else entry)
                    (s_mv if pair % 2 == 0 else s_mw).inc(e.dma_start(
                        out=wcur[pair % 2][:].rearrange("p (a m) -> p a m", m=1024),
                        in_=wbf[mat][pair * 256:(pair + 1) * 256, :].rearrange(
                            "(a p) m -> p a m", p=128)), 16)
                on("gpsimd", f)

                def f(e, pair=pair):
                    (s_mv if pair % 2 == 0 else s_mw).wait(e)
                    if rhs_vals is not None:
                        s_v.wait(e, rhs_vals[min(2 * pair + 1, 7)])
                    elif pair == 0:
                        s_v.wait(e)
                    if pair == 0 and first_extra is not None:
                        first_extra(e)
                    for sl in range(2):
                        cj = pair * 2 + sl
                        for ci in range(8):
                            mm = e.matmul(pb[ci][0:128, 0:512],
                                          wcur[pair % 2][:, sl * 1024 + ci * 128:sl * 1024 + (ci + 1) * 128],
                                          rhs_fn(cj), start=(cj == 0), stop=(cj == 7))
                    s_t.inc(mm)
                    tvals[pair] = s_t.v
                    wcur_free["v"] = s_t.v
                on("tensor", f)

        stream_mm("Wo", lambda h: gow[:, h * 512:(h + 1) * 512])

        def f(e):
            s_t.wait(e)
            for i in range(8):
                ad = e.tensor_add(y3w(i), pb[i][0:128, 0:512], x2w(i))
            s_v.inc(ad)
        on("vector", f)

        emit_win_ln([y3w(i) for i in range(8)],
                    lambda i: x3v[:, i * 512:(i + 1) * 512], convt)
        if wi == 0 and "x3" in dbg:
            dump("x3", [x3v[:, 0:4096]])

        e0v = {}

        def f(e):
            s_v.wait(e)
            s_g.wait(e)
            for ci in range(8):
                e.tensor_copy(convt[0][:, 1:512], x3v[:, ci * 512:ci * 512 + 511])
                e.tensor_copy(convt[0][:, 0:1], bndx[:, ci:ci + 1])
                e.drain()
                e.tensor_sub(convt[1][:], x3v[:, ci * 512:(ci + 1) * 512], convt[0][:])
                e.drain()
                e.scalar_tensor_tensor(xrv[:, ci * 512:(ci + 1) * 512], convt[1][:],
                                       mur_c[:, ci:ci + 1], convt[0][:], OP.mult, OP.add)
                s_v.inc(e.scalar_tensor_tensor(xkv[:, ci * 512:(ci + 1) * 512], convt[1][:],
                                               muk_c[:, ci:ci + 1], convt[0][:], OP.mult, OP.add))
                e0v[ci] = s_v.v
                e.drain()
        on("vector", f)

        def f(e):
            s_v.wait(e)
            e.tensor_copy(bndx[:, 0:8].rearrange("p (c o) -> p c o", o=1),
                          x3v.rearrange("p (c t) -> p c t", t=512)[:, :, 511:512])
            s_g.inc(e.drain())
        on("gpsimd", f)

        stream_mm("W_rT", lambda cj: xrv[:, cj * 512:(cj + 1) * 512], rhs_vals=e0v)

        def f(e):
            s_t.wait(e)
            for i in range(8):
                ac = e.activation(rw(i), pb[i][0:128, 0:512], AT.Sigmoid)
            s_s.inc(ac)
        on("scalar", f)

        def kwait(e):
            s_s.wait(e)
        stream_mm("W_kT", lambda cj: xkv[:, cj * 512:(cj + 1) * 512], first_extra=kwait)
        e2v = {}
        for i in range(8):
            def f(e, i=i):
                if i == 0:
                    s_t.wait(e)
                if i >= 2:
                    s_v.wait(e, e2v[i - 2])
                s_s.inc(e.activation(convt[i % 2][:], pb[i][0:128, 0:512], AT.Relu))
            on("scalar", f)

            def f(e, i=i):
                s_s.wait(e)
                s_v.inc(e.tensor_mul(kactv[:, i * 512:(i + 1) * 512], convt[i % 2][:],
                                     convt[i % 2][:]))
                e2v[i] = s_v.v
            on("vector", f)

        stream_mm("W_vT", lambda cj: kactv[:, cj * 512:(cj + 1) * 512])

        def f(e):
            s_t.wait(e)
            for i in range(8):
                e.tensor_mul(convt[i % 2][:], pb[i][0:128, 0:512], rw(i))
                e.drain()
                ad = e.tensor_add(y4w(i), convt[i % 2][:], x3v[:, i * 512:(i + 1) * 512])
                if i < 7:
                    e.drain()
            s_v.inc(ad)
        on("vector", f)

        emit_win_ln([y4w(i) for i in range(8)], outw, convt)

        def f(e):
            s_v.wait(e)
            for ci in range(8):
                s_st.inc(e.dma_start(out=out_t[ci * 128:(ci + 1) * 128, tw],
                                     in_=outw(ci)), 16)
        on("sync", f)

    phase0()
    ph0s.close()
    if phases >= 1:
        l0_attn_v2()
    if phases >= 2:
        for i in range(8):
            x2.append(xa[i][:, 0:1024].bitcast(BF16))
        l0_ffn()
    if phases >= 3:
        l1s = ExitStack()
        l1_attn(l1s)

    def store_out():
        def f(e):
            s_v.wait(e)
            s_s.wait(e)
            if phases < 3:
                for i in range(8):
                    s_st.inc(e.dma_start(out=out_t[i * 128:(i + 1) * 128, :], in_=xa[i][:]), 16)
            s_st.wait(e)
        on("sync", f)

    store_out()
    es.close()
    return nc, dbg_t


def _prep_weights(inputs):
    f32 = np.float32
    g = lambda n: np.ascontiguousarray(np.asarray(inputs[n], dtype=f32))
    wm = {}
    wm["W_DQ"] = g("L0_attn_W_DQ")
    wm["W_UQT"] = np.ascontiguousarray(g("L0_attn_W_UQ").T)
    wm["W_DKV"] = g("L0_attn_W_DKV")
    wm["W_UKVT"] = np.ascontiguousarray(g("L0_attn_W_UKV").T)
    wm["W_QRT"] = np.ascontiguousarray(g("L0_attn_W_QR").T)
    wm["W_KRT"] = np.ascontiguousarray(g("L0_attn_W_KR").T)
    wm["W_OT"] = np.ascontiguousarray(g("L0_attn_W_O").T)
    wm["q_norm_w"] = g("L0_attn_q_norm_w").reshape(64, 1)
    wm["k_norm_w"] = g("L0_attn_k_norm_w").reshape(64, 1)
    wm["f0_kernel"] = np.ascontiguousarray(np.asarray(inputs["L0_ffn_kernel"]))
    wm["f0_in_scale"] = g("L0_ffn_input_scale").reshape(1, 1)
    wm["f0_k_scale"] = g("L0_ffn_kernel_scale").reshape(1, 1)
    wm["f0_bias"] = np.ascontiguousarray(np.asarray(inputs["L0_ffn_bias"]).reshape(8, 128).T)
    for n in ("Wq", "Wk", "Wv", "Wa", "Wg", "Wo"):
        wm[n] = g("L1_attn_" + n)
    wm["Wb"] = np.ascontiguousarray(g("L1_attn_Wb").reshape(8, 128, 8).transpose(1, 0, 2).reshape(128, 64))
    wm["ba"] = np.ascontiguousarray(g("L1_attn_ba").reshape(8, 128).T)
    wm["bb"] = g("L1_attn_bb").reshape(H1, 1)
    for n in ("cqw", "ckw", "cvw"):
        wm[n] = np.ascontiguousarray(g("L1_attn_" + n).reshape(8, 128, 4).transpose(1, 0, 2).reshape(128, 32))
    for n in ("cqb", "ckb", "cvb"):
        wm[n] = np.ascontiguousarray(g("L1_attn_" + n).reshape(8, 128).T)
    wm["ln_g"] = g("L1_attn_ln_g").reshape(128, 1)
    wm["ln_b"] = g("L1_attn_ln_b").reshape(128, 1)
    wm["W_rT"] = np.ascontiguousarray(g("L1_ffn_W_r").T)
    wm["W_kT"] = np.ascontiguousarray(g("L1_ffn_W_k").T)
    wm["W_vT"] = np.ascontiguousarray(g("L1_ffn_W_v").T)
    wm["mu_r"] = np.ascontiguousarray(g("L1_ffn_mu_r").reshape(8, 128).T)
    wm["mu_k"] = np.ascontiguousarray(g("L1_ffn_mu_k").reshape(8, 128).T)
    return wm


def kernel(**inputs):
    x = np.asarray(inputs["L0_attn_x"], dtype=np.float32)
    B = x.shape[0]
    wm = _prep_weights(inputs)
    nc, _ = build()
    in_maps = []
    for b in range(B):
        m = {"x": np.ascontiguousarray(x[b].T)}
        m.update(wm)
        in_maps.append(m)
    res = run_bass_kernel_spmd(nc, in_maps, core_ids=list(range(B)))
    out = np.stack([np.asarray(r["out"]).T for r in res.results], axis=0)
    return out.astype(np.float32)
